# version 16
# baseline (speedup 1.0000x reference)
import math
import contextlib
import numpy as np
import concourse.bass as bass
import concourse.mybir as mybir
from concourse.bass_utils import run_bass_kernel_spmd

F32 = mybir.dt.float32
BF16 = mybir.dt.bfloat16
AF = mybir.ActivationFunctionType
ALU = mybir.AluOpType
AX = mybir.AxisListType

C = 4096
T = 2048
NQ = 1024
KVD = 2048
NCC = C // 128
NKVC = KVD // 128
NH = 32
NKVH = 16
INT = 8192
LAT = 2048
NIC = INT // 128
NLC = LAT // 128
SCALE = 1.0 / math.sqrt(32.0)
EPS = 1e-5

_DBG_PHASE = 99
_DBG = False


def _img_T(W, blkw):
    dout, cin = W.shape
    nb, ncc = dout // blkw, cin // 128
    return np.ascontiguousarray(
        W.reshape(nb, blkw, ncc, 128).transpose(0, 3, 2, 1)).reshape(nb, 128, ncc * blkw)


def _img_N(W, blkw):
    cin, dout = W.shape
    nb, ncc = dout // blkw, cin // 128
    return np.ascontiguousarray(
        W.reshape(ncc, 128, nb, blkw).transpose(2, 1, 0, 3)).reshape(nb, 128, ncc * blkw)


def _prep(inputs):
    g = lambda k: np.asarray(inputs[k], dtype=np.float32)
    x = g("L0_attn_x")
    shared = dict(
        wq=_img_T(g("L0_attn_Wq"), 256),
        wk=_img_T(g("L0_attn_Wk"), 256),
        wv=_img_T(g("L0_attn_Wv"), 256),
        up=_img_N(g("L0_ffn_up_w"), 128),
        dwn=_img_N(g("L0_ffn_down_w"), 128),
        fc1=_img_N(g("L0_ffn_fc1_w"), 128),
        fc2=_img_N(g("L0_ffn_fc2_w"), 128),
        ew=_img_N(g("L0_ffn_expert_w"), 128),
        gate=np.ascontiguousarray(
            g("L0_ffn_gate_w").reshape(32, 128, 8).transpose(1, 0, 2)).reshape(128, 256),
        bq=np.ascontiguousarray(g("L0_attn_bq").reshape(32, 128).T),
        bk=np.ascontiguousarray(g("L0_attn_bk").reshape(16, 128).T),
        bv=np.ascontiguousarray(g("L0_attn_bv").reshape(16, 128).T),
        gam=np.ascontiguousarray(g("L0_ffn_ln_gamma").reshape(32, 128).T),
        bet=np.ascontiguousarray(g("L0_ffn_ln_beta").reshape(32, 128).T),
    )
    in_maps = []
    for c in range(8):
        s, e = c // 2, c % 2
        perm = list(range(e, 16, 2))
        xtc = np.ascontiguousarray(x[s].reshape(16, 128, C)[perm].reshape(NQ, C).T)
        m = dict(shared)
        m["xt"] = xtc
        m["epar"] = np.full((128, 1), float(e), np.float32)
        in_maps.append(m)
    return in_maps


def _reassemble(couts):
    out = np.empty((4, T, C), np.float32)
    for c in range(8):
        s, e = c // 2, c % 2
        co = couts[c]
        for u in range(8):
            out[s, (2 * u + e) * 128:(2 * u + e + 1) * 128, :] = \
                co[:, u * 128:(u + 1) * 128].T
    return out


class Sem:

    def __init__(self, es, nc, name):
        self.s = es.enter_context(nc.semaphore(name))
        self.n = 0

    def inc(self, instr, k=1):
        instr.then_inc(self.s, k)
        self.n += k
        return self.n

    def dma(self, instr):
        return self.inc(instr, 16)


class Builder:
    def __init__(self):
        self.nc = bass.Bass("TRN2", target_bir_lowering=False)

    def sem(self, name):
        if not hasattr(self, "_sems"):
            self._sems = {}
        if name not in self._sems:
            self._sems[name] = Sem(self.es, self.nc, name)
        return self._sems[name]

    def sb(self, name, shape, dt):
        return self.es.enter_context(self.nc.sbuf_tensor(name, shape, dt))

    def build(self):
        nc = self.nc
        self.xt = nc.dram_tensor("xt", [C, NQ], F32, kind="ExternalInput")
        self.wq = nc.dram_tensor("wq", [16, 128, 8192], F32, kind="ExternalInput")
        self.wk = nc.dram_tensor("wk", [8, 128, 8192], F32, kind="ExternalInput")
        self.wv = nc.dram_tensor("wv", [8, 128, 8192], F32, kind="ExternalInput")
        self.up_w = nc.dram_tensor("up", [64, 128, 4096], F32, kind="ExternalInput")
        self.dwn_w = nc.dram_tensor("dwn", [32, 128, 8192], F32, kind="ExternalInput")
        self.fc1_w = nc.dram_tensor("fc1", [16, 128, 4096], F32, kind="ExternalInput")
        self.fc2_w = nc.dram_tensor("fc2", [32, 128, 2048], F32, kind="ExternalInput")
        self.ew_w = nc.dram_tensor("ew", [16, 128, 2048], F32, kind="ExternalInput")
        self.gate_w = nc.dram_tensor("gate", [128, 256], F32, kind="ExternalInput")
        self.bq_t = nc.dram_tensor("bq", [128, 32], F32, kind="ExternalInput")
        self.bk_t = nc.dram_tensor("bk", [128, 16], F32, kind="ExternalInput")
        self.bv_t = nc.dram_tensor("bv", [128, 16], F32, kind="ExternalInput")
        self.gam_t = nc.dram_tensor("gam", [128, 32], F32, kind="ExternalInput")
        self.bet_t = nc.dram_tensor("bet", [128, 32], F32, kind="ExternalInput")
        self.epar_t = nc.dram_tensor("epar", [128, 1], F32, kind="ExternalInput")
        self.out_t = nc.dram_tensor("out", [C, NQ], F32, kind="ExternalOutput")

        scr_kind = "ExternalOutput" if _DBG else "Internal"
        self.kvo_scr = nc.dram_tensor("kvo_scr", [32 * 128, NQ], BF16, kind="Internal")
        self.kva_scr = nc.dram_tensor("kva_scr", [2 * 32 * 128, NQ], BF16,
                                      kind=scr_kind)
        self.up_bf = nc.dram_tensor("up_bf", [64, 128, 4096], BF16, kind="Internal")
        self.dwn_bf = nc.dram_tensor("dwn_bf", [32, 128, 8192], BF16, kind="Internal")
        self.mk_in = nc.dram_tensor("mk_in", [1, 1], F32, kind="Internal")
        self.mk_out = nc.dram_tensor("mk_out", [1, 1], F32, kind="Internal")
        self.q_scr = nc.dram_tensor("q_scr", [NH, 128, NQ], BF16, kind=scr_kind)
        self.x2_scr = nc.dram_tensor("x2_scr", [NCC, 128, NQ], F32, kind=scr_kind)
        if _DBG:
            self.dbg_xln = nc.dram_tensor("dbg_xln", [128, NCC * T], BF16,
                                          kind="ExternalOutput")
            self.dbg_misc = nc.dram_tensor("dbg_misc", [128, 16], F32,
                                           kind="ExternalOutput")

        with contextlib.ExitStack() as es:
            self.es = es
            self.alloc_persistent()
            self.p0_consts()
            if _DBG_PHASE >= 1:
                self.p1_ln1_stats()
            if _DBG_PHASE >= 2:
                self.p2_xln()
            self.ln1_es.close()
            if _DBG_PHASE >= 3:
                self.proj_feat("wk", self.wk, 8, self.bks, 0, norms=self.maxk2,
                               spill_ks=range(0, 16))
            if _DBG_PHASE >= 4:
                self.proj_feat("wv", self.wv, 8, self.bvs, 16, norms=None,
                               spill_ks=range(16, 32))
            if _DBG_PHASE >= 5:
                self.p5c_issue()
                self.proj_q()
                self.p5c_wait()
                self.p5b_bias()
            if _DBG_PHASE >= 6:
                self.p6_attention()
            self.spb_es.close()
            if _DBG_PHASE >= 7:
                self.br = es.enter_context(nc.sbuf_tensor("p8br", [128, NQ], F32))
                self.lgf_es = contextlib.ExitStack()
                self.lgf = self.lgf_es.enter_context(
                    nc.sbuf_tensor("lgf", [8, NQ], F32))
                self.gates = self.lgf_es.enter_context(
                    nc.sbuf_tensor("gates", [128, 256], BF16))
                self.p7_ln2()
            if _DBG_PHASE >= 8:
                self.p8_ffn()
            self.dbg_dumps()
        return nc

    def spill_load(self, g, k):
        b = k % 2
        if k >= 2:
            g.wait_ge(self.dSpS[b].s, 16 * (k // 2))
        if k < 64:
            src = self.up_w[k, :, :]
        else:
            j = k - 64
            src = self.dwn_w[j // 2, :, (j % 2) * 4096:(j % 2) * 4096 + 4096]
        g.dma_start(self.spb[b][:, :], src).then_inc(self.dSpL[b].s, 16)
        self.dSpL[b].n += 16

    def spill_store(self, sy, k):
        b = k % 2
        sy.wait_ge(self.dSpL[b].s, 16 * (k // 2 + 1))
        if k < 64:
            dst = self.up_bf[k, :, :]
        else:
            j = k - 64
            dst = self.dwn_bf[j // 2, :, (j % 2) * 4096:(j % 2) * 4096 + 4096]
        sy.dma_start(dst, self.spb[b][:, :]).then_inc(self.dSpS[b].s, 16)
        self.dSpS[b].n += 16

    def alloc_persistent(self):
        self.ones_bf = self.sb("ones_bf", [128, 128], BF16)
        self.ident_bf = self.sb("ident_bf", [128, 128], BF16)
        self.ident_f = self.sb("ident_f", [128, 128], F32)
        self.tri = self.sb("tri", [128, 128], F32)
        self.bqs = self.sb("bqs", [128, 32], F32)
        self.bks = self.sb("bks", [128, 16], F32)
        self.bvs = self.sb("bvs", [128, 16], F32)
        self.gams = self.sb("gams", [128, 32], F32)
        self.bets = self.sb("bets", [128, 32], F32)
        self.epar = self.sb("epar_s", [128, 1], F32)
        self.eps_t = self.sb("eps_t", [128, 1], F32)
        self.neg_half = self.sb("neg_half", [128, 1], F32)
        self.neg_one = self.sb("neg_one", [128, 1], F32)
        self.scale_t = self.sb("scale_t", [128, 1], F32)
        self.maxq2 = self.sb("maxq2", [128, 1], F32)
        self.maxk2 = self.sb("maxk2", [128, 1], F32)
        self.bias_m = self.sb("bias_m", [128, 1], F32)
        self.maskA = self.sb("maskA", [128, 128], F32)
        self.maskB = self.sb("maskB", [128, 128], F32)
        self.onesf = self.sb("onesf", [1, 128], F32)
        self.dSpL = [self.sem("sp_dLa"), self.sem("sp_dLb")]
        self.dSpS = [self.sem("sp_dSa"), self.sem("sp_dSb")]
        self.mk_sb = self.sb("mk_sb", [1, 1], F32)
        self.reg = self.es.enter_context(self.nc.sbuf_tensor("reg", [128, NCC * NQ], F32))
        self.reg_bf = self.reg.bitcast(BF16)
        self.spb_es = contextlib.ExitStack()
        self.spb = [self.spb_es.enter_context(
            self.nc.sbuf_tensor(f"spb{i}", [128, 4096], BF16)) for i in range(2)]
        self.ln1_es = contextlib.ExitStack()
        self.mu1 = self.ln1_es.enter_context(self.nc.sbuf_tensor("mu1", [128, NQ], F32))
        self.a1 = self.ln1_es.enter_context(self.nc.sbuf_tensor("a1", [128, NQ], F32))

    def p0_consts(self):
        nc = self.nc
        d0 = self.sem("p0_d")
        c0 = self.sem("p0_c")
        with nc.Block() as blk:
            @blk.gpsimd
            def _(g):
                c0.inc(g.memset(self.ident_bf[:, :], 1.0))
                c0.inc(g.memset(self.ident_f[:, :], 1.0))
                c0.inc(g.memset(self.tri[:, :], 0.0))
                g.wait_ge(c0.s, c0.n)
                c0.inc(g.affine_select(self.ident_bf[:, :], self.ident_bf[:, :],
                                       [[1, 128]], ALU.is_equal, 0.0,
                                       channel_multiplier=-1))
                c0.inc(g.affine_select(self.ident_f[:, :], self.ident_f[:, :],
                                       [[1, 128]], ALU.is_equal, 0.0,
                                       channel_multiplier=-1))
                c0.inc(g.affine_select(self.tri[:, :], self.tri[:, :], [[1, 128]],
                                       ALU.is_ge, -30000.0, channel_multiplier=-1))
                g.memset(self.ones_bf[:, :], 1.0)
                g.memset(self.eps_t[:, :], EPS)
                g.memset(self.neg_half[:, :], -0.5)
                g.memset(self.neg_one[:, :], -1.0)
                g.memset(self.scale_t[:, :], SCALE)
                g.memset(self.maxq2[:, :], 0.0)
                g.memset(self.maxk2[:, :], 0.0)
                g.memset(self.onesf[:, :], 1.0)

            @blk.sync
            def _(sy):
                for dst, src in ((self.bqs, self.bq_t), (self.bks, self.bk_t),
                                 (self.bvs, self.bv_t), (self.gams, self.gam_t),
                                 (self.bets, self.bet_t), (self.epar, self.epar_t)):
                    d0.dma(sy.dma_start(dst[:, :], src[:, :]))
                sy.wait_ge(d0.s, d0.n)

            @blk.vector
            def _(v):
                v.wait_ge(d0.s, d0.n)
                v.wait_ge(c0.s, 6)
                v.tensor_scalar(self.maskA[:, :], self.tri[:, :],
                                self.epar[:, 0:1], None, ALU.mult).then_inc(c0.s, 1)
                v.wait_ge(c0.s, 7)
                v.tensor_scalar(self.bias_m[:, :], self.epar[:, 0:1], 30000.0, -30000.0,
                                ALU.mult, ALU.add).then_inc(c0.s, 1)
                v.wait_ge(c0.s, 8)
                v.tensor_scalar(self.maskB[:, :], self.maskA[:, :],
                                self.bias_m[:, 0:1], None, ALU.add).then_inc(c0.s, 1)
                v.wait_ge(c0.s, 9)
                v.tensor_sub(self.maskA[:, :], self.tri[:, :],
                             self.maskA[:, :]).then_inc(c0.s, 1)

    def p1_ln1_stats(self):
        nc = self.nc
        xln = self.reg_bf
        with contextlib.ExitStack() as p:
            xbf = [p.enter_context(nc.sbuf_tensor(f"p1xbf{i}", [128, NQ], BF16)) for i in range(2)]
            sqb = [p.enter_context(nc.sbuf_tensor(f"p1sqb{i}", [128, NQ], BF16)) for i in range(2)]
            xf = [p.enter_context(nc.sbuf_tensor(f"p2xf{i}", [128, NQ], F32)) for i in range(2)]
            t1b = [p.enter_context(nc.sbuf_tensor(f"p2t1{i}", [128, NQ], F32)) for i in range(2)]
            ps_s1 = p.enter_context(nc.psum_tensor("p1s1", [128, NQ], F32))
            ps_s2 = p.enter_context(nc.psum_tensor("p1s2", [128, NQ], F32))
            t_e2 = p.enter_context(nc.sbuf_tensor("p1e2", [128, NQ], F32))
            dL = [self.sem("p1_dLa"), self.sem("p1_dLb")]
            dL2 = [self.sem("p2_dLa"), self.sem("p2_dLb")]
            cSq = self.sem("p1_cSq")
            cMM = self.sem("p1_cMM")
            cR = self.sem("p1_cR")
            cAp = self.sem("p2_cAp")

            with nc.Block() as blk:
                @blk.gpsimd
                def _(g):
                    for cc in range(NCC):
                        if cc >= 2:
                            g.wait_ge(cMM.s, cc - 1)
                        g.dma_start(xbf[cc % 2][:, :],
                                    self.xt[cc * 128:(cc + 1) * 128, :]
                                    ).then_inc(dL[cc % 2].s, 16)
                    g.wait_ge(cMM.s, NCC)

                @blk.sync
                def _(sy):
                    for cc in range(NCC):
                        if cc >= 2:
                            sy.wait_ge(cAp.s, 2 * (cc - 1))
                        sy.dma_start(xf[cc % 2][:, :],
                                     self.xt[cc * 128:(cc + 1) * 128, :]
                                     ).then_inc(dL2[cc % 2].s, 16)
                    sy.wait_ge(cAp.s, 2 * NCC)

                @blk.scalar
                def _(s):
                    for cc in range(NCC):
                        s.wait_ge(dL[cc % 2].s, 16 * (cc // 2 + 1))
                        if cc >= 2:
                            s.wait_ge(cMM.s, cc - 1)
                        s.activation(sqb[cc % 2][:, :], xbf[cc % 2][:, :],
                                     AF.Square).then_inc(cSq.s, 1)
                    s.wait_ge(cR.s, 4)
                    s.activation(t_e2[:, :], t_e2[:, :], AF.Ln,
                                 bias=self.eps_t[:, :]).then_inc(cR.s, 1)
                    s.wait_ge(cR.s, 5)
                    s.activation(self.a1[:, :], t_e2[:, :], AF.Exp,
                                 scale=self.neg_half[:, :]).then_inc(cR.s, 1)

                @blk.tensor
                def _(t):
                    for cc in range(NCC):
                        t.wait_ge(cSq.s, cc + 1)
                        for n in range(2):
                            t.matmul(ps_s1[:, 512 * n:512 * (n + 1)], self.ones_bf[:, :],
                                     xbf[cc % 2][:, 512 * n:512 * (n + 1)],
                                     start=(cc == 0), stop=(cc == NCC - 1))
                            mm = t.matmul(ps_s2[:, 512 * n:512 * (n + 1)], self.ones_bf[:, :],
                                          sqb[cc % 2][:, 512 * n:512 * (n + 1)],
                                          start=(cc == 0), stop=(cc == NCC - 1))
                        mm.then_inc(cMM.s, 1)

                @blk.vector
                def _(v):
                    v.wait_ge(cMM.s, NCC)
                    v.tensor_scalar(self.mu1[:, :], ps_s1[:, :], 1.0 / C,
                                    None, ALU.mult).then_inc(cR.s, 1)
                    v.tensor_scalar(t_e2[:, :], ps_s2[:, :], 1.0 / C,
                                    None, ALU.mult).then_inc(cR.s, 1)
                    v.wait_ge(cR.s, 2)
                    v.tensor_mul(self.a1[:, :], self.mu1[:, :],
                                 self.mu1[:, :]).then_inc(cR.s, 1)
                    v.wait_ge(cR.s, 3)
                    v.tensor_sub(t_e2[:, :], t_e2[:, :], self.a1[:, :]).then_inc(cR.s, 1)
                    v.wait_ge(cR.s, 6)
                    for cc in range(NCC):
                        v.wait_ge(dL2[cc % 2].s, 16 * (cc // 2 + 1))
                        v.tensor_sub(t1b[cc % 2][:, :], xf[cc % 2][:, :],
                                     self.mu1[:, :]).then_inc(cAp.s, 1)
                        v.wait_ge(cAp.s, 2 * cc + 1)
                        v.tensor_mul(xln[:, cc * NQ:(cc + 1) * NQ], t1b[cc % 2][:, :],
                                     self.a1[:, :]).then_inc(cAp.s, 1)

    def p2_xln(self):
        pass

    def proj_feat(self, name, wimg, nblk, bias_img, scr_off, norms, spill_ks=()):
        nc = self.nc
        xln = self.reg_bf
        n_slot = nblk * 2
        spill_ks = list(spill_ks)
        with contextlib.ExitStack() as p:
            wbuf = [p.enter_context(nc.sbuf_tensor(f"{name}wb{i}", [128, 8192], BF16))
                    for i in range(2)]
            kbf = [p.enter_context(nc.sbuf_tensor(f"{name}kb{i}", [128, 1024], BF16))
                   for i in range(2)]
            ksq = p.enter_context(nc.sbuf_tensor(f"{name}sq", [128, 1024], BF16))
            nrm_s = p.enter_context(nc.sbuf_tensor(f"{name}nrm", [128, 1], F32))
            ps = [p.enter_context(nc.psum_tensor(f"{name}ps{i}", [128, 1024], F32))
                  for i in range(2)]
            ps_n = (p.enter_context(nc.psum_tensor(f"{name}pn", [128, 1024], F32))
                    if norms is not None else None)
            dW = [self.sem(name + "_dWa"), self.sem(name + "_dWb")]
            dS = [self.sem(name + "_dSa"), self.sem(name + "_dSb")]
            cPE = self.sem(name + "_cPE")
            cC = self.sem(name + "_cC")
            cSq = self.sem(name + "_cSq")
            cN = self.sem(name + "_cN")
            cFree = cSq if norms is not None else cC

            def main_val(s):
                if norms is None:
                    return s + 1
                return 1 if s == 0 else 2 * s

            def norm_val(s):
                return 2 * s + 3 if s < n_slot - 1 else 2 * n_slot

            with nc.Block() as blk:
                @blk.gpsimd
                def _(g):
                    for b in range(nblk):
                        if b >= 2:
                            g.wait_ge(cFree.s, 2 * b - 2)
                        g.dma_start(wbuf[b % 2][:, :],
                                    wimg[b, :, :]).then_inc(dW[b % 2].s, 16)
                        for k in spill_ks[b * len(spill_ks) // nblk:
                                          (b + 1) * len(spill_ks) // nblk]:
                            self.spill_load(g, k)
                    g.wait_ge(cFree.s, n_slot)

                @blk.tensor
                def _(t):
                    for s in range(n_slot):
                        dch = s
                        b, sub = dch // 2, dch % 2
                        t.wait_ge(dW[b % 2].s, 16 * (b // 2 + 1))
                        if s >= 2:
                            t.wait_ge(cFree.s, s - 1)
                        for cc in range(NCC):
                            lhs = wbuf[b % 2][:, cc * 256 + sub * 128:cc * 256 + sub * 128 + 128]
                            for n in range(2):
                                mm = t.matmul(ps[s % 2][:, 512 * n:512 * (n + 1)], lhs,
                                              xln[:, cc * NQ + 512 * n:
                                                  cc * NQ + 512 * (n + 1)],
                                              start=(cc == 0), stop=(cc == NCC - 1))
                        mm.then_inc(cPE.s, 1)
                        if norms is not None and s >= 1:
                            t.wait_ge(cSq.s, s)
                            if s >= 2:
                                t.wait_ge(cN.s, s - 1)
                            t.matmul(ps_n[:, 0:512], self.ones_bf[:, :], ksq[:, 0:512],
                                     start=True, stop=True)
                            t.matmul(ps_n[:, 512:1024], self.ones_bf[:, :], ksq[:, 512:1024],
                                     start=True, stop=True).then_inc(cPE.s, 1)
                    if norms is not None:
                        t.wait_ge(cSq.s, n_slot)
                        t.wait_ge(cN.s, n_slot - 1)
                        t.matmul(ps_n[:, 0:512], self.ones_bf[:, :], ksq[:, 0:512],
                                 start=True, stop=True)
                        t.matmul(ps_n[:, 512:1024], self.ones_bf[:, :], ksq[:, 512:1024],
                                 start=True, stop=True).then_inc(cPE.s, 1)

                @blk.scalar
                def _(sc):
                    for s in range(n_slot):
                        dch = s
                        sc.wait_ge(cPE.s, main_val(s) if s == 0 or norms is None
                                   else norm_val(s - 1))
                        if s >= 2:
                            sc.wait_ge(dS[s % 2].s, 16 * (s // 2))
                        cC.inc(sc.activation(kbf[s % 2][:, :], ps[s % 2][:, :], AF.Identity,
                                             bias=bias_img[:, dch:dch + 1]))
                        if norms is not None:
                            sc.activation(ksq[:, :], ps[s % 2][:, :], AF.Square,
                                          bias=bias_img[:, dch:dch + 1]).then_inc(cSq.s, 1)

                if norms is not None:
                    cNr = self.sem(name + "_cNr")

                    @blk.vector
                    def _(v):
                        for s in range(n_slot):
                            v.wait_ge(cPE.s, norm_val(s))
                            v.reduce_max(nrm_s[:, :], ps_n[:, :], AX.X).then_inc(cNr.s, 1)
                            v.wait_ge(cNr.s, s + 1)
                            v.tensor_max(norms[:, :], norms[:, :],
                                         nrm_s[:, :]).then_inc(cN.s, 1)
                            if s < n_slot - 1:
                                v.wait_ge(cN.s, s + 1)

                @blk.sync
                def _(sy):
                    for s in range(n_slot):
                        sy.wait_ge(cC.s, s + 1)
                        sy.dma_start(
                            self.kvo_scr[(scr_off + s) * 128:(scr_off + s + 1) * 128, :],
                            kbf[s % 2][:, :]).then_inc(dS[s % 2].s, 16)
                        for k in spill_ks[s * len(spill_ks) // n_slot:
                                          (s + 1) * len(spill_ks) // n_slot]:
                            self.spill_store(sy, k)
                    sy.wait_ge(dS[0].s, 16 * (n_slot // 2))
                    sy.wait_ge(dS[1].s, 16 * (n_slot // 2))

    def p5c_issue(self):
        nc = self.nc
        groups = [[0, 1], [2, 3], [4, 5], [6, 7]]
        cg = self.sem("p5c_c")
        dg = self.sem("p5c_d")
        with nc.Block() as blk:
            @blk.gpsimd
            def _(g):
                g.dma_start(self.mk_in[:, :], self.maxk2[0:1, 0:1]).then_inc(dg.s, 16)
                g.wait_ge(dg.s, 16)
                g.collective_compute(
                    "AllGather", ALU.bypass, replica_groups=groups,
                    ins=[self.kvo_scr.ap().opt()],
                    outs=[self.kva_scr.ap().opt()]).then_inc(cg.s, 1)
                g.collective_compute(
                    "AllReduce", ALU.max, replica_groups=groups,
                    ins=[self.mk_in.ap().opt()],
                    outs=[self.mk_out.ap().opt()]).then_inc(cg.s, 1)

    def p5c_wait(self):
        nc = self.nc
        cg = self.sem("p5c_c")
        dg = self.sem("p5c_d")
        with contextlib.ExitStack() as pp:
            ps_mk = pp.enter_context(nc.psum_tensor("ps_mk", [128, 1], F32))
            with nc.Block() as blk:
                @blk.gpsimd
                def _(g):
                    g.wait_ge(cg.s, 2)
                    g.dma_start(self.mk_sb[:, :], self.mk_out[:, :]).then_inc(dg.s, 16)
                    g.wait_ge(dg.s, 32)

                @blk.tensor
                def _(t):
                    t.wait_ge(dg.s, 32)
                    t.matmul(ps_mk[:, :], self.onesf[:, :], self.mk_sb[:, :],
                             start=True, stop=True).then_inc(cg.s, 1)

                @blk.scalar
                def _(s):
                    s.wait_ge(cg.s, 3)
                    s.copy(self.maxk2[:, :], ps_mk[:, :])

    def proj_q(self):
        nc = self.nc
        xln = self.reg_bf
        n_slot = 64
        with contextlib.ExitStack() as p:
            wbuf = [p.enter_context(nc.sbuf_tensor(f"q_wb{i}", [128, 8192], BF16))
                    for i in range(2)]
            kbf = [p.enter_context(nc.sbuf_tensor(f"q_kb{i}", [128, 512], BF16))
                   for i in range(2)]
            ksq = p.enter_context(nc.sbuf_tensor("q_sq", [128, 512], BF16))
            nrm_s = p.enter_context(nc.sbuf_tensor("q_nrm", [128, 1], F32))
            ps = [p.enter_context(nc.psum_tensor(f"q_ps{i}", [128, 512], F32))
                  for i in range(2)]
            ps_n = p.enter_context(nc.psum_tensor("q_pn", [128, 512], F32))
            dW = [self.sem("q_dWa"), self.sem("q_dWb")]
            dS = [self.sem("q_dSa"), self.sem("q_dSb")]
            cPE = self.sem("q_cPE")
            cC = self.sem("q_cC")
            cSq = self.sem("q_cSq")
            cN = self.sem("q_cN")
            with nc.Block() as blk:
                @blk.gpsimd
                def _(g):
                    qs = list(range(32, 52))
                    for b in range(16):
                        if b >= 2:
                            g.wait_ge(cSq.s, 4 * b - 4)
                        g.dma_start(wbuf[b % 2][:, :],
                                    self.wq[b, :, :]).then_inc(dW[b % 2].s, 16)
                        for k in qs[b * 20 // 16:(b + 1) * 20 // 16]:
                            self.spill_load(g, k)
                    g.wait_ge(cSq.s, n_slot)

                @blk.tensor
                def _(t):
                    for s in range(n_slot):
                        dch, half = s // 2, s % 2
                        b, sub = dch // 2, dch % 2
                        t.wait_ge(dW[b % 2].s, 16 * (b // 2 + 1))
                        if s >= 2:
                            t.wait_ge(cSq.s, s - 1)
                        for cc in range(NCC):
                            lhs = wbuf[b % 2][:, cc * 256 + sub * 128:cc * 256 + sub * 128 + 128]
                            mm = t.matmul(ps[s % 2][:, :], lhs,
                                          xln[:, cc * NQ + half * 512:cc * NQ + half * 512 + 512],
                                          start=(cc == 0), stop=(cc == NCC - 1))
                        mm.then_inc(cPE.s, 1)
                        if s >= 1:
                            t.wait_ge(cSq.s, s)
                            if s >= 2:
                                t.wait_ge(cN.s, s - 1)
                            t.matmul(ps_n[:, :], self.ones_bf[:, :], ksq[:, :],
                                     start=True, stop=True).then_inc(cPE.s, 1)
                    t.wait_ge(cSq.s, n_slot)
                    t.wait_ge(cN.s, n_slot - 1)
                    t.matmul(ps_n[:, :], self.ones_bf[:, :], ksq[:, :],
                             start=True, stop=True).then_inc(cPE.s, 1)

                @blk.scalar
                def _(sc):
                    for s in range(n_slot):
                        dch = s // 2
                        val = 2 * (s + 1) - 1
                        sc.wait_ge(cPE.s, val)
                        if s >= 2:
                            sc.wait_ge(dS[s % 2].s, 16 * (s // 2))
                        cC.inc(sc.activation(kbf[s % 2][:, :], ps[s % 2][:, :], AF.Identity,
                                             bias=self.bqs[:, dch:dch + 1]))
                        sc.activation(ksq[:, :], ps[s % 2][:, :], AF.Square,
                                      bias=self.bqs[:, dch:dch + 1]).then_inc(cSq.s, 1)

                cNr = self.sem("q_cNr")

                @blk.vector
                def _(v):
                    for s in range(n_slot):
                        val = 2 * s + 3 if s < n_slot - 1 else 2 * n_slot
                        v.wait_ge(cPE.s, val)
                        v.reduce_max(nrm_s[:, :], ps_n[:, :], AX.X).then_inc(cNr.s, 1)
                        v.wait_ge(cNr.s, s + 1)
                        v.tensor_max(self.maxq2[:, :], self.maxq2[:, :],
                                     nrm_s[:, :]).then_inc(cN.s, 1)
                        if s < n_slot - 1:
                            v.wait_ge(cN.s, s + 1)

                @blk.sync
                def _(sy):
                    qs = list(range(32, 52))
                    for s in range(n_slot):
                        dch, half = s // 2, s % 2
                        sy.wait_ge(cC.s, s + 1)
                        sy.dma_start(self.q_scr[dch, :, half * 512:(half + 1) * 512],
                                     kbf[s % 2][:, :]).then_inc(dS[s % 2].s, 16)
                        for k in qs[s * 20 // n_slot:(s + 1) * 20 // n_slot]:
                            self.spill_store(sy, k)
                    sy.wait_ge(dS[0].s, 16 * (n_slot // 2))
                    sy.wait_ge(dS[1].s, 16 * (n_slot // 2))

    def p5b_bias(self):
        nc = self.nc
        cX = self.sem("p5b_c")
        with nc.Block() as blk:
            @blk.scalar
            def _(s):
                s.activation(self.maxq2[:, :], self.maxq2[:, :], AF.Sqrt)
                s.activation(self.maxk2[:, :], self.maxk2[:, :], AF.Sqrt).then_inc(cX.s, 1)

            @blk.vector
            def _(v):
                v.wait_ge(cX.s, 1)
                v.tensor_mul(self.bias_m[:, :], self.maxq2[:, :],
                             self.maxk2[:, :]).then_inc(cX.s, 1)
                v.wait_ge(cX.s, 2)
                v.tensor_scalar(self.bias_m[:, :], self.bias_m[:, :],
                                -SCALE * 1.02, None, ALU.mult).then_inc(cX.s, 1)

    def p6_attention(self):
        nc = self.nc
        x2 = self.reg

        heads = []
        v_idx = 0
        d_idx = 0
        for J in (0, 1):
            for kvh in range(NKVH):
                pair = J * NKVH + kvh
                for g in (0, 1):
                    hi = len(heads)
                    h = kvh + NKVH * g
                    visits = []
                    for side in (0, 1):
                        for w in range(4 * J + 4):
                            s0 = max(0, w - 4 * J)
                            masked = (w >= 4 * J)
                            visits.append(dict(
                                v=v_idx, side=side, w=w, s0=s0, masked=masked,
                                d=(d_idx if masked else None),
                                tile=side * (4 * J + 4) + w))
                            d_idx += 1 if masked else 0
                            v_idx += 1
                    heads.append(dict(hi=hi, J=J, kvh=kvh, g=g, h=h, pair=pair,
                                      visits=visits))
        n_heads = len(heads)
        n_pairs = 2 * NKVH
        exp_done = {v: v + 1 for v in range(v_idx)}

        with contextlib.ExitStack() as p:
            NKB, NQB, NPT = 3, 4, 4
            kbuf = [p.enter_context(nc.sbuf_tensor(f"p6k{i}", [128, 2048], BF16))
                    for i in range(NKB)]
            vbuf = [p.enter_context(nc.sbuf_tensor(f"p6v{i}", [128, 2048], BF16))
                    for i in range(NKB)]
            qbuf = [p.enter_context(nc.sbuf_tensor(f"p6q{i}", [128, 512], BF16))
                    for i in range(NQB)]
            xbuf = [p.enter_context(nc.sbuf_tensor(f"p6x{i}", [128, 512], F32))
                    for i in range(NQB)]
            pt = [p.enter_context(nc.sbuf_tensor(f"p6p{i}", [128, 512], BF16))
                  for i in range(NPT)]
            zl = [p.enter_context(nc.sbuf_tensor(f"p6zl{i}", [128, 512], F32))
                  for i in range(2)]
            tev = [p.enter_context(nc.sbuf_tensor(f"p6te{i}", [128, 512], F32))
                   for i in range(2)]
            psS = [p.enter_context(nc.psum_tensor(f"p6s{i}", [128, 512], F32))
                   for i in range(NPT)]
            psAV = [p.enter_context(nc.psum_tensor(f"p6av{i}", [128, 512], F32))
                    for i in range(2)]
            psZ = [p.enter_context(nc.psum_tensor(f"p6z{i}", [128, 512], F32))
                   for i in range(2)]
            dk = [self.sem(f"p6_dk{i}") for i in range(NKB)]
            dq = [self.sem(f"p6_dq{i}") for i in range(NQB)]
            cS = self.sem("p6_cS")
            cM = self.sem("p6_cM")
            cE = self.sem("p6_cE")
            cAV = self.sem("p6_cAV")
            cL = self.sem("p6_cL")
            cEv = self.sem("p6_cEv")
            cEvm = self.sem("p6_cEvm")

            kv_ready = {}
            q_ready = {}
            npar = [0] * NKB
            for J in (0, 1):
                for kvh in range(NKVH):
                    pair = J * NKVH + kvh
                    npar[pair % NKB] += 16 * (1 + 2 * (4 * J + 4))
                    kv_ready[pair] = npar[pair % NKB]
            hpar = [0] * NQB
            for hi in range(n_heads):
                hpar[hi % NQB] += 32
                q_ready[hi] = hpar[hi % NQB]

            with nc.Block() as blk:
                kv4 = self.kva_scr.rearrange("(r c p) k -> p r c k", r=2, p=128)
                p6ks = list(range(52, 64))

                @blk.sync
                def _(sy):
                    for J in (0, 1):
                        segw = 512 * (J + 1)
                        nt = 4 * J + 4
                        for kvh in range(NKVH):
                            pair = J * NKVH + kvh
                            if pair >= NKB:
                                sy.wait_ge(cEv.s, 2 * (pair - NKB + 1))
                            bi = pair % NKB
                            dst = kbuf[bi][:, 0:2 * segw].rearrange(
                                "p (s k) -> p s k", s=2)
                            src = kv4[:, :, kvh, 0:segw]
                            sy.dma_start(dst, src).then_inc(dk[bi].s, 16)
                            for side in (0, 1):
                                for w in range(nt):
                                    ti = side * nt + w
                                    sy.dma_start(
                                        vbuf[bi][:, ti * 128:(ti + 1) * 128],
                                        kv4[:, side, 16 + kvh,
                                            w * 128:w * 128 + 128],
                                        transpose=True).then_inc(dk[bi].s, 16)
                            for k in p6ks[pair * 12 // n_pairs:
                                          (pair + 1) * 12 // n_pairs]:
                                self.spill_store(sy, k)

                @blk.gpsimd
                def _(g):
                    for hd in heads:
                        hi, h, J = hd["hi"], hd["h"], hd["J"]
                        if hi >= NQB:
                            g.wait_ge(cEv.s, hi - NQB + 1)
                        g.dma_start(qbuf[hi % NQB][:, :],
                                    self.q_scr[h, :, J * 512:(J + 1) * 512]
                                    ).then_inc(dq[hi % NQB].s, 16)
                        g.dma_start(xbuf[hi % NQB][:, :],
                                    self.xt[h * 128:(h + 1) * 128,
                                            J * 512:(J + 1) * 512]
                                    ).then_inc(dq[hi % NQB].s, 16)
                        for k in p6ks[hi * 12 // n_heads:(hi + 1) * 12 // n_heads]:
                            self.spill_load(g, k)

                @blk.tensor
                def _(t):
                    def av_ones(hd, vis, first, last):
                        v, s0 = vis["v"], vis["s0"]
                        bi = hd["pair"] % NKB
                        hb = hd["hi"] % 2
                        cols = slice(s0 * 128, 512)
                        t.wait_ge(cE.s, exp_done[v])
                        if first and hd["hi"] >= 2:
                            t.wait_ge(cEv.s, hd["hi"] - 1)
                        t.matmul(psAV[hb][:, cols],
                                 vbuf[bi][:, vis["tile"] * 128:(vis["tile"] + 1) * 128],
                                 pt[v % NPT][:, cols], start=first, stop=last)
                        t.matmul(psZ[hb][:, cols], self.ones_bf[:, :],
                                 pt[v % NPT][:, cols], start=first,
                                 stop=last).then_inc(cAV.s, 1)

                    for hd in heads:
                        hi = hd["hi"]
                        bi = hd["pair"] % NKB
                        t.wait_ge(dk[hd["pair"] % NKB].s, kv_ready[hd["pair"]])
                        t.wait_ge(dq[hi % NQB].s, q_ready[hi])
                        vl = hd["visits"]
                        for i, vis in enumerate(vl):
                            v, s0 = vis["v"], vis["s0"]
                            cols = slice(s0 * 128, 512)
                            if v >= NPT:
                                t.wait_ge(cE.s, exp_done[v - NPT])
                            t.matmul(psS[v % NPT][:, cols],
                                     kbuf[bi][:, vis["tile"] * 128:(vis["tile"] + 1) * 128],
                                     qbuf[hi % NQB][:, cols],
                                     start=True, stop=True).then_inc(cS.s, 1)
                            if i >= 2:
                                av_ones(hd, vl[i - 2], i == 2, False)
                        if len(vl) >= 2:
                            av_ones(hd, vl[-2], len(vl) == 2, False)
                        av_ones(hd, vl[-1], len(vl) == 1, True)

                @blk.scalar
                def _(sc):
                    for hd in heads:
                        hi = hd["hi"]
                        for vis in hd["visits"]:
                            v, s0 = vis["v"], vis["s0"]
                            sc.wait_ge(cS.s, v + 1)
                            if vis["masked"]:
                                sc.wait_ge(cM.s, vis["d"] + 1)
                            if v >= NPT:
                                sc.wait_ge(cAV.s, v - NPT + 1)
                            sc.activation(pt[v % NPT][:, s0 * 128:512],
                                          psS[v % NPT][:, s0 * 128:512],
                                          AF.Exp, bias=self.bias_m[:, :],
                                          scale=self.scale_t[:, :]).then_inc(cE.s, 1)
                        hb = hi % 2
                        sc.wait_ge(cAV.s, hd["visits"][-1]["v"] + 1)
                        sc.activation(zl[hb][:, :], psZ[hb][:, :],
                                      AF.Ln).then_inc(cL.s, 1)
                        sc.wait_ge(cL.s, 2 * hi + 1)
                        sc.activation(zl[hb][:, :], zl[hb][:, :], AF.Exp,
                                      scale=self.neg_one[:, :]).then_inc(cL.s, 1)

                @blk.vector
                def _(v_):
                    for hd in heads:
                        hi, h, J = hd["hi"], hd["h"], hd["J"]
                        for vis in hd["visits"]:
                            if not vis["masked"]:
                                continue
                            v, s0 = vis["v"], vis["s0"]
                            msk = self.maskA if vis["side"] == 0 else self.maskB
                            v_.wait_ge(cS.s, v + 1)
                            v_.tensor_add(psS[v % NPT][:, s0 * 128:(s0 + 1) * 128],
                                          psS[v % NPT][:, s0 * 128:(s0 + 1) * 128],
                                          msk[:, :]).then_inc(cM.s, 1)
                        hb = hi % 2
                        v_.wait_ge(cL.s, 2 * (hi + 1))
                        v_.wait_ge(dq[hi % NQB].s, q_ready[hi])
                        v_.tensor_mul(tev[hb][:, :], psAV[hb][:, :],
                                      zl[hb][:, :]).then_inc(cEvm.s, 1)
                        v_.wait_ge(cEvm.s, hi + 1)
                        v_.tensor_add(x2[:, h * 1024 + J * 512:h * 1024 + J * 512 + 512],
                                      tev[hb][:, :], xbuf[hi % NQB][:, :]).then_inc(cEv.s, 1)

    def p7_ln2(self):
        nc = self.nc
        x2 = self.reg
        with contextlib.ExitStack() as p:
            x2bf = [p.enter_context(nc.sbuf_tensor(f"p7b{i}", [128, NQ], BF16))
                    for i in range(2)]
            sqb = [p.enter_context(nc.sbuf_tensor(f"p7s{i}", [128, NQ], BF16))
                   for i in range(2)]
            t1b = [p.enter_context(nc.sbuf_tensor(f"p7t{i}", [128, NQ], F32))
                   for i in range(2)]
            ps_s1 = p.enter_context(nc.psum_tensor("p7ps1", [128, NQ], F32))
            ps_s2 = p.enter_context(nc.psum_tensor("p7ps2", [128, NQ], F32))
            self.mu2 = p.enter_context(nc.sbuf_tensor("p7mu", [128, NQ], F32))
            self.a2 = p.enter_context(nc.sbuf_tensor("p7a", [128, NQ], F32))
            te2 = p.enter_context(nc.sbuf_tensor("p7e2", [128, NQ], F32))
            cCp = self.sem("p7_cCp")
            cSq = self.sem("p7_cSq")
            cMM = self.sem("p7_cMM")
            dS = self.sem("p7_dS")
            cR = self.sem("p7_cR")
            cAp = self.sem("p7_cAp")
            dg7 = self.sem("p7_dg")
            with nc.Block() as blk:
                @blk.gpsimd
                def _(g):
                    dg7.dma(g.dma_start(self.gates[:, :], self.gate_w[:, :]))
                    g.wait_ge(dg7.s, dg7.n)

                @blk.vector
                def _(v):
                    for cc in range(NCC):
                        if cc >= 2:
                            v.wait_ge(cMM.s, cc - 1)
                        v.tensor_copy(x2bf[cc % 2][:, :],
                                      x2[:, cc * NQ:(cc + 1) * NQ]).then_inc(cCp.s, 1)

                @blk.scalar
                def _(s):
                    for cc in range(NCC):
                        if cc >= 2:
                            s.wait_ge(cMM.s, cc - 1)
                        s.activation(sqb[cc % 2][:, :], x2[:, cc * NQ:(cc + 1) * NQ],
                                     AF.Square).then_inc(cSq.s, 1)

                @blk.tensor
                def _(t):
                    for cc in range(NCC):
                        t.wait_ge(cCp.s, cc + 1)
                        t.wait_ge(cSq.s, cc + 1)
                        for n in range(2):
                            t.matmul(ps_s1[:, 512 * n:512 * (n + 1)], self.ones_bf[:, :],
                                     x2bf[cc % 2][:, 512 * n:512 * (n + 1)],
                                     start=(cc == 0), stop=(cc == NCC - 1))
                            mm = t.matmul(ps_s2[:, 512 * n:512 * (n + 1)], self.ones_bf[:, :],
                                          sqb[cc % 2][:, 512 * n:512 * (n + 1)],
                                          start=(cc == 0), stop=(cc == NCC - 1))
                        mm.then_inc(cMM.s, 1)

                @blk.sync
                def _(sy):
                    for cc in range(NCC):
                        sy.dma_start(self.x2_scr[cc, :, :],
                                     x2[:, cc * NQ:(cc + 1) * NQ]).then_inc(dS.s, 16)
                    sy.wait_ge(dS.s, 16 * NCC)

            with nc.Block() as blk:
                @blk.vector
                def _(v):
                    v.tensor_scalar(self.mu2[:, :], ps_s1[:, :], 1.0 / C,
                                    None, ALU.mult).then_inc(cR.s, 1)
                    v.tensor_scalar(te2[:, :], ps_s2[:, :], 1.0 / C,
                                    None, ALU.mult).then_inc(cR.s, 1)
                    v.wait_ge(cR.s, 2)
                    v.tensor_mul(self.a2[:, :], self.mu2[:, :],
                                 self.mu2[:, :]).then_inc(cR.s, 1)
                    v.wait_ge(cR.s, 3)
                    v.tensor_sub(te2[:, :], te2[:, :], self.a2[:, :]).then_inc(cR.s, 1)

                @blk.scalar
                def _(s):
                    s.wait_ge(cR.s, 4)
                    s.activation(te2[:, :], te2[:, :], AF.Ln,
                                 bias=self.eps_t[:, :]).then_inc(cR.s, 1)
                    s.wait_ge(cR.s, 5)
                    s.activation(self.a2[:, :], te2[:, :], AF.Exp,
                                 scale=self.neg_half[:, :])

            hv = self.reg_bf
            ps_g = p.enter_context(nc.psum_tensor("p7pg", [8, NQ], F32))
            cGt = self.sem("p7_gt")
            with nc.Block() as blk:
                @blk.vector
                def _(v):
                    for cc in range(NCC):
                        v.tensor_sub(t1b[cc % 2][:, :], x2[:, cc * NQ:(cc + 1) * NQ],
                                     self.mu2[:, :]).then_inc(cAp.s, 1)
                        v.wait_ge(cAp.s, 2 * cc + 1)
                        v.tensor_mul(hv[:, cc * 2048:cc * 2048 + 1024],
                                     t1b[cc % 2][:, :], self.a2[:, :]).then_inc(cAp.s, 1)

                @blk.tensor
                def _(t):
                    for cc in range(NCC):
                        t.wait_ge(cAp.s, 2 * (cc + 1))
                        for n in range(2):
                            mm = t.matmul(ps_g[:, 512 * n:512 * (n + 1)],
                                          self.gates[:, cc * 8:(cc + 1) * 8],
                                          hv[:, cc * 2048 + 512 * n:
                                             cc * 2048 + 512 * (n + 1)],
                                          start=(cc == 0), stop=(cc == NCC - 1))
                    mm.then_inc(cGt.s, 1)

                @blk.scalar
                def _(s):
                    s.wait_ge(cGt.s, 1)
                    s.copy(self.lgf[:, :], ps_g[:, :])

    def p8_ffn(self):
        nc = self.nc
        hv = self.reg_bf
        comb = self.reg

        def hslice(cc, th):
            return hv[:, cc * 2048 + th * 512: cc * 2048 + th * 512 + 512]

        def cslice(hc):
            return comb[:, hc * NQ + 512: hc * NQ + 1024]

        br = self.br
        self.p8a_routing(br)
        self.lgf_es.close()
        for th in (0, 1):
            self.p8b_updown(th, hslice, cslice, None)
            self.p8c_routed(th, hslice, cslice, br)
            self.p8d_ln3_out(th, cslice)

    def p8a_routing(self, br):
        nc = self.nc
        with contextlib.ExitStack() as p:
            lgf = self.lgf
            lgT = p.enter_context(nc.sbuf_tensor("p8lgT", [128, 64], F32))
            scr1 = p.enter_context(nc.sbuf_tensor("p8r1", [128, 8], F32))
            scr2 = p.enter_context(nc.sbuf_tensor("p8r2", [128, 8], F32))
            r_all = p.enter_context(nc.sbuf_tensor("p8ra", [128, 8], F32))
            rT2 = p.enter_context(nc.sbuf_tensor("p8rT", [1, NQ], F32))
            onesf1 = p.enter_context(nc.sbuf_tensor("p8o1", [1, 128], F32))
            ps_t = [p.enter_context(nc.psum_tensor(f"p8pt{i}", [128, 8], F32))
                    for i in range(2)]
            ps_r = [p.enter_context(nc.psum_tensor(f"p8pr{i}", [1, 128], F32))
                    for i in range(2)]
            ps_b = p.enter_context(nc.psum_tensor("p8pb", [128, NQ], F32))
            cv = self.sem("p8a_cv")
            ct = self.sem("p8a_ct")
            ce = self.sem("p8a_ce")
            with nc.Block() as blk:
                @blk.tensor
                def _(t):
                    for u in range(8):
                        if u >= 2:
                            t.wait_ge(ce.s, u - 1)
                        t.transpose(ps_t[u % 2][:, :], lgf[:, u * 128:(u + 1) * 128],
                                    self.ident_f[0:8, 0:8]).then_inc(ct.s, 1)

                @blk.scalar
                def _(s):
                    for u in range(8):
                        s.wait_ge(ct.s, u + 1)
                        s.copy(lgT[:, u * 8:(u + 1) * 8],
                               ps_t[u % 2][:, :]).then_inc(ce.s, 1)

                @blk.vector
                def _(v):
                    n = 0
                    for u in range(8):
                        lg_u = lgT[:, u * 8:(u + 1) * 8]
                        v.wait_ge(ce.s, u + 1)
                        if n:
                            v.wait_ge(cv.s, n)
                        v.reduce_max(scr1[:, 0:1], lg_u, AX.X).then_inc(cv.s, 1)
                        n += 1
                        v.wait_ge(cv.s, n)
                        v.tensor_scalar(scr2[:, :], lg_u, scr1[:, 0:1], None,
                                        ALU.is_equal).then_inc(cv.s, 1)
                        n += 1
                        v.wait_ge(cv.s, n)
                        v.scalar_tensor_tensor(scr2[:, :], scr2[:, :], -1e30, lg_u,
                                               ALU.mult, ALU.add).then_inc(cv.s, 1)
                        n += 1
                        v.wait_ge(cv.s, n)
                        v.reduce_max(scr1[:, 1:2], scr2[:, :], AX.X).then_inc(cv.s, 1)
                        n += 1
                        v.wait_ge(cv.s, n)
                        v.tensor_add(r_all[:, u:u + 1], scr1[:, 0:1],
                                     scr1[:, 1:2]).then_inc(cv.s, 1)
                        n += 1

            cb = self.sem("p8a_cb")
            ct2 = self.sem("p8a_ct2")
            ce2 = self.sem("p8a_ce2")
            with nc.Block() as blk:
                @blk.gpsimd
                def _(g):
                    g.memset(onesf1[:, :], 1.0).then_inc(cb.s, 1)

                @blk.tensor
                def _(t):
                    for u in range(8):
                        if u >= 2:
                            t.wait_ge(ce2.s, u - 1)
                        t.transpose(ps_r[u % 2][:, :], r_all[:, u:u + 1],
                                    self.ident_f[:, :]).then_inc(ct2.s, 1)

                @blk.scalar
                def _(s):
                    for u in range(8):
                        s.wait_ge(ct2.s, u + 1)
                        s.copy(rT2[0:1, u * 128:(u + 1) * 128],
                               ps_r[u % 2][:, :]).then_inc(ce2.s, 1)

            with nc.Block() as blk:
                @blk.tensor
                def _(t):
                    for n in range(2):
                        mm = t.matmul(ps_b[:, 512 * n:512 * (n + 1)], onesf1[:, :],
                                      rT2[0:1, 512 * n:512 * (n + 1)],
                                      start=True, stop=True)
                    mm.then_inc(cb.s, 2)

                @blk.vector
                def _(v):
                    v.wait_ge(cb.s, 3)
                    v.tensor_copy(br[:, :], ps_b[:, :])

    def p8b_updown(self, th, hslice, cslice, _unused):
        nc = self.nc
        with contextlib.ExitStack() as p:
            wub = [p.enter_context(nc.sbuf_tensor(f"p8wu{th}{i}", [128, 4096], BF16))
                   for i in range(2)]
            NWD = 3
            wdb = [p.enter_context(nc.sbuf_tensor(f"p8wd{th}{i}", [128, 2048], BF16))
                   for i in range(NWD)]
            wdf = [p.enter_context(nc.sbuf_tensor(f"p8wf{th}{i}", [128, 2048], F32))
                   for i in range(NWD)]
            ubuf = p.enter_context(nc.sbuf_tensor(f"p8u{th}", [128, 16 * 512], BF16))
            uscr = [p.enter_context(nc.sbuf_tensor(f"p8us{th}{i}", [128, 512], F32))
                    for i in range(2)]
            psU = [p.enter_context(nc.psum_tensor(f"p8pu{th}{i}", [128, 512], F32))
                   for i in range(2)]
            psD = [p.enter_context(nc.psum_tensor(f"p8pd{th}{i}", [128, 512], F32))
                   for i in range(2)]
            dU = [self.sem("p8b_dUa"), self.sem("p8b_dUb")]
            cPE = self.sem("p8b_cPE")
            cRl = self.sem("p8b_cRl")
            cSq = self.sem("p8b_cSq")
            dD = [self.sem(f"p8b_dD{i}") for i in range(3)]
            cPD = self.sem("p8b_cPD")
            cEv = self.sem("p8b_cEv")
            cDc = self.sem("p8b_cDc")
            bU = [dU[0].n, dU[1].n]
            bD = [d.n for d in dD]
            bPE, bRl, bSq, bPD, bEv = cPE.n, cRl.n, cSq.n, cPD.n, cEv.n
            bDc = cDc.n
            with nc.Block() as blk:
                @blk.gpsimd
                def _(g):
                    for ib in range(4):
                        for j in range(16):
                            gj = 16 * ib + j
                            if gj >= 2:
                                g.wait_ge(cPE.s, bPE + gj - 1)
                            g.dma_start(wub[gj % 2][:, :],
                                        self.up_bf[gj, :, :]).then_inc(dU[gj % 2].s, 16)
                    for z in range(2):
                        dU[z].n += 16 * 32

                @blk.sync
                def _(sy):
                    for ib in range(4):
                        for hh in range(32):
                            ghh = 32 * ib + hh
                            if ghh >= NWD:
                                sy.wait_ge(cDc.s, bDc + ghh - NWD + 1)
                            sy.dma_start(wdf[ghh % NWD][:, :],
                                         self.dwn_w[hh, :, ib * 2048:(ib + 1) * 2048]
                                         ).then_inc(dD[ghh % NWD].s, 16)
                    for z in range(3):
                        dD[z].n += 16 * len([i for i in range(128) if i % 3 == z])

                @blk.tensor
                def _(t):
                    for ib in range(4):
                        for ic in range(16):
                            gic = 16 * ib + ic
                            t.wait_ge(dU[gic % 2].s, bU[gic % 2] + 16 * (gic // 2 + 1))
                            if gic >= 2:
                                t.wait_ge(cRl.s, bRl + gic - 1)
                            wsl = wub[gic % 2]
                            for cc in range(NCC):
                                mm = t.matmul(psU[gic % 2][:, :],
                                              wsl[:, cc * 128:(cc + 1) * 128],
                                              hslice(cc, th),
                                              start=(cc == 0), stop=(cc == NCC - 1))
                            mm.then_inc(cPE.s, 1)
                        for hh in range(32):
                            ghh = 32 * ib + hh
                            t.wait_ge(cDc.s, bDc + ghh + 1)
                            t.wait_ge(cSq.s, bSq + 16 * (ib + 1))
                            if ghh >= 2:
                                t.wait_ge(cEv.s, bEv + ghh - 1)
                            wsl = wdb[ghh % NWD]
                            for ic in range(16):
                                mm = t.matmul(psD[ghh % 2][:, :],
                                              wsl[:, ic * 128:(ic + 1) * 128],
                                              ubuf[:, ic * 512:(ic + 1) * 512],
                                              start=(ic == 0), stop=(ic == 15))
                            mm.then_inc(cPD.s, 1)
                    cPE.n += 64
                    cPD.n += 128

                @blk.vector
                def _(v):
                    for ib in range(4):
                        for ic in range(16):
                            gic = 16 * ib + ic
                            v.wait_ge(cPE.s, bPE + gic + 1)
                            if gic >= 2:
                                v.wait_ge(cSq.s, bSq + gic - 1)
                            v.tensor_scalar(uscr[gic % 2][:, :], psU[gic % 2][:, :],
                                            0.0, None, ALU.max).then_inc(cRl.s, 1)
                        def cast_dn(ghh2):
                            nth = len([i for i in range(ghh2 + 1) if i % 3 == ghh2 % 3])
                            v.wait_ge(dD[ghh2 % 3].s, bD[ghh2 % 3] + 16 * nth)
                            if ghh2 >= NWD:
                                v.wait_ge(cPD.s, bPD + ghh2 - NWD + 1)
                            v.tensor_copy(wdb[ghh2 % NWD][:, :],
                                          wdf[ghh2 % NWD][:, :]).then_inc(cDc.s, 1)

                        for hh in range(32):
                            ghh = 32 * ib + hh
                            if ghh == 0:
                                for kk in range(NWD):
                                    cast_dn(kk)
                            if ghh + NWD < 128:
                                cast_dn(ghh + NWD)
                            v.wait_ge(cPD.s, bPD + ghh + 1)
                            if ib == 0:
                                v.tensor_copy(cslice(hh), psD[ghh % 2][:, :]).then_inc(cEv.s, 1)
                            else:
                                v.tensor_add(cslice(hh), cslice(hh),
                                             psD[ghh % 2][:, :]).then_inc(cEv.s, 1)
                    cRl.n += 64
                    cEv.n += 128
                    cDc.n += 128

                @blk.scalar
                def _(s):
                    for ib in range(4):
                        for ic in range(16):
                            gic = 16 * ib + ic
                            s.wait_ge(cRl.s, bRl + gic + 1)
                            if ib > 0:
                                s.wait_ge(cPD.s, bPD + 32 * ib)
                            s.activation(ubuf[:, ic * 512:(ic + 1) * 512],
                                         uscr[gic % 2][:, :], AF.Square).then_inc(cSq.s, 1)
                    cSq.n += 64

    def p8c_routed(self, th, hslice, cslice, br):
        nc = self.nc
        with contextlib.ExitStack() as p:
            wb = [p.enter_context(nc.sbuf_tensor(f"p8cw{th}{i}", [128, 4096], BF16))
                  for i in range(2)]
            xlat = p.enter_context(nc.sbuf_tensor(f"p8xl{th}", [128, 16 * 512], BF16))
            moe = p.enter_context(nc.sbuf_tensor(f"p8mo{th}", [128, 16 * 512], BF16))
            psF = [p.enter_context(nc.psum_tensor(f"p8pf{th}{i}", [128, 512], F32))
                   for i in range(2)]
            dW = [self.sem("p8c_dWa"), self.sem("p8c_dWb")]
            cPE = self.sem("p8c_cPE")
            cEv = self.sem("p8c_cEv")

            def stage(nin, nk, wsrc, wwidth, rhs_fn, evac_fn):
                bW = [dW[0].n, dW[1].n]
                bPE, bEv = cPE.n, cEv.n
                with nc.Block() as blk:
                    @blk.gpsimd
                    def _(g):
                        for j in range(nin):
                            if j >= 2:
                                g.wait_ge(cEv.s, bEv + j - 1)
                            g.dma_start(wb[j % 2][:, 0:wwidth],
                                        wsrc[j, :, :]).then_inc(dW[j % 2].s, 16)
                        for z in range(2):
                            dW[z].n += 16 * (nin // 2)

                    @blk.tensor
                    def _(t):
                        for j in range(nin):
                            t.wait_ge(dW[j % 2].s, bW[j % 2] + 16 * (j // 2 + 1))
                            if j >= 2:
                                t.wait_ge(cEv.s, bEv + j - 1)
                            for k in range(nk):
                                mm = t.matmul(psF[j % 2][:, :],
                                              wb[j % 2][:, k * 128:(k + 1) * 128],
                                              rhs_fn(k),
                                              start=(k == 0), stop=(k == nk - 1))
                            mm.then_inc(cPE.s, 1)
                        cPE.n += nin

                    evac_fn(blk, nin, bPE)

            def evac_fc1(blk, nin, bPE):
                @blk.scalar
                def _(s):
                    for j in range(nin):
                        s.wait_ge(cPE.s, bPE + j + 1)
                        s.copy(xlat[:, j * 512:(j + 1) * 512],
                               psF[j % 2][:, :]).then_inc(cEv.s, 1)
                    cEv.n += nin

            stage(16, NCC, self.fc1_w, 4096, lambda cc: hslice(cc, th), evac_fc1)

            def evac_ew(blk, nin, bPE):
                @blk.vector
                def _(v):
                    for j in range(nin):
                        v.wait_ge(cPE.s, bPE + j + 1)
                        v.tensor_mul(moe[:, j * 512:(j + 1) * 512], psF[j % 2][:, :],
                                     br[:, th * 512:(th + 1) * 512]).then_inc(cEv.s, 1)
                    cEv.n += nin

            stage(16, NLC, self.ew_w, 2048,
                  lambda lc: xlat[:, lc * 512:(lc + 1) * 512], evac_ew)

            def evac_fc2(blk, nin, bPE):
                @blk.vector
                def _(v):
                    for j in range(nin):
                        v.wait_ge(cPE.s, bPE + j + 1)
                        v.tensor_add(cslice(j), cslice(j),
                                     psF[j % 2][:, :]).then_inc(cEv.s, 1)
                    cEv.n += nin

            stage(NCC, NLC, self.fc2_w, 2048,
                  lambda lc: moe[:, lc * 512:(lc + 1) * 512], evac_fc2)

    def p8d_ln3_out(self, th, cslice):
        nc = self.nc
        with contextlib.ExitStack() as p:
            cbf = [p.enter_context(nc.sbuf_tensor(f"p8db{th}{i}", [128, 512], BF16))
                   for i in range(2)]
            sqb = [p.enter_context(nc.sbuf_tensor(f"p8ds{th}{i}", [128, 512], BF16))
                   for i in range(2)]
            t1b = [p.enter_context(nc.sbuf_tensor(f"p8dt{th}{i}", [128, 512], F32))
                   for i in range(2)]
            t2b = [p.enter_context(nc.sbuf_tensor(f"p8dg{th}{i}", [128, 512], F32))
                   for i in range(2)]
            xrb = [p.enter_context(nc.sbuf_tensor(f"p8dx{th}{i}", [128, 512], F32))
                   for i in range(2)]
            obuf = [p.enter_context(nc.sbuf_tensor(f"p8do{th}{i}", [128, 512], F32))
                    for i in range(2)]
            mu3 = p.enter_context(nc.sbuf_tensor(f"p8dm{th}", [128, 512], F32))
            a3 = p.enter_context(nc.sbuf_tensor(f"p8da{th}", [128, 512], F32))
            te3 = p.enter_context(nc.sbuf_tensor(f"p8de{th}", [128, 512], F32))
            ps1 = p.enter_context(nc.psum_tensor(f"p8dp1{th}", [128, 512], F32))
            ps2 = p.enter_context(nc.psum_tensor(f"p8dp2{th}", [128, 512], F32))
            cCp = self.sem("p8d_cCp")
            cSq = self.sem("p8d_cSq")
            cMM = self.sem("p8d_cMM")
            cR = self.sem("p8d_cR")
            dX = [self.sem("p8d_dXa"), self.sem("p8d_dXb")]
            cT = self.sem("p8d_cT")
            cG = self.sem("p8d_cG")
            cO = self.sem("p8d_cO")
            dOut = [self.sem("p8d_dOa"), self.sem("p8d_dOb")]
            bCp, bSq, bMM, bR = cCp.n, cSq.n, cMM.n, cR.n
            bT, bG, bO = cT.n, cG.n, cO.n
            bX = [dX[0].n, dX[1].n]
            bOut = [dOut[0].n, dOut[1].n]
            with nc.Block() as blk:
                @blk.vector
                def _(v):
                    for hc in range(NCC):
                        if hc >= 2:
                            v.wait_ge(cMM.s, bMM + hc - 1)
                        v.tensor_copy(cbf[hc % 2][:, :], cslice(hc)).then_inc(cCp.s, 1)
                    cCp.n += NCC

                @blk.scalar
                def _(s):
                    for hc in range(NCC):
                        if hc >= 2:
                            s.wait_ge(cMM.s, bMM + hc - 1)
                        s.activation(sqb[hc % 2][:, :], cslice(hc),
                                     AF.Square).then_inc(cSq.s, 1)
                    cSq.n += NCC

                @blk.tensor
                def _(t):
                    for hc in range(NCC):
                        t.wait_ge(cCp.s, bCp + hc + 1)
                        t.wait_ge(cSq.s, bSq + hc + 1)
                        t.matmul(ps1[:, :], self.ones_bf[:, :], cbf[hc % 2][:, :],
                                 start=(hc == 0), stop=(hc == NCC - 1))
                        t.matmul(ps2[:, :], self.ones_bf[:, :], sqb[hc % 2][:, :],
                                 start=(hc == 0), stop=(hc == NCC - 1)).then_inc(cMM.s, 1)
                    cMM.n += NCC

            with nc.Block() as blk:
                @blk.vector
                def _(v):
                    v.tensor_scalar(mu3[:, :], ps1[:, :], 1.0 / C,
                                    None, ALU.mult).then_inc(cR.s, 1)
                    v.tensor_scalar(te3[:, :], ps2[:, :], 1.0 / C,
                                    None, ALU.mult).then_inc(cR.s, 1)
                    v.wait_ge(cR.s, bR + 2)
                    v.tensor_mul(a3[:, :], mu3[:, :], mu3[:, :]).then_inc(cR.s, 1)
                    v.wait_ge(cR.s, bR + 3)
                    v.tensor_sub(te3[:, :], te3[:, :], a3[:, :]).then_inc(cR.s, 1)
                    cR.n += 4

                @blk.scalar
                def _(s):
                    s.wait_ge(cR.s, bR + 4)
                    s.activation(te3[:, :], te3[:, :], AF.Ln,
                                 bias=self.eps_t[:, :]).then_inc(cR.s, 1)
                    cR.n += 1
                    s.wait_ge(cR.s, bR + 5)
                    s.activation(a3[:, :], te3[:, :], AF.Exp, scale=self.neg_half[:, :])

            with nc.Block() as blk:
                @blk.sync
                def _(sy):
                    for hc in range(NCC):
                        if hc >= 2:
                            sy.wait_ge(cO.s, bO + hc - 1)
                        sy.dma_start(xrb[hc % 2][:, :],
                                     self.x2_scr[hc, :, th * 512:(th + 1) * 512]
                                     ).then_inc(dX[hc % 2].s, 16)
                    for z in range(2):
                        dX[z].n += 16 * (NCC // 2)
                    sy.wait_ge(cO.s, bO + NCC)

                @blk.vector
                def _(v):
                    for hc in range(NCC):
                        if hc >= 2:
                            v.wait_ge(dOut[hc % 2].s,
                                      bOut[hc % 2] + 16 * (hc // 2))
                        v.tensor_sub(t1b[hc % 2][:, :], cslice(hc), mu3[:, :])
                        v.tensor_mul(t2b[hc % 2][:, :], t1b[hc % 2][:, :], a3[:, :])
                        v.tensor_scalar(t1b[hc % 2][:, :], t2b[hc % 2][:, :],
                                        self.gams[:, hc:hc + 1],
                                        self.bets[:, hc:hc + 1], ALU.mult, ALU.add)
                        v.wait_ge(dX[hc % 2].s, bX[hc % 2] + 16 * (hc // 2 + 1))
                        v.tensor_add(obuf[hc % 2][:, :], t1b[hc % 2][:, :],
                                     xrb[hc % 2][:, :]).then_inc(cO.s, 1)
                    cO.n += NCC

                @blk.gpsimd
                def _(g):
                    for hc in range(NCC):
                        g.wait_ge(cO.s, bO + hc + 1)
                        g.dma_start(self.out_t[hc * 128:(hc + 1) * 128,
                                               th * 512:(th + 1) * 512],
                                    obuf[hc % 2][:, :]).then_inc(dOut[hc % 2].s, 16)
                    for z in range(2):
                        dOut[z].n += 16 * (NCC // 2)
                        g.wait_ge(dOut[z].s, bOut[z] + 16 * (NCC // 2))

    def dbg_dumps(self):
        if not _DBG:
            return
        nc = self.nc
        dD = self.sem("dbg_d")
        with nc.Block() as blk:
            @blk.sync
            def _(sy):
                if 2 <= _DBG_PHASE <= 6:
                    dD.dma(sy.dma_start(self.dbg_xln[:, :], self.reg_bf[:, :]))
                pairs = [(0, self.bias_m[:, :]), (1, self.bias_o[:, :]),
                         (2, self.maxq2[:, :]), (3, self.maxk2[:, :])]
                for i, src in pairs:
                    dD.dma(sy.dma_start(self.dbg_misc[:, i:i + 1], src,
                                        allow_slow_non_contiguous=True))
                sy.wait_ge(dD.s, dD.n)


def _build():
    return Builder().build()


def _enable_ldw_opt():
    try:
        from concourse.compiler_utils import get_compiler_flags, set_compiler_flags
        flags = [f.replace("--enable-ldw-opt=false", "--enable-ldw-opt=true")
                 for f in get_compiler_flags()]
        set_compiler_flags(flags)
    except Exception:
        pass


def kernel(**inputs):
    _enable_ldw_opt()
    in_maps = _prep(inputs)
    nc = _build()
    res = run_bass_kernel_spmd(nc, in_maps, core_ids=list(range(8)))
    return _reassemble([np.asarray(r["out"]) for r in res.results])


# version 17
# speedup vs baseline: 1.1631x; 1.1631x over previous
import math
import contextlib
import numpy as np
import concourse.bass as bass
import concourse.mybir as mybir
from concourse.bass_utils import run_bass_kernel_spmd

F32 = mybir.dt.float32
BF16 = mybir.dt.bfloat16
AF = mybir.ActivationFunctionType
ALU = mybir.AluOpType
AX = mybir.AxisListType

C = 4096
T = 2048
NQ = 1024
KVD = 2048
NCC = C // 128
NKVC = KVD // 128
NH = 32
NKVH = 16
INT = 8192
LAT = 2048
NIC = INT // 128
NLC = LAT // 128
SCALE = 1.0 / math.sqrt(32.0)
EPS = 1e-5

_DBG_PHASE = 99
_DBG = False


def _img_T(W, blkw):
    dout, cin = W.shape
    nb, ncc = dout // blkw, cin // 128
    return np.ascontiguousarray(
        W.reshape(nb, blkw, ncc, 128).transpose(0, 3, 2, 1)).reshape(nb, 128, ncc * blkw)


def _img_N(W, blkw):
    cin, dout = W.shape
    nb, ncc = dout // blkw, cin // 128
    return np.ascontiguousarray(
        W.reshape(ncc, 128, nb, blkw).transpose(2, 1, 0, 3)).reshape(nb, 128, ncc * blkw)


def _prep(inputs):
    g = lambda k: np.asarray(inputs[k], dtype=np.float32)
    x = g("L0_attn_x")
    shared = dict(
        wq=_img_T(g("L0_attn_Wq"), 256),
        wk=_img_T(g("L0_attn_Wk"), 256),
        wv=_img_T(g("L0_attn_Wv"), 256),
        up=_img_N(g("L0_ffn_up_w"), 128),
        dwn=_img_N(g("L0_ffn_down_w"), 128),
        fc1=_img_N(g("L0_ffn_fc1_w"), 128),
        fc2=_img_N(g("L0_ffn_fc2_w"), 128),
        ew=_img_N(g("L0_ffn_expert_w"), 128),
        gate=np.ascontiguousarray(
            g("L0_ffn_gate_w").reshape(32, 128, 8).transpose(1, 0, 2)).reshape(128, 256),
        bq=np.ascontiguousarray(g("L0_attn_bq").reshape(32, 128).T),
        bk=np.ascontiguousarray(g("L0_attn_bk").reshape(16, 128).T),
        bv=np.ascontiguousarray(g("L0_attn_bv").reshape(16, 128).T),
        gam=np.ascontiguousarray(g("L0_ffn_ln_gamma").reshape(32, 128).T),
        bet=np.ascontiguousarray(g("L0_ffn_ln_beta").reshape(32, 128).T),
    )
    in_maps = []
    for c in range(8):
        s, e = c // 2, c % 2
        perm = list(range(e, 16, 2))
        xtc = np.ascontiguousarray(x[s].reshape(16, 128, C)[perm].reshape(NQ, C).T)
        m = dict(shared)
        m["xt"] = xtc
        m["epar"] = np.full((128, 1), float(e), np.float32)
        in_maps.append(m)
    return in_maps


def _reassemble(couts):
    out = np.empty((4, T, C), np.float32)
    for c in range(8):
        s, e = c // 2, c % 2
        co = couts[c]
        for u in range(8):
            out[s, (2 * u + e) * 128:(2 * u + e + 1) * 128, :] = \
                co[:, u * 128:(u + 1) * 128].T
    return out


class Sem:

    def __init__(self, es, nc, name):
        self.s = es.enter_context(nc.semaphore(name))
        self.n = 0

    def inc(self, instr, k=1):
        instr.then_inc(self.s, k)
        self.n += k
        return self.n

    def dma(self, instr):
        return self.inc(instr, 16)


class Builder:
    def __init__(self):
        self.nc = bass.Bass("TRN2", target_bir_lowering=False)

    def sem(self, name):
        if not hasattr(self, "_sems"):
            self._sems = {}
        if name not in self._sems:
            self._sems[name] = Sem(self.es, self.nc, name)
        return self._sems[name]

    def sb(self, name, shape, dt):
        return self.es.enter_context(self.nc.sbuf_tensor(name, shape, dt))

    def build(self):
        nc = self.nc
        self.xt = nc.dram_tensor("xt", [C, NQ], F32, kind="ExternalInput")
        self.wq = nc.dram_tensor("wq", [16, 128, 8192], F32, kind="ExternalInput")
        self.wk = nc.dram_tensor("wk", [8, 128, 8192], F32, kind="ExternalInput")
        self.wv = nc.dram_tensor("wv", [8, 128, 8192], F32, kind="ExternalInput")
        self.up_w = nc.dram_tensor("up", [64, 128, 4096], F32, kind="ExternalInput")
        self.dwn_w = nc.dram_tensor("dwn", [32, 128, 8192], F32, kind="ExternalInput")
        self.fc1_w = nc.dram_tensor("fc1", [16, 128, 4096], F32, kind="ExternalInput")
        self.fc2_w = nc.dram_tensor("fc2", [32, 128, 2048], F32, kind="ExternalInput")
        self.ew_w = nc.dram_tensor("ew", [16, 128, 2048], F32, kind="ExternalInput")
        self.gate_w = nc.dram_tensor("gate", [128, 256], F32, kind="ExternalInput")
        self.bq_t = nc.dram_tensor("bq", [128, 32], F32, kind="ExternalInput")
        self.bk_t = nc.dram_tensor("bk", [128, 16], F32, kind="ExternalInput")
        self.bv_t = nc.dram_tensor("bv", [128, 16], F32, kind="ExternalInput")
        self.gam_t = nc.dram_tensor("gam", [128, 32], F32, kind="ExternalInput")
        self.bet_t = nc.dram_tensor("bet", [128, 32], F32, kind="ExternalInput")
        self.epar_t = nc.dram_tensor("epar", [128, 1], F32, kind="ExternalInput")
        self.out_t = nc.dram_tensor("out", [C, NQ], F32, kind="ExternalOutput")

        scr_kind = "ExternalOutput" if _DBG else "Internal"
        self.kvo_scr = nc.dram_tensor("kvo_scr", [32 * 128, NQ], BF16, kind="Internal")
        self.kva_scr = nc.dram_tensor("kva_scr", [2 * 32 * 128, NQ], BF16,
                                      kind=scr_kind)
        self.up_bf = nc.dram_tensor("up_bf", [64, 128, 4096], BF16, kind="Internal")
        self.dwn_bf = nc.dram_tensor("dwn_bf", [32, 128, 8192], BF16, kind="Internal")
        self.mk_in = nc.dram_tensor("mk_in", [1, 1], F32, kind="Internal")
        self.mk_out = nc.dram_tensor("mk_out", [1, 1], F32, kind="Internal")
        self.q_scr = nc.dram_tensor("q_scr", [NH, 128, NQ], BF16, kind=scr_kind)
        self.x2_scr = nc.dram_tensor("x2_scr", [NCC, 128, NQ], F32, kind=scr_kind)
        if _DBG:
            self.dbg_xln = nc.dram_tensor("dbg_xln", [128, NCC * T], BF16,
                                          kind="ExternalOutput")
            self.dbg_misc = nc.dram_tensor("dbg_misc", [128, 16], F32,
                                           kind="ExternalOutput")

        with contextlib.ExitStack() as es:
            self.es = es
            self.alloc_persistent()
            self.p0_consts()
            if _DBG_PHASE >= 1:
                self.p1_ln1_stats()
            if _DBG_PHASE >= 2:
                self.p2_xln()
            self.ln1_es.close()
            if _DBG_PHASE >= 3:
                self.proj_feat("wk", self.wk, 8, self.bks, 0, norms=self.maxk2,
                               spill_ks=range(0, 16))
            if _DBG_PHASE >= 4:
                self.proj_feat("wv", self.wv, 8, self.bvs, 16, norms=None,
                               spill_ks=range(16, 32))
            if _DBG_PHASE >= 5:
                self.p5c_issue()
                self.proj_q()
                self.p5c_wait()
                self.p5b_bias()
            if _DBG_PHASE >= 6:
                self.p6_attention()
            self.spb_es.close()
            if _DBG_PHASE >= 7:
                self.br = es.enter_context(nc.sbuf_tensor("p8br", [128, NQ], F32))
                self.lgf_es = contextlib.ExitStack()
                self.lgf = self.lgf_es.enter_context(
                    nc.sbuf_tensor("lgf", [8, NQ], F32))
                self.gates = self.lgf_es.enter_context(
                    nc.sbuf_tensor("gates", [128, 256], BF16))
                self.p7_ln2()
            if _DBG_PHASE >= 8:
                self.p8_ffn()
            self.dbg_dumps()
        return nc

    def spill_load(self, g, k):
        b = k % 2
        if k >= 2:
            g.wait_ge(self.dSpS[b].s, 16 * (k // 2))
        if k < 64:
            src = self.up_w[k, :, :]
        else:
            j = k - 64
            src = self.dwn_w[j // 2, :, (j % 2) * 4096:(j % 2) * 4096 + 4096]
        g.dma_start(self.spb[b][:, :], src).then_inc(self.dSpL[b].s, 16)
        self.dSpL[b].n += 16

    def spill_store(self, sy, k):
        b = k % 2
        sy.wait_ge(self.dSpL[b].s, 16 * (k // 2 + 1))
        if k < 64:
            dst = self.up_bf[k, :, :]
        else:
            j = k - 64
            dst = self.dwn_bf[j // 2, :, (j % 2) * 4096:(j % 2) * 4096 + 4096]
        sy.dma_start(dst, self.spb[b][:, :]).then_inc(self.dSpS[b].s, 16)
        self.dSpS[b].n += 16

    def alloc_persistent(self):
        self.ones_bf = self.sb("ones_bf", [128, 128], BF16)
        self.ident_bf = self.sb("ident_bf", [128, 128], BF16)
        self.ident_f = self.sb("ident_f", [128, 128], F32)
        self.tri = self.sb("tri", [128, 128], F32)
        self.bqs = self.sb("bqs", [128, 32], F32)
        self.bks = self.sb("bks", [128, 16], F32)
        self.bvs = self.sb("bvs", [128, 16], F32)
        self.gams = self.sb("gams", [128, 32], F32)
        self.bets = self.sb("bets", [128, 32], F32)
        self.epar = self.sb("epar_s", [128, 1], F32)
        self.eps_t = self.sb("eps_t", [128, 1], F32)
        self.neg_half = self.sb("neg_half", [128, 1], F32)
        self.neg_one = self.sb("neg_one", [128, 1], F32)
        self.scale_t = self.sb("scale_t", [128, 1], F32)
        self.maxq2 = self.sb("maxq2", [128, 1], F32)
        self.maxk2 = self.sb("maxk2", [128, 1], F32)
        self.bias_m = self.sb("bias_m", [128, 1], F32)
        self.maskA = self.sb("maskA", [128, 128], F32)
        self.maskB = self.sb("maskB", [128, 128], F32)
        self.onesf = self.sb("onesf", [1, 128], F32)
        self.dSpL = [self.sem("sp_dLa"), self.sem("sp_dLb")]
        self.dSpS = [self.sem("sp_dSa"), self.sem("sp_dSb")]
        self.mk_sb = self.sb("mk_sb", [1, 1], F32)
        self.reg = self.es.enter_context(self.nc.sbuf_tensor("reg", [128, NCC * NQ], F32))
        self.reg_bf = self.reg.bitcast(BF16)
        self.spb_es = contextlib.ExitStack()
        self.spb = [self.spb_es.enter_context(
            self.nc.sbuf_tensor(f"spb{i}", [128, 4096], BF16)) for i in range(2)]
        self.ln1_es = contextlib.ExitStack()
        self.mu1 = self.ln1_es.enter_context(self.nc.sbuf_tensor("mu1", [128, NQ], F32))
        self.a1 = self.ln1_es.enter_context(self.nc.sbuf_tensor("a1", [128, NQ], F32))

    def p0_consts(self):
        nc = self.nc
        d0 = self.sem("p0_d")
        c0 = self.sem("p0_c")
        with nc.Block() as blk:
            @blk.gpsimd
            def _(g):
                c0.inc(g.memset(self.ident_bf[:, :], 1.0))
                c0.inc(g.memset(self.ident_f[:, :], 1.0))
                c0.inc(g.memset(self.tri[:, :], 0.0))
                g.wait_ge(c0.s, c0.n)
                c0.inc(g.affine_select(self.ident_bf[:, :], self.ident_bf[:, :],
                                       [[1, 128]], ALU.is_equal, 0.0,
                                       channel_multiplier=-1))
                c0.inc(g.affine_select(self.ident_f[:, :], self.ident_f[:, :],
                                       [[1, 128]], ALU.is_equal, 0.0,
                                       channel_multiplier=-1))
                c0.inc(g.affine_select(self.tri[:, :], self.tri[:, :], [[1, 128]],
                                       ALU.is_ge, -30000.0, channel_multiplier=-1))
                g.memset(self.ones_bf[:, :], 1.0)
                g.memset(self.eps_t[:, :], EPS)
                g.memset(self.neg_half[:, :], -0.5)
                g.memset(self.neg_one[:, :], -1.0)
                g.memset(self.scale_t[:, :], SCALE)
                g.memset(self.maxq2[:, :], 0.0)
                g.memset(self.maxk2[:, :], 0.0)
                g.memset(self.onesf[:, :], 1.0)

            @blk.sync
            def _(sy):
                for dst, src in ((self.bqs, self.bq_t), (self.bks, self.bk_t),
                                 (self.bvs, self.bv_t), (self.gams, self.gam_t),
                                 (self.bets, self.bet_t), (self.epar, self.epar_t)):
                    d0.dma(sy.dma_start(dst[:, :], src[:, :]))
                sy.wait_ge(d0.s, d0.n)

            @blk.vector
            def _(v):
                v.wait_ge(d0.s, d0.n)
                v.wait_ge(c0.s, 6)
                v.tensor_scalar(self.maskA[:, :], self.tri[:, :],
                                self.epar[:, 0:1], None, ALU.mult).then_inc(c0.s, 1)
                v.wait_ge(c0.s, 7)
                v.tensor_scalar(self.bias_m[:, :], self.epar[:, 0:1], 30000.0, -30000.0,
                                ALU.mult, ALU.add).then_inc(c0.s, 1)
                v.wait_ge(c0.s, 8)
                v.tensor_scalar(self.maskB[:, :], self.maskA[:, :],
                                self.bias_m[:, 0:1], None, ALU.add).then_inc(c0.s, 1)
                v.wait_ge(c0.s, 9)
                v.tensor_sub(self.maskA[:, :], self.tri[:, :],
                             self.maskA[:, :]).then_inc(c0.s, 1)

    def p1_ln1_stats(self):
        nc = self.nc
        xln = self.reg_bf
        with contextlib.ExitStack() as p:
            xbf = [p.enter_context(nc.sbuf_tensor(f"p1xbf{i}", [128, NQ], BF16)) for i in range(2)]
            sqb = [p.enter_context(nc.sbuf_tensor(f"p1sqb{i}", [128, NQ], BF16)) for i in range(2)]
            xf = [p.enter_context(nc.sbuf_tensor(f"p2xf{i}", [128, NQ], F32)) for i in range(2)]
            t1b = [p.enter_context(nc.sbuf_tensor(f"p2t1{i}", [128, NQ], F32)) for i in range(2)]
            ps_s1 = p.enter_context(nc.psum_tensor("p1s1", [128, NQ], F32))
            ps_s2 = p.enter_context(nc.psum_tensor("p1s2", [128, NQ], F32))
            t_e2 = p.enter_context(nc.sbuf_tensor("p1e2", [128, NQ], F32))
            dL = [self.sem("p1_dLa"), self.sem("p1_dLb")]
            dL2 = [self.sem("p2_dLa"), self.sem("p2_dLb")]
            cSq = self.sem("p1_cSq")
            cMM = self.sem("p1_cMM")
            cR = self.sem("p1_cR")
            cAp = self.sem("p2_cAp")

            with nc.Block() as blk:
                @blk.gpsimd
                def _(g):
                    for cc in range(NCC):
                        if cc >= 2:
                            g.wait_ge(cMM.s, cc - 1)
                        g.dma_start(xbf[cc % 2][:, :],
                                    self.xt[cc * 128:(cc + 1) * 128, :]
                                    ).then_inc(dL[cc % 2].s, 16)
                    g.wait_ge(cMM.s, NCC)

                @blk.sync
                def _(sy):
                    for cc in range(NCC):
                        if cc >= 2:
                            sy.wait_ge(cAp.s, 2 * (cc - 1))
                        sy.dma_start(xf[cc % 2][:, :],
                                     self.xt[cc * 128:(cc + 1) * 128, :]
                                     ).then_inc(dL2[cc % 2].s, 16)
                    sy.wait_ge(cAp.s, 2 * NCC)

                @blk.scalar
                def _(s):
                    for cc in range(NCC):
                        s.wait_ge(dL[cc % 2].s, 16 * (cc // 2 + 1))
                        if cc >= 2:
                            s.wait_ge(cMM.s, cc - 1)
                        s.activation(sqb[cc % 2][:, :], xbf[cc % 2][:, :],
                                     AF.Square).then_inc(cSq.s, 1)
                    s.wait_ge(cR.s, 4)
                    s.activation(t_e2[:, :], t_e2[:, :], AF.Ln,
                                 bias=self.eps_t[:, :]).then_inc(cR.s, 1)
                    s.wait_ge(cR.s, 5)
                    s.activation(self.a1[:, :], t_e2[:, :], AF.Exp,
                                 scale=self.neg_half[:, :]).then_inc(cR.s, 1)

                @blk.tensor
                def _(t):
                    for cc in range(NCC):
                        t.wait_ge(cSq.s, cc + 1)
                        for n in range(2):
                            t.matmul(ps_s1[:, 512 * n:512 * (n + 1)], self.ones_bf[:, :],
                                     xbf[cc % 2][:, 512 * n:512 * (n + 1)],
                                     start=(cc == 0), stop=(cc == NCC - 1))
                            mm = t.matmul(ps_s2[:, 512 * n:512 * (n + 1)], self.ones_bf[:, :],
                                          sqb[cc % 2][:, 512 * n:512 * (n + 1)],
                                          start=(cc == 0), stop=(cc == NCC - 1))
                        mm.then_inc(cMM.s, 1)

                @blk.vector
                def _(v):
                    v.wait_ge(cMM.s, NCC)
                    v.tensor_scalar(self.mu1[:, :], ps_s1[:, :], 1.0 / C,
                                    None, ALU.mult).then_inc(cR.s, 1)
                    v.tensor_scalar(t_e2[:, :], ps_s2[:, :], 1.0 / C,
                                    None, ALU.mult).then_inc(cR.s, 1)
                    v.wait_ge(cR.s, 2)
                    v.tensor_mul(self.a1[:, :], self.mu1[:, :],
                                 self.mu1[:, :]).then_inc(cR.s, 1)
                    v.wait_ge(cR.s, 3)
                    v.tensor_sub(t_e2[:, :], t_e2[:, :], self.a1[:, :]).then_inc(cR.s, 1)
                    v.wait_ge(cR.s, 6)
                    for cc in range(NCC):
                        v.wait_ge(dL2[cc % 2].s, 16 * (cc // 2 + 1))
                        v.tensor_sub(t1b[cc % 2][:, :], xf[cc % 2][:, :],
                                     self.mu1[:, :]).then_inc(cAp.s, 1)
                        v.wait_ge(cAp.s, 2 * cc + 1)
                        v.tensor_mul(xln[:, cc * NQ:(cc + 1) * NQ], t1b[cc % 2][:, :],
                                     self.a1[:, :]).then_inc(cAp.s, 1)

    def p2_xln(self):
        pass

    def proj_feat(self, name, wimg, nblk, bias_img, scr_off, norms, spill_ks=()):
        nc = self.nc
        xln = self.reg_bf
        n_slot = nblk * 2
        spill_ks = list(spill_ks)
        with contextlib.ExitStack() as p:
            wbuf = [p.enter_context(nc.sbuf_tensor(f"{name}wb{i}", [128, 8192], BF16))
                    for i in range(2)]
            kbf = [p.enter_context(nc.sbuf_tensor(f"{name}kb{i}", [128, 1024], BF16))
                   for i in range(2)]
            ksq = p.enter_context(nc.sbuf_tensor(f"{name}sq", [128, 1024], BF16))
            nrm_s = p.enter_context(nc.sbuf_tensor(f"{name}nrm", [128, 1], F32))
            ps = [p.enter_context(nc.psum_tensor(f"{name}ps{i}", [128, 1024], F32))
                  for i in range(2)]
            ps_n = (p.enter_context(nc.psum_tensor(f"{name}pn", [128, 1024], F32))
                    if norms is not None else None)
            dW = [self.sem(name + "_dWa"), self.sem(name + "_dWb")]
            dS = [self.sem(name + "_dSa"), self.sem(name + "_dSb")]
            cPE = self.sem(name + "_cPE")
            cC = self.sem(name + "_cC")
            cSq = self.sem(name + "_cSq")
            cN = self.sem(name + "_cN")
            cFree = cSq if norms is not None else cC

            def main_val(s):
                if norms is None:
                    return s + 1
                return 1 if s == 0 else 2 * s

            def norm_val(s):
                return 2 * s + 3 if s < n_slot - 1 else 2 * n_slot

            with nc.Block() as blk:
                @blk.gpsimd
                def _(g):
                    for b in range(nblk):
                        if b >= 2:
                            g.wait_ge(cFree.s, 2 * b - 2)
                        g.dma_start(wbuf[b % 2][:, :],
                                    wimg[b, :, :]).then_inc(dW[b % 2].s, 16)
                        for k in spill_ks[b * len(spill_ks) // nblk:
                                          (b + 1) * len(spill_ks) // nblk]:
                            self.spill_load(g, k)
                    g.wait_ge(cFree.s, n_slot)

                @blk.tensor
                def _(t):
                    for s in range(n_slot):
                        dch = s
                        b, sub = dch // 2, dch % 2
                        t.wait_ge(dW[b % 2].s, 16 * (b // 2 + 1))
                        if s >= 2:
                            t.wait_ge(cFree.s, s - 1)
                        for cc in range(NCC):
                            lhs = wbuf[b % 2][:, cc * 256 + sub * 128:cc * 256 + sub * 128 + 128]
                            for n in range(2):
                                mm = t.matmul(ps[s % 2][:, 512 * n:512 * (n + 1)], lhs,
                                              xln[:, cc * NQ + 512 * n:
                                                  cc * NQ + 512 * (n + 1)],
                                              start=(cc == 0), stop=(cc == NCC - 1))
                        mm.then_inc(cPE.s, 1)
                        if norms is not None and s >= 1:
                            t.wait_ge(cSq.s, s)
                            if s >= 2:
                                t.wait_ge(cN.s, s - 1)
                            t.matmul(ps_n[:, 0:512], self.ones_bf[:, :], ksq[:, 0:512],
                                     start=True, stop=True)
                            t.matmul(ps_n[:, 512:1024], self.ones_bf[:, :], ksq[:, 512:1024],
                                     start=True, stop=True).then_inc(cPE.s, 1)
                    if norms is not None:
                        t.wait_ge(cSq.s, n_slot)
                        t.wait_ge(cN.s, n_slot - 1)
                        t.matmul(ps_n[:, 0:512], self.ones_bf[:, :], ksq[:, 0:512],
                                 start=True, stop=True)
                        t.matmul(ps_n[:, 512:1024], self.ones_bf[:, :], ksq[:, 512:1024],
                                 start=True, stop=True).then_inc(cPE.s, 1)

                @blk.scalar
                def _(sc):
                    for s in range(n_slot):
                        dch = s
                        sc.wait_ge(cPE.s, main_val(s) if s == 0 or norms is None
                                   else norm_val(s - 1))
                        if s >= 2:
                            sc.wait_ge(dS[s % 2].s, 16 * (s // 2))
                        cC.inc(sc.activation(kbf[s % 2][:, :], ps[s % 2][:, :], AF.Identity,
                                             bias=bias_img[:, dch:dch + 1]))
                        if norms is not None:
                            sc.activation(ksq[:, :], ps[s % 2][:, :], AF.Square,
                                          bias=bias_img[:, dch:dch + 1]).then_inc(cSq.s, 1)

                if norms is not None:
                    cNr = self.sem(name + "_cNr")

                    @blk.vector
                    def _(v):
                        for s in range(n_slot):
                            v.wait_ge(cPE.s, norm_val(s))
                            v.reduce_max(nrm_s[:, :], ps_n[:, :], AX.X).then_inc(cNr.s, 1)
                            v.wait_ge(cNr.s, s + 1)
                            v.tensor_max(norms[:, :], norms[:, :],
                                         nrm_s[:, :]).then_inc(cN.s, 1)
                            if s < n_slot - 1:
                                v.wait_ge(cN.s, s + 1)

                @blk.sync
                def _(sy):
                    for s in range(n_slot):
                        sy.wait_ge(cC.s, s + 1)
                        sy.dma_start(
                            self.kvo_scr[(scr_off + s) * 128:(scr_off + s + 1) * 128, :],
                            kbf[s % 2][:, :]).then_inc(dS[s % 2].s, 16)
                        for k in spill_ks[s * len(spill_ks) // n_slot:
                                          (s + 1) * len(spill_ks) // n_slot]:
                            self.spill_store(sy, k)
                    sy.wait_ge(dS[0].s, 16 * (n_slot // 2))
                    sy.wait_ge(dS[1].s, 16 * (n_slot // 2))

    def p5c_issue(self):
        nc = self.nc
        groups = [[0, 1], [2, 3], [4, 5], [6, 7]]
        cg = self.sem("p5c_c")
        dg = self.sem("p5c_d")
        with nc.Block() as blk:
            @blk.gpsimd
            def _(g):
                g.dma_start(self.mk_in[:, :], self.maxk2[0:1, 0:1]).then_inc(dg.s, 16)
                g.wait_ge(dg.s, 16)
                g.collective_compute(
                    "AllGather", ALU.bypass, replica_groups=groups,
                    ins=[self.kvo_scr.ap().opt()],
                    outs=[self.kva_scr.ap().opt()]).then_inc(cg.s, 1)
                g.collective_compute(
                    "AllReduce", ALU.max, replica_groups=groups,
                    ins=[self.mk_in.ap().opt()],
                    outs=[self.mk_out.ap().opt()]).then_inc(cg.s, 1)

    def p5c_wait(self):
        nc = self.nc
        cg = self.sem("p5c_c")
        dg = self.sem("p5c_d")
        with contextlib.ExitStack() as pp:
            ps_mk = pp.enter_context(nc.psum_tensor("ps_mk", [128, 1], F32))
            with nc.Block() as blk:
                @blk.gpsimd
                def _(g):
                    g.wait_ge(cg.s, 2)
                    g.dma_start(self.mk_sb[:, :], self.mk_out[:, :]).then_inc(dg.s, 16)
                    g.wait_ge(dg.s, 32)

                @blk.tensor
                def _(t):
                    t.wait_ge(dg.s, 32)
                    t.matmul(ps_mk[:, :], self.onesf[:, :], self.mk_sb[:, :],
                             start=True, stop=True).then_inc(cg.s, 1)

                @blk.scalar
                def _(s):
                    s.wait_ge(cg.s, 3)
                    s.copy(self.maxk2[:, :], ps_mk[:, :])

    def proj_q(self):
        nc = self.nc
        xln = self.reg_bf
        n_slot = 64
        with contextlib.ExitStack() as p:
            wbuf = [p.enter_context(nc.sbuf_tensor(f"q_wb{i}", [128, 8192], BF16))
                    for i in range(2)]
            kbf = [p.enter_context(nc.sbuf_tensor(f"q_kb{i}", [128, 512], BF16))
                   for i in range(2)]
            ksq = p.enter_context(nc.sbuf_tensor("q_sq", [128, 512], BF16))
            nrm_s = p.enter_context(nc.sbuf_tensor("q_nrm", [128, 1], F32))
            ps = [p.enter_context(nc.psum_tensor(f"q_ps{i}", [128, 512], F32))
                  for i in range(2)]
            ps_n = p.enter_context(nc.psum_tensor("q_pn", [128, 512], F32))
            dW = [self.sem("q_dWa"), self.sem("q_dWb")]
            dS = [self.sem("q_dSa"), self.sem("q_dSb")]
            cPE = self.sem("q_cPE")
            cC = self.sem("q_cC")
            cSq = self.sem("q_cSq")
            cN = self.sem("q_cN")
            with nc.Block() as blk:
                @blk.gpsimd
                def _(g):
                    qs = list(range(32, 52))
                    for b in range(16):
                        if b >= 2:
                            g.wait_ge(cSq.s, 4 * b - 4)
                        g.dma_start(wbuf[b % 2][:, :],
                                    self.wq[b, :, :]).then_inc(dW[b % 2].s, 16)
                        for k in qs[b * 20 // 16:(b + 1) * 20 // 16]:
                            self.spill_load(g, k)
                    g.wait_ge(cSq.s, n_slot)

                @blk.tensor
                def _(t):
                    for s in range(n_slot):
                        dch, half = s // 2, s % 2
                        b, sub = dch // 2, dch % 2
                        t.wait_ge(dW[b % 2].s, 16 * (b // 2 + 1))
                        if s >= 2:
                            t.wait_ge(cSq.s, s - 1)
                        for cc in range(NCC):
                            lhs = wbuf[b % 2][:, cc * 256 + sub * 128:cc * 256 + sub * 128 + 128]
                            mm = t.matmul(ps[s % 2][:, :], lhs,
                                          xln[:, cc * NQ + half * 512:cc * NQ + half * 512 + 512],
                                          start=(cc == 0), stop=(cc == NCC - 1))
                        mm.then_inc(cPE.s, 1)
                        if s >= 1:
                            t.wait_ge(cSq.s, s)
                            if s >= 2:
                                t.wait_ge(cN.s, s - 1)
                            t.matmul(ps_n[:, :], self.ones_bf[:, :], ksq[:, :],
                                     start=True, stop=True).then_inc(cPE.s, 1)
                    t.wait_ge(cSq.s, n_slot)
                    t.wait_ge(cN.s, n_slot - 1)
                    t.matmul(ps_n[:, :], self.ones_bf[:, :], ksq[:, :],
                             start=True, stop=True).then_inc(cPE.s, 1)

                @blk.scalar
                def _(sc):
                    for s in range(n_slot):
                        dch = s // 2
                        val = 2 * (s + 1) - 1
                        sc.wait_ge(cPE.s, val)
                        if s >= 2:
                            sc.wait_ge(dS[s % 2].s, 16 * (s // 2))
                        cC.inc(sc.activation(kbf[s % 2][:, :], ps[s % 2][:, :], AF.Identity,
                                             bias=self.bqs[:, dch:dch + 1]))
                        sc.activation(ksq[:, :], ps[s % 2][:, :], AF.Square,
                                      bias=self.bqs[:, dch:dch + 1]).then_inc(cSq.s, 1)

                cNr = self.sem("q_cNr")

                @blk.vector
                def _(v):
                    for s in range(n_slot):
                        val = 2 * s + 3 if s < n_slot - 1 else 2 * n_slot
                        v.wait_ge(cPE.s, val)
                        v.reduce_max(nrm_s[:, :], ps_n[:, :], AX.X).then_inc(cNr.s, 1)
                        v.wait_ge(cNr.s, s + 1)
                        v.tensor_max(self.maxq2[:, :], self.maxq2[:, :],
                                     nrm_s[:, :]).then_inc(cN.s, 1)
                        if s < n_slot - 1:
                            v.wait_ge(cN.s, s + 1)

                @blk.sync
                def _(sy):
                    qs = list(range(32, 52))
                    for s in range(n_slot):
                        dch, half = s // 2, s % 2
                        sy.wait_ge(cC.s, s + 1)
                        sy.dma_start(self.q_scr[dch, :, half * 512:(half + 1) * 512],
                                     kbf[s % 2][:, :]).then_inc(dS[s % 2].s, 16)
                        for k in qs[s * 20 // n_slot:(s + 1) * 20 // n_slot]:
                            self.spill_store(sy, k)
                    sy.wait_ge(dS[0].s, 16 * (n_slot // 2))
                    sy.wait_ge(dS[1].s, 16 * (n_slot // 2))

    def p5b_bias(self):
        nc = self.nc
        cX = self.sem("p5b_c")
        with nc.Block() as blk:
            @blk.scalar
            def _(s):
                s.activation(self.maxq2[:, :], self.maxq2[:, :], AF.Sqrt)
                s.activation(self.maxk2[:, :], self.maxk2[:, :], AF.Sqrt).then_inc(cX.s, 1)

            @blk.vector
            def _(v):
                v.wait_ge(cX.s, 1)
                v.tensor_mul(self.bias_m[:, :], self.maxq2[:, :],
                             self.maxk2[:, :]).then_inc(cX.s, 1)
                v.wait_ge(cX.s, 2)
                v.tensor_scalar(self.bias_m[:, :], self.bias_m[:, :],
                                -SCALE * 1.02, None, ALU.mult).then_inc(cX.s, 1)

    def p6_attention(self):
        nc = self.nc
        x2 = self.reg

        heads = []
        v_idx = 0
        d_idx = 0
        for J in (0, 1):
            for kvh in range(NKVH):
                pair = J * NKVH + kvh
                for g in (0, 1):
                    hi = len(heads)
                    h = kvh + NKVH * g
                    visits = []
                    for side in (0, 1):
                        for w in range(4 * J + 4):
                            s0 = max(0, w - 4 * J)
                            masked = (w >= 4 * J)
                            visits.append(dict(
                                v=v_idx, side=side, w=w, s0=s0, masked=masked,
                                d=(d_idx if masked else None),
                                tile=side * (4 * J + 4) + w))
                            d_idx += 1 if masked else 0
                            v_idx += 1
                    heads.append(dict(hi=hi, J=J, kvh=kvh, g=g, h=h, pair=pair,
                                      visits=visits))
        n_heads = len(heads)
        n_pairs = 2 * NKVH
        exp_done = {v: v + 1 for v in range(v_idx)}

        with contextlib.ExitStack() as p:
            NKB, NQB, NPT = 4, 4, 4
            kbuf = [p.enter_context(nc.sbuf_tensor(f"p6k{i}", [128, 2048], BF16))
                    for i in range(NKB)]
            vbuf = [p.enter_context(nc.sbuf_tensor(f"p6v{i}", [128, 2048], BF16))
                    for i in range(NKB)]
            qbuf = [p.enter_context(nc.sbuf_tensor(f"p6q{i}", [128, 512], BF16))
                    for i in range(NQB)]
            xbuf = [p.enter_context(nc.sbuf_tensor(f"p6x{i}", [128, 512], F32))
                    for i in range(NQB)]
            pt = [p.enter_context(nc.sbuf_tensor(f"p6p{i}", [128, 512], BF16))
                  for i in range(NPT)]
            zl = [p.enter_context(nc.sbuf_tensor(f"p6zl{i}", [128, 512], F32))
                  for i in range(2)]
            tev = [p.enter_context(nc.sbuf_tensor(f"p6te{i}", [128, 512], F32))
                   for i in range(2)]
            psS = [p.enter_context(nc.psum_tensor(f"p6s{i}", [128, 512], F32))
                   for i in range(NPT)]
            psAV = [p.enter_context(nc.psum_tensor(f"p6av{i}", [128, 512], F32))
                    for i in range(2)]
            psZ = [p.enter_context(nc.psum_tensor(f"p6z{i}", [128, 512], F32))
                   for i in range(2)]
            dk = [self.sem(f"p6_dk{i}") for i in range(NKB)]
            dq = [self.sem(f"p6_dq{i}") for i in range(NQB)]
            cS = self.sem("p6_cS")
            cM = self.sem("p6_cM")
            cE = self.sem("p6_cE")
            cAV = self.sem("p6_cAV")
            cL = self.sem("p6_cL")
            cEv = self.sem("p6_cEv")
            cEvm = self.sem("p6_cEvm")

            kv_ready = {}
            q_ready = {}
            npar = [0] * NKB
            for J in (0, 1):
                for kvh in range(NKVH):
                    pair = J * NKVH + kvh
                    npar[pair % NKB] += 16 * (1 + 2 * (4 * J + 4))
                    kv_ready[pair] = npar[pair % NKB]
            hpar = [0] * NQB
            for hi in range(n_heads):
                hpar[hi % NQB] += 32
                q_ready[hi] = hpar[hi % NQB]

            with nc.Block() as blk:
                kv4 = self.kva_scr.rearrange("(r c p) k -> p r c k", r=2, p=128)
                p6ks = list(range(52, 64))

                @blk.sync
                def _(sy):
                    for J in (0, 1):
                        segw = 512 * (J + 1)
                        nt = 4 * J + 4
                        for kvh in range(NKVH):
                            pair = J * NKVH + kvh
                            if pair >= NKB:
                                sy.wait_ge(cEv.s, 2 * (pair - NKB + 1))
                            bi = pair % NKB
                            dst = kbuf[bi][:, 0:2 * segw].rearrange(
                                "p (s k) -> p s k", s=2)
                            src = kv4[:, :, kvh, 0:segw]
                            sy.dma_start(dst, src).then_inc(dk[bi].s, 16)
                            for side in (0, 1):
                                for w in range(nt):
                                    ti = side * nt + w
                                    sy.dma_start(
                                        vbuf[bi][:, ti * 128:(ti + 1) * 128],
                                        kv4[:, side, 16 + kvh,
                                            w * 128:w * 128 + 128],
                                        transpose=True).then_inc(dk[bi].s, 16)
                            for k in p6ks[pair * 12 // n_pairs:
                                          (pair + 1) * 12 // n_pairs]:
                                self.spill_store(sy, k)

                @blk.gpsimd
                def _(g):
                    for hd in heads:
                        hi, h, J = hd["hi"], hd["h"], hd["J"]
                        if hi >= NQB:
                            g.wait_ge(cEv.s, hi - NQB + 1)
                        g.dma_start(qbuf[hi % NQB][:, :],
                                    self.q_scr[h, :, J * 512:(J + 1) * 512]
                                    ).then_inc(dq[hi % NQB].s, 16)
                        g.dma_start(xbuf[hi % NQB][:, :],
                                    self.xt[h * 128:(h + 1) * 128,
                                            J * 512:(J + 1) * 512]
                                    ).then_inc(dq[hi % NQB].s, 16)
                        for k in p6ks[hi * 12 // n_heads:(hi + 1) * 12 // n_heads]:
                            self.spill_load(g, k)

                @blk.tensor
                def _(t):
                    def av_ones(hd, vis, first, last):
                        v, s0 = vis["v"], vis["s0"]
                        bi = hd["pair"] % NKB
                        hb = hd["hi"] % 2
                        cols = slice(s0 * 128, 512)
                        t.wait_ge(cE.s, exp_done[v])
                        if first and hd["hi"] >= 2:
                            t.wait_ge(cEv.s, hd["hi"] - 1)
                        t.matmul(psAV[hb][:, cols],
                                 vbuf[bi][:, vis["tile"] * 128:(vis["tile"] + 1) * 128],
                                 pt[v % NPT][:, cols], start=first, stop=last)
                        t.matmul(psZ[hb][:, cols], self.ones_bf[:, :],
                                 pt[v % NPT][:, cols], start=first,
                                 stop=last).then_inc(cAV.s, 1)

                    for hd in heads:
                        hi = hd["hi"]
                        bi = hd["pair"] % NKB
                        t.wait_ge(dk[hd["pair"] % NKB].s, kv_ready[hd["pair"]])
                        t.wait_ge(dq[hi % NQB].s, q_ready[hi])
                        vl = hd["visits"]
                        for i, vis in enumerate(vl):
                            v, s0 = vis["v"], vis["s0"]
                            cols = slice(s0 * 128, 512)
                            if v >= NPT:
                                t.wait_ge(cE.s, exp_done[v - NPT])
                            t.matmul(psS[v % NPT][:, cols],
                                     kbuf[bi][:, vis["tile"] * 128:(vis["tile"] + 1) * 128],
                                     qbuf[hi % NQB][:, cols],
                                     start=True, stop=True).then_inc(cS.s, 1)
                            if i >= 2:
                                av_ones(hd, vl[i - 2], i == 2, False)
                        if len(vl) >= 2:
                            av_ones(hd, vl[-2], len(vl) == 2, False)
                        av_ones(hd, vl[-1], len(vl) == 1, True)

                @blk.scalar
                def _(sc):
                    for hd in heads:
                        hi = hd["hi"]
                        for vis in hd["visits"]:
                            v, s0 = vis["v"], vis["s0"]
                            sc.wait_ge(cS.s, v + 1)
                            if vis["masked"]:
                                sc.wait_ge(cM.s, vis["d"] + 1)
                            if v >= NPT:
                                sc.wait_ge(cAV.s, v - NPT + 1)
                            sc.activation(pt[v % NPT][:, s0 * 128:512],
                                          psS[v % NPT][:, s0 * 128:512],
                                          AF.Exp, bias=self.bias_m[:, :],
                                          scale=self.scale_t[:, :]).then_inc(cE.s, 1)
                        hb = hi % 2
                        sc.wait_ge(cAV.s, hd["visits"][-1]["v"] + 1)
                        sc.activation(zl[hb][:, :], psZ[hb][:, :],
                                      AF.Ln).then_inc(cL.s, 1)
                        sc.wait_ge(cL.s, 2 * hi + 1)
                        sc.activation(zl[hb][:, :], zl[hb][:, :], AF.Exp,
                                      scale=self.neg_one[:, :]).then_inc(cL.s, 1)

                @blk.vector
                def _(v_):
                    for hd in heads:
                        hi, h, J = hd["hi"], hd["h"], hd["J"]
                        for vis in hd["visits"]:
                            if not vis["masked"]:
                                continue
                            v, s0 = vis["v"], vis["s0"]
                            msk = self.maskA if vis["side"] == 0 else self.maskB
                            v_.wait_ge(cS.s, v + 1)
                            v_.tensor_add(psS[v % NPT][:, s0 * 128:(s0 + 1) * 128],
                                          psS[v % NPT][:, s0 * 128:(s0 + 1) * 128],
                                          msk[:, :]).then_inc(cM.s, 1)
                        hb = hi % 2
                        v_.wait_ge(cL.s, 2 * (hi + 1))
                        v_.wait_ge(dq[hi % NQB].s, q_ready[hi])
                        v_.tensor_mul(tev[hb][:, :], psAV[hb][:, :],
                                      zl[hb][:, :]).then_inc(cEvm.s, 1)
                        v_.wait_ge(cEvm.s, hi + 1)
                        v_.tensor_add(x2[:, h * 1024 + J * 512:h * 1024 + J * 512 + 512],
                                      tev[hb][:, :], xbuf[hi % NQB][:, :]).then_inc(cEv.s, 1)

    def p7_ln2(self):
        nc = self.nc
        x2 = self.reg
        with contextlib.ExitStack() as p:
            x2bf = [p.enter_context(nc.sbuf_tensor(f"p7b{i}", [128, NQ], BF16))
                    for i in range(2)]
            sqb = [p.enter_context(nc.sbuf_tensor(f"p7s{i}", [128, NQ], BF16))
                   for i in range(2)]
            t1b = [p.enter_context(nc.sbuf_tensor(f"p7t{i}", [128, NQ], F32))
                   for i in range(2)]
            ps_s1 = p.enter_context(nc.psum_tensor("p7ps1", [128, NQ], F32))
            ps_s2 = p.enter_context(nc.psum_tensor("p7ps2", [128, NQ], F32))
            self.mu2 = p.enter_context(nc.sbuf_tensor("p7mu", [128, NQ], F32))
            self.a2 = p.enter_context(nc.sbuf_tensor("p7a", [128, NQ], F32))
            te2 = p.enter_context(nc.sbuf_tensor("p7e2", [128, NQ], F32))
            cCp = self.sem("p7_cCp")
            cSq = self.sem("p7_cSq")
            cMM = self.sem("p7_cMM")
            dS = self.sem("p7_dS")
            cR = self.sem("p7_cR")
            cAp = self.sem("p7_cAp")
            dg7 = self.sem("p7_dg")
            with nc.Block() as blk:
                @blk.gpsimd
                def _(g):
                    dg7.dma(g.dma_start(self.gates[:, :], self.gate_w[:, :]))
                    g.wait_ge(dg7.s, dg7.n)

                @blk.vector
                def _(v):
                    for cc in range(NCC):
                        if cc >= 2:
                            v.wait_ge(cMM.s, cc - 1)
                        v.tensor_copy(x2bf[cc % 2][:, :],
                                      x2[:, cc * NQ:(cc + 1) * NQ]).then_inc(cCp.s, 1)

                @blk.scalar
                def _(s):
                    for cc in range(NCC):
                        if cc >= 2:
                            s.wait_ge(cMM.s, cc - 1)
                        s.activation(sqb[cc % 2][:, :], x2[:, cc * NQ:(cc + 1) * NQ],
                                     AF.Square).then_inc(cSq.s, 1)

                @blk.tensor
                def _(t):
                    for cc in range(NCC):
                        t.wait_ge(cCp.s, cc + 1)
                        t.wait_ge(cSq.s, cc + 1)
                        for n in range(2):
                            t.matmul(ps_s1[:, 512 * n:512 * (n + 1)], self.ones_bf[:, :],
                                     x2bf[cc % 2][:, 512 * n:512 * (n + 1)],
                                     start=(cc == 0), stop=(cc == NCC - 1))
                            mm = t.matmul(ps_s2[:, 512 * n:512 * (n + 1)], self.ones_bf[:, :],
                                          sqb[cc % 2][:, 512 * n:512 * (n + 1)],
                                          start=(cc == 0), stop=(cc == NCC - 1))
                        mm.then_inc(cMM.s, 1)

                @blk.sync
                def _(sy):
                    for cc in range(NCC):
                        sy.dma_start(self.x2_scr[cc, :, :],
                                     x2[:, cc * NQ:(cc + 1) * NQ]).then_inc(dS.s, 16)
                    sy.wait_ge(dS.s, 16 * NCC)

            with nc.Block() as blk:
                @blk.vector
                def _(v):
                    v.tensor_scalar(self.mu2[:, :], ps_s1[:, :], 1.0 / C,
                                    None, ALU.mult).then_inc(cR.s, 1)
                    v.tensor_scalar(te2[:, :], ps_s2[:, :], 1.0 / C,
                                    None, ALU.mult).then_inc(cR.s, 1)
                    v.wait_ge(cR.s, 2)
                    v.tensor_mul(self.a2[:, :], self.mu2[:, :],
                                 self.mu2[:, :]).then_inc(cR.s, 1)
                    v.wait_ge(cR.s, 3)
                    v.tensor_sub(te2[:, :], te2[:, :], self.a2[:, :]).then_inc(cR.s, 1)

                @blk.scalar
                def _(s):
                    s.wait_ge(cR.s, 4)
                    s.activation(te2[:, :], te2[:, :], AF.Ln,
                                 bias=self.eps_t[:, :]).then_inc(cR.s, 1)
                    s.wait_ge(cR.s, 5)
                    s.activation(self.a2[:, :], te2[:, :], AF.Exp,
                                 scale=self.neg_half[:, :])

            hv = self.reg_bf
            ps_g = p.enter_context(nc.psum_tensor("p7pg", [8, NQ], F32))
            cGt = self.sem("p7_gt")
            with nc.Block() as blk:
                @blk.vector
                def _(v):
                    for cc in range(NCC):
                        v.tensor_sub(t1b[cc % 2][:, :], x2[:, cc * NQ:(cc + 1) * NQ],
                                     self.mu2[:, :]).then_inc(cAp.s, 1)
                        v.wait_ge(cAp.s, 2 * cc + 1)
                        v.tensor_mul(hv[:, cc * 2048:cc * 2048 + 1024],
                                     t1b[cc % 2][:, :], self.a2[:, :]).then_inc(cAp.s, 1)

                @blk.tensor
                def _(t):
                    for cc in range(NCC):
                        t.wait_ge(cAp.s, 2 * (cc + 1))
                        for n in range(2):
                            mm = t.matmul(ps_g[:, 512 * n:512 * (n + 1)],
                                          self.gates[:, cc * 8:(cc + 1) * 8],
                                          hv[:, cc * 2048 + 512 * n:
                                             cc * 2048 + 512 * (n + 1)],
                                          start=(cc == 0), stop=(cc == NCC - 1))
                    mm.then_inc(cGt.s, 1)

                @blk.scalar
                def _(s):
                    s.wait_ge(cGt.s, 1)
                    s.copy(self.lgf[:, :], ps_g[:, :])

    def p8_ffn(self):
        nc = self.nc
        hv = self.reg_bf
        comb = self.reg

        def hslice(cc, th):
            return hv[:, cc * 2048 + th * 512: cc * 2048 + th * 512 + 512]

        def cslice(hc):
            return comb[:, hc * NQ + 512: hc * NQ + 1024]

        br = self.br
        self.p8a_routing(br)
        self.lgf_es.close()
        for th in (0, 1):
            self.p8b_updown(th, hslice, cslice, None)
            self.p8c_routed(th, hslice, cslice, br)
            self.p8d_ln3_out(th, cslice)

    def p8a_routing(self, br):
        nc = self.nc
        with contextlib.ExitStack() as p:
            lgf = self.lgf
            lgT = p.enter_context(nc.sbuf_tensor("p8lgT", [128, 64], F32))
            scr1 = p.enter_context(nc.sbuf_tensor("p8r1", [128, 8], F32))
            scr2 = p.enter_context(nc.sbuf_tensor("p8r2", [128, 8], F32))
            r_all = p.enter_context(nc.sbuf_tensor("p8ra", [128, 8], F32))
            rT2 = p.enter_context(nc.sbuf_tensor("p8rT", [1, NQ], F32))
            onesf1 = p.enter_context(nc.sbuf_tensor("p8o1", [1, 128], F32))
            ps_t = [p.enter_context(nc.psum_tensor(f"p8pt{i}", [128, 8], F32))
                    for i in range(2)]
            ps_r = [p.enter_context(nc.psum_tensor(f"p8pr{i}", [1, 128], F32))
                    for i in range(2)]
            ps_b = p.enter_context(nc.psum_tensor("p8pb", [128, NQ], F32))
            cv = self.sem("p8a_cv")
            ct = self.sem("p8a_ct")
            ce = self.sem("p8a_ce")
            with nc.Block() as blk:
                @blk.tensor
                def _(t):
                    for u in range(8):
                        if u >= 2:
                            t.wait_ge(ce.s, u - 1)
                        t.transpose(ps_t[u % 2][:, :], lgf[:, u * 128:(u + 1) * 128],
                                    self.ident_f[0:8, 0:8]).then_inc(ct.s, 1)

                @blk.scalar
                def _(s):
                    for u in range(8):
                        s.wait_ge(ct.s, u + 1)
                        s.copy(lgT[:, u * 8:(u + 1) * 8],
                               ps_t[u % 2][:, :]).then_inc(ce.s, 1)

                @blk.vector
                def _(v):
                    n = 0
                    for u in range(8):
                        lg_u = lgT[:, u * 8:(u + 1) * 8]
                        v.wait_ge(ce.s, u + 1)
                        if n:
                            v.wait_ge(cv.s, n)
                        v.reduce_max(scr1[:, 0:1], lg_u, AX.X).then_inc(cv.s, 1)
                        n += 1
                        v.wait_ge(cv.s, n)
                        v.tensor_scalar(scr2[:, :], lg_u, scr1[:, 0:1], None,
                                        ALU.is_equal).then_inc(cv.s, 1)
                        n += 1
                        v.wait_ge(cv.s, n)
                        v.scalar_tensor_tensor(scr2[:, :], scr2[:, :], -1e30, lg_u,
                                               ALU.mult, ALU.add).then_inc(cv.s, 1)
                        n += 1
                        v.wait_ge(cv.s, n)
                        v.reduce_max(scr1[:, 1:2], scr2[:, :], AX.X).then_inc(cv.s, 1)
                        n += 1
                        v.wait_ge(cv.s, n)
                        v.tensor_add(r_all[:, u:u + 1], scr1[:, 0:1],
                                     scr1[:, 1:2]).then_inc(cv.s, 1)
                        n += 1

            cb = self.sem("p8a_cb")
            ct2 = self.sem("p8a_ct2")
            ce2 = self.sem("p8a_ce2")
            with nc.Block() as blk:
                @blk.gpsimd
                def _(g):
                    g.memset(onesf1[:, :], 1.0).then_inc(cb.s, 1)

                @blk.tensor
                def _(t):
                    for u in range(8):
                        if u >= 2:
                            t.wait_ge(ce2.s, u - 1)
                        t.transpose(ps_r[u % 2][:, :], r_all[:, u:u + 1],
                                    self.ident_f[:, :]).then_inc(ct2.s, 1)

                @blk.scalar
                def _(s):
                    for u in range(8):
                        s.wait_ge(ct2.s, u + 1)
                        s.copy(rT2[0:1, u * 128:(u + 1) * 128],
                               ps_r[u % 2][:, :]).then_inc(ce2.s, 1)

            with nc.Block() as blk:
                @blk.tensor
                def _(t):
                    for n in range(2):
                        mm = t.matmul(ps_b[:, 512 * n:512 * (n + 1)], onesf1[:, :],
                                      rT2[0:1, 512 * n:512 * (n + 1)],
                                      start=True, stop=True)
                    mm.then_inc(cb.s, 2)

                @blk.vector
                def _(v):
                    v.wait_ge(cb.s, 3)
                    v.tensor_copy(br[:, :], ps_b[:, :])

    def p8b_updown(self, th, hslice, cslice, _unused):
        nc = self.nc
        with contextlib.ExitStack() as p:
            wub = [p.enter_context(nc.sbuf_tensor(f"p8wu{th}{i}", [128, 4096], BF16))
                   for i in range(2)]
            NWD = 3
            wdb = [p.enter_context(nc.sbuf_tensor(f"p8wd{th}{i}", [128, 2048], BF16))
                   for i in range(NWD)]
            wdf = [p.enter_context(nc.sbuf_tensor(f"p8wf{th}{i}", [128, 2048], F32))
                   for i in range(NWD)]
            ubuf = p.enter_context(nc.sbuf_tensor(f"p8u{th}", [128, 16 * 512], BF16))
            uscr = [p.enter_context(nc.sbuf_tensor(f"p8us{th}{i}", [128, 512], F32))
                    for i in range(2)]
            psU = [p.enter_context(nc.psum_tensor(f"p8pu{th}{i}", [128, 512], F32))
                   for i in range(2)]
            psD = [p.enter_context(nc.psum_tensor(f"p8pd{th}{i}", [128, 512], F32))
                   for i in range(2)]
            dU = [self.sem("p8b_dUa"), self.sem("p8b_dUb")]
            cPE = self.sem("p8b_cPE")
            cRl = self.sem("p8b_cRl")
            cSq = self.sem("p8b_cSq")
            dD = [self.sem(f"p8b_dD{i}") for i in range(3)]
            cPD = self.sem("p8b_cPD")
            cEv = self.sem("p8b_cEv")
            cDc = self.sem("p8b_cDc")
            bU = [dU[0].n, dU[1].n]
            bD = [d.n for d in dD]
            bPE, bRl, bSq, bPD, bEv = cPE.n, cRl.n, cSq.n, cPD.n, cEv.n
            bDc = cDc.n
            with nc.Block() as blk:
                @blk.gpsimd
                def _(g):
                    for ib in range(4):
                        for j in range(16):
                            gj = 16 * ib + j
                            if gj >= 2:
                                g.wait_ge(cPE.s, bPE + gj - 1)
                            g.dma_start(wub[gj % 2][:, :],
                                        self.up_bf[gj, :, :]).then_inc(dU[gj % 2].s, 16)
                    for z in range(2):
                        dU[z].n += 16 * 32

                @blk.sync
                def _(sy):
                    for ib in range(4):
                        for hh in range(32):
                            ghh = 32 * ib + hh
                            if ghh >= NWD:
                                sy.wait_ge(cDc.s, bDc + ghh - NWD + 1)
                            sy.dma_start(wdf[ghh % NWD][:, :],
                                         self.dwn_w[hh, :, ib * 2048:(ib + 1) * 2048]
                                         ).then_inc(dD[ghh % NWD].s, 16)
                    for z in range(3):
                        dD[z].n += 16 * len([i for i in range(128) if i % 3 == z])

                @blk.tensor
                def _(t):
                    for ib in range(4):
                        for ic in range(16):
                            gic = 16 * ib + ic
                            t.wait_ge(dU[gic % 2].s, bU[gic % 2] + 16 * (gic // 2 + 1))
                            if gic >= 2:
                                t.wait_ge(cRl.s, bRl + gic - 1)
                            wsl = wub[gic % 2]
                            for cc in range(NCC):
                                mm = t.matmul(psU[gic % 2][:, :],
                                              wsl[:, cc * 128:(cc + 1) * 128],
                                              hslice(cc, th),
                                              start=(cc == 0), stop=(cc == NCC - 1))
                            mm.then_inc(cPE.s, 1)
                        for hh in range(32):
                            ghh = 32 * ib + hh
                            t.wait_ge(cDc.s, bDc + ghh + 1)
                            t.wait_ge(cSq.s, bSq + 16 * (ib + 1))
                            if ghh >= 2:
                                t.wait_ge(cEv.s, bEv + ghh - 1)
                            wsl = wdb[ghh % NWD]
                            for ic in range(16):
                                mm = t.matmul(psD[ghh % 2][:, :],
                                              wsl[:, ic * 128:(ic + 1) * 128],
                                              ubuf[:, ic * 512:(ic + 1) * 512],
                                              start=(ic == 0), stop=(ic == 15))
                            mm.then_inc(cPD.s, 1)
                    cPE.n += 64
                    cPD.n += 128

                @blk.vector
                def _(v):
                    for ib in range(4):
                        for ic in range(16):
                            gic = 16 * ib + ic
                            v.wait_ge(cPE.s, bPE + gic + 1)
                            if gic >= 2:
                                v.wait_ge(cSq.s, bSq + gic - 1)
                            v.tensor_scalar(uscr[gic % 2][:, :], psU[gic % 2][:, :],
                                            0.0, None, ALU.max).then_inc(cRl.s, 1)
                        def cast_dn(ghh2):
                            nth = len([i for i in range(ghh2 + 1) if i % 3 == ghh2 % 3])
                            v.wait_ge(dD[ghh2 % 3].s, bD[ghh2 % 3] + 16 * nth)
                            if ghh2 >= NWD:
                                v.wait_ge(cPD.s, bPD + ghh2 - NWD + 1)
                            v.tensor_copy(wdb[ghh2 % NWD][:, :],
                                          wdf[ghh2 % NWD][:, :]).then_inc(cDc.s, 1)

                        for hh in range(32):
                            ghh = 32 * ib + hh
                            if ghh == 0:
                                for kk in range(NWD):
                                    cast_dn(kk)
                            if ghh + NWD < 128:
                                cast_dn(ghh + NWD)
                            v.wait_ge(cPD.s, bPD + ghh + 1)
                            if ib == 0:
                                v.tensor_copy(cslice(hh), psD[ghh % 2][:, :]).then_inc(cEv.s, 1)
                            else:
                                v.tensor_add(cslice(hh), cslice(hh),
                                             psD[ghh % 2][:, :]).then_inc(cEv.s, 1)
                    cRl.n += 64
                    cEv.n += 128
                    cDc.n += 128

                @blk.scalar
                def _(s):
                    for ib in range(4):
                        for ic in range(16):
                            gic = 16 * ib + ic
                            s.wait_ge(cRl.s, bRl + gic + 1)
                            if ib > 0:
                                s.wait_ge(cPD.s, bPD + 32 * ib)
                            s.activation(ubuf[:, ic * 512:(ic + 1) * 512],
                                         uscr[gic % 2][:, :], AF.Square).then_inc(cSq.s, 1)
                    cSq.n += 64

    def p8c_routed(self, th, hslice, cslice, br):
        nc = self.nc
        with contextlib.ExitStack() as p:
            NFC = 3
            wb = [p.enter_context(nc.sbuf_tensor(f"p8cw{th}{i}", [128, 4096], BF16))
                  for i in range(NFC)]
            xlat = p.enter_context(nc.sbuf_tensor(f"p8xl{th}", [128, 16 * 512], BF16))
            moe = p.enter_context(nc.sbuf_tensor(f"p8mo{th}", [128, 16 * 512], BF16))
            psF = [p.enter_context(nc.psum_tensor(f"p8pf{th}{i}", [128, 512], F32))
                   for i in range(NFC)]
            dW = [self.sem(f"p8c_dW{i}") for i in range(NFC)]
            cPE = self.sem("p8c_cPE")
            cEv = self.sem("p8c_cEv")

            def stage(nin, nk, wsrc, wwidth, rhs_fn, evac_fn):
                bW = [d.n for d in dW]
                bPE, bEv = cPE.n, cEv.n

                def nth(j):
                    return len([i for i in range(j + 1) if i % NFC == j % NFC])

                with nc.Block() as blk:
                    @blk.gpsimd
                    def _(g):
                        for j in range(nin):
                            if j >= NFC:
                                g.wait_ge(cEv.s, bEv + j - NFC + 1)
                            g.dma_start(wb[j % NFC][:, 0:wwidth],
                                        wsrc[j, :, :]).then_inc(dW[j % NFC].s, 16)
                        for z in range(NFC):
                            dW[z].n += 16 * len([i for i in range(nin) if i % NFC == z])

                    @blk.tensor
                    def _(t):
                        for j in range(nin):
                            t.wait_ge(dW[j % NFC].s, bW[j % NFC] + 16 * nth(j))
                            if j >= NFC:
                                t.wait_ge(cEv.s, bEv + j - NFC + 1)
                            for k in range(nk):
                                mm = t.matmul(psF[j % NFC][:, :],
                                              wb[j % NFC][:, k * 128:(k + 1) * 128],
                                              rhs_fn(k),
                                              start=(k == 0), stop=(k == nk - 1))
                            mm.then_inc(cPE.s, 1)
                        cPE.n += nin

                    evac_fn(blk, nin, bPE)

            def evac_fc1(blk, nin, bPE):
                @blk.scalar
                def _(s):
                    for j in range(nin):
                        s.wait_ge(cPE.s, bPE + j + 1)
                        s.copy(xlat[:, j * 512:(j + 1) * 512],
                               psF[j % NFC][:, :]).then_inc(cEv.s, 1)
                    cEv.n += nin

            stage(16, NCC, self.fc1_w, 4096, lambda cc: hslice(cc, th), evac_fc1)

            def evac_ew(blk, nin, bPE):
                @blk.vector
                def _(v):
                    for j in range(nin):
                        v.wait_ge(cPE.s, bPE + j + 1)
                        v.tensor_mul(moe[:, j * 512:(j + 1) * 512], psF[j % NFC][:, :],
                                     br[:, th * 512:(th + 1) * 512]).then_inc(cEv.s, 1)
                    cEv.n += nin

            stage(16, NLC, self.ew_w, 2048,
                  lambda lc: xlat[:, lc * 512:(lc + 1) * 512], evac_ew)

            def evac_fc2(blk, nin, bPE):
                @blk.vector
                def _(v):
                    for j in range(nin):
                        v.wait_ge(cPE.s, bPE + j + 1)
                        v.tensor_add(cslice(j), cslice(j),
                                     psF[j % NFC][:, :]).then_inc(cEv.s, 1)
                    cEv.n += nin

            stage(NCC, NLC, self.fc2_w, 2048,
                  lambda lc: moe[:, lc * 512:(lc + 1) * 512], evac_fc2)

    def p8d_ln3_out(self, th, cslice):
        nc = self.nc
        with contextlib.ExitStack() as p:
            cbf = [p.enter_context(nc.sbuf_tensor(f"p8db{th}{i}", [128, 512], BF16))
                   for i in range(2)]
            sqb = [p.enter_context(nc.sbuf_tensor(f"p8ds{th}{i}", [128, 512], BF16))
                   for i in range(2)]
            t1b = [p.enter_context(nc.sbuf_tensor(f"p8dt{th}{i}", [128, 512], F32))
                   for i in range(2)]
            t2b = [p.enter_context(nc.sbuf_tensor(f"p8dg{th}{i}", [128, 512], F32))
                   for i in range(2)]
            xrb = [p.enter_context(nc.sbuf_tensor(f"p8dx{th}{i}", [128, 512], F32))
                   for i in range(2)]
            obuf = [p.enter_context(nc.sbuf_tensor(f"p8do{th}{i}", [128, 512], F32))
                    for i in range(2)]
            mu3 = p.enter_context(nc.sbuf_tensor(f"p8dm{th}", [128, 512], F32))
            a3 = p.enter_context(nc.sbuf_tensor(f"p8da{th}", [128, 512], F32))
            te3 = p.enter_context(nc.sbuf_tensor(f"p8de{th}", [128, 512], F32))
            ps1 = p.enter_context(nc.psum_tensor(f"p8dp1{th}", [128, 512], F32))
            ps2 = p.enter_context(nc.psum_tensor(f"p8dp2{th}", [128, 512], F32))
            cCp = self.sem("p8d_cCp")
            cSq = self.sem("p8d_cSq")
            cMM = self.sem("p8d_cMM")
            cR = self.sem("p8d_cR")
            dX = [self.sem("p8d_dXa"), self.sem("p8d_dXb")]
            cT = self.sem("p8d_cT")
            cG = self.sem("p8d_cG")
            cO = self.sem("p8d_cO")
            dOut = [self.sem("p8d_dOa"), self.sem("p8d_dOb")]
            bCp, bSq, bMM, bR = cCp.n, cSq.n, cMM.n, cR.n
            bT, bG, bO = cT.n, cG.n, cO.n
            bX = [dX[0].n, dX[1].n]
            bOut = [dOut[0].n, dOut[1].n]
            with nc.Block() as blk:
                @blk.vector
                def _(v):
                    for hc in range(NCC):
                        if hc >= 2:
                            v.wait_ge(cMM.s, bMM + hc - 1)
                        v.tensor_copy(cbf[hc % 2][:, :], cslice(hc)).then_inc(cCp.s, 1)
                    cCp.n += NCC

                @blk.scalar
                def _(s):
                    for hc in range(NCC):
                        if hc >= 2:
                            s.wait_ge(cMM.s, bMM + hc - 1)
                        s.activation(sqb[hc % 2][:, :], cslice(hc),
                                     AF.Square).then_inc(cSq.s, 1)
                    cSq.n += NCC

                @blk.tensor
                def _(t):
                    for hc in range(NCC):
                        t.wait_ge(cCp.s, bCp + hc + 1)
                        t.wait_ge(cSq.s, bSq + hc + 1)
                        t.matmul(ps1[:, :], self.ones_bf[:, :], cbf[hc % 2][:, :],
                                 start=(hc == 0), stop=(hc == NCC - 1))
                        t.matmul(ps2[:, :], self.ones_bf[:, :], sqb[hc % 2][:, :],
                                 start=(hc == 0), stop=(hc == NCC - 1)).then_inc(cMM.s, 1)
                    cMM.n += NCC

            with nc.Block() as blk:
                @blk.vector
                def _(v):
                    v.tensor_scalar(mu3[:, :], ps1[:, :], 1.0 / C,
                                    None, ALU.mult).then_inc(cR.s, 1)
                    v.tensor_scalar(te3[:, :], ps2[:, :], 1.0 / C,
                                    None, ALU.mult).then_inc(cR.s, 1)
                    v.wait_ge(cR.s, bR + 2)
                    v.tensor_mul(a3[:, :], mu3[:, :], mu3[:, :]).then_inc(cR.s, 1)
                    v.wait_ge(cR.s, bR + 3)
                    v.tensor_sub(te3[:, :], te3[:, :], a3[:, :]).then_inc(cR.s, 1)
                    cR.n += 4

                @blk.scalar
                def _(s):
                    s.wait_ge(cR.s, bR + 4)
                    s.activation(te3[:, :], te3[:, :], AF.Ln,
                                 bias=self.eps_t[:, :]).then_inc(cR.s, 1)
                    cR.n += 1
                    s.wait_ge(cR.s, bR + 5)
                    s.activation(a3[:, :], te3[:, :], AF.Exp, scale=self.neg_half[:, :])

            with nc.Block() as blk:
                @blk.sync
                def _(sy):
                    for hc in range(NCC):
                        if hc >= 2:
                            sy.wait_ge(cO.s, bO + hc - 1)
                        sy.dma_start(xrb[hc % 2][:, :],
                                     self.x2_scr[hc, :, th * 512:(th + 1) * 512]
                                     ).then_inc(dX[hc % 2].s, 16)
                    for z in range(2):
                        dX[z].n += 16 * (NCC // 2)
                    sy.wait_ge(cO.s, bO + NCC)

                @blk.vector
                def _(v):
                    for hc in range(NCC):
                        if hc >= 2:
                            v.wait_ge(dOut[hc % 2].s,
                                      bOut[hc % 2] + 16 * (hc // 2))
                        v.tensor_sub(t1b[hc % 2][:, :], cslice(hc), mu3[:, :])
                        v.tensor_mul(t2b[hc % 2][:, :], t1b[hc % 2][:, :], a3[:, :])
                        v.tensor_scalar(t1b[hc % 2][:, :], t2b[hc % 2][:, :],
                                        self.gams[:, hc:hc + 1],
                                        self.bets[:, hc:hc + 1], ALU.mult, ALU.add)
                        v.wait_ge(dX[hc % 2].s, bX[hc % 2] + 16 * (hc // 2 + 1))
                        v.tensor_add(obuf[hc % 2][:, :], t1b[hc % 2][:, :],
                                     xrb[hc % 2][:, :]).then_inc(cO.s, 1)
                    cO.n += NCC

                @blk.gpsimd
                def _(g):
                    for hc in range(NCC):
                        g.wait_ge(cO.s, bO + hc + 1)
                        g.dma_start(self.out_t[hc * 128:(hc + 1) * 128,
                                               th * 512:(th + 1) * 512],
                                    obuf[hc % 2][:, :]).then_inc(dOut[hc % 2].s, 16)
                    for z in range(2):
                        dOut[z].n += 16 * (NCC // 2)
                        g.wait_ge(dOut[z].s, bOut[z] + 16 * (NCC // 2))

    def dbg_dumps(self):
        if not _DBG:
            return
        nc = self.nc
        dD = self.sem("dbg_d")
        with nc.Block() as blk:
            @blk.sync
            def _(sy):
                if 2 <= _DBG_PHASE <= 6:
                    dD.dma(sy.dma_start(self.dbg_xln[:, :], self.reg_bf[:, :]))
                pairs = [(0, self.bias_m[:, :]), (1, self.bias_o[:, :]),
                         (2, self.maxq2[:, :]), (3, self.maxk2[:, :])]
                for i, src in pairs:
                    dD.dma(sy.dma_start(self.dbg_misc[:, i:i + 1], src,
                                        allow_slow_non_contiguous=True))
                sy.wait_ge(dD.s, dD.n)


def _build():
    return Builder().build()


def _enable_ldw_opt():
    try:
        from concourse.compiler_utils import get_compiler_flags, set_compiler_flags
        flags = [f.replace("--enable-ldw-opt=false", "--enable-ldw-opt=true")
                 for f in get_compiler_flags()]
        set_compiler_flags(flags)
    except Exception:
        pass


def kernel(**inputs):
    _enable_ldw_opt()
    in_maps = _prep(inputs)
    nc = _build()
    res = run_bass_kernel_spmd(nc, in_maps, core_ids=list(range(8)))
    return _reassemble([np.asarray(r["out"]) for r in res.results])


# version 19
# speedup vs baseline: 1.1716x; 1.0073x over previous
import math
import contextlib
import numpy as np
import concourse.bass as bass
import concourse.mybir as mybir
from concourse.bass_utils import run_bass_kernel_spmd

F32 = mybir.dt.float32
BF16 = mybir.dt.bfloat16
AF = mybir.ActivationFunctionType
ALU = mybir.AluOpType
AX = mybir.AxisListType

C = 4096
T = 2048
NQ = 1024
KVD = 2048
NCC = C // 128
NKVC = KVD // 128
NH = 32
NKVH = 16
INT = 8192
LAT = 2048
NIC = INT // 128
NLC = LAT // 128
SCALE = 1.0 / math.sqrt(32.0)
EPS = 1e-5

_DBG_PHASE = 99
_DBG = False


def _img_T(W, blkw):
    dout, cin = W.shape
    nb, ncc = dout // blkw, cin // 128
    return np.ascontiguousarray(
        W.reshape(nb, blkw, ncc, 128).transpose(0, 3, 2, 1)).reshape(nb, 128, ncc * blkw)


def _img_N(W, blkw):
    cin, dout = W.shape
    nb, ncc = dout // blkw, cin // 128
    return np.ascontiguousarray(
        W.reshape(ncc, 128, nb, blkw).transpose(2, 1, 0, 3)).reshape(nb, 128, ncc * blkw)


def _prep(inputs):
    g = lambda k: np.asarray(inputs[k], dtype=np.float32)
    x = g("L0_attn_x")
    shared = dict(
        wq=_img_T(g("L0_attn_Wq"), 256),
        wk=_img_T(g("L0_attn_Wk"), 256),
        wv=_img_T(g("L0_attn_Wv"), 256),
        up=_img_N(g("L0_ffn_up_w"), 128),
        dwn=_img_N(g("L0_ffn_down_w"), 128),
        fc1=_img_N(g("L0_ffn_fc1_w"), 128),
        fc2=_img_N(g("L0_ffn_fc2_w"), 128),
        ew=_img_N(g("L0_ffn_expert_w"), 128),
        gate=np.ascontiguousarray(
            g("L0_ffn_gate_w").reshape(32, 128, 8).transpose(1, 0, 2)).reshape(128, 256),
        bq=np.ascontiguousarray(g("L0_attn_bq").reshape(32, 128).T),
        bk=np.ascontiguousarray(g("L0_attn_bk").reshape(16, 128).T),
        bv=np.ascontiguousarray(g("L0_attn_bv").reshape(16, 128).T),
        gam=np.ascontiguousarray(g("L0_ffn_ln_gamma").reshape(32, 128).T),
        bet=np.ascontiguousarray(g("L0_ffn_ln_beta").reshape(32, 128).T),
    )
    in_maps = []
    for c in range(8):
        s, e = c // 2, c % 2
        perm = list(range(e, 16, 2))
        xtc = np.ascontiguousarray(x[s].reshape(16, 128, C)[perm].reshape(NQ, C).T)
        m = dict(shared)
        m["xt"] = xtc
        m["epar"] = np.full((128, 1), float(e), np.float32)
        in_maps.append(m)
    return in_maps


def _reassemble(couts):
    out = np.empty((4, T, C), np.float32)
    for c in range(8):
        s, e = c // 2, c % 2
        co = couts[c]
        for u in range(8):
            out[s, (2 * u + e) * 128:(2 * u + e + 1) * 128, :] = \
                co[:, u * 128:(u + 1) * 128].T
    return out


class Sem:

    def __init__(self, es, nc, name):
        self.s = es.enter_context(nc.semaphore(name))
        self.n = 0

    def inc(self, instr, k=1):
        instr.then_inc(self.s, k)
        self.n += k
        return self.n

    def dma(self, instr):
        return self.inc(instr, 16)


class Builder:
    def __init__(self):
        self.nc = bass.Bass("TRN2", target_bir_lowering=False)

    def sem(self, name):
        if not hasattr(self, "_sems"):
            self._sems = {}
        if name not in self._sems:
            self._sems[name] = Sem(self.es, self.nc, name)
        return self._sems[name]

    def sb(self, name, shape, dt):
        return self.es.enter_context(self.nc.sbuf_tensor(name, shape, dt))

    def build(self):
        nc = self.nc
        self.xt = nc.dram_tensor("xt", [C, NQ], F32, kind="ExternalInput")
        self.wq = nc.dram_tensor("wq", [16, 128, 8192], F32, kind="ExternalInput")
        self.wk = nc.dram_tensor("wk", [8, 128, 8192], F32, kind="ExternalInput")
        self.wv = nc.dram_tensor("wv", [8, 128, 8192], F32, kind="ExternalInput")
        self.up_w = nc.dram_tensor("up", [64, 128, 4096], F32, kind="ExternalInput")
        self.dwn_w = nc.dram_tensor("dwn", [32, 128, 8192], F32, kind="ExternalInput")
        self.fc1_w = nc.dram_tensor("fc1", [16, 128, 4096], F32, kind="ExternalInput")
        self.fc2_w = nc.dram_tensor("fc2", [32, 128, 2048], F32, kind="ExternalInput")
        self.ew_w = nc.dram_tensor("ew", [16, 128, 2048], F32, kind="ExternalInput")
        self.gate_w = nc.dram_tensor("gate", [128, 256], F32, kind="ExternalInput")
        self.bq_t = nc.dram_tensor("bq", [128, 32], F32, kind="ExternalInput")
        self.bk_t = nc.dram_tensor("bk", [128, 16], F32, kind="ExternalInput")
        self.bv_t = nc.dram_tensor("bv", [128, 16], F32, kind="ExternalInput")
        self.gam_t = nc.dram_tensor("gam", [128, 32], F32, kind="ExternalInput")
        self.bet_t = nc.dram_tensor("bet", [128, 32], F32, kind="ExternalInput")
        self.epar_t = nc.dram_tensor("epar", [128, 1], F32, kind="ExternalInput")
        self.out_t = nc.dram_tensor("out", [C, NQ], F32, kind="ExternalOutput")

        scr_kind = "ExternalOutput" if _DBG else "Internal"
        self.kvo_scr = nc.dram_tensor("kvo_scr", [32 * 128, NQ], BF16, kind="Internal")
        self.kva_scr = nc.dram_tensor("kva_scr", [2 * 32 * 128, NQ], BF16,
                                      kind=scr_kind)
        self.up_bf = nc.dram_tensor("up_bf", [64, 128, 4096], BF16, kind="Internal")
        self.dwn_bf = nc.dram_tensor("dwn_bf", [32, 128, 8192], BF16, kind="Internal")
        self.mk_in = nc.dram_tensor("mk_in", [1, 1], F32, kind="Internal")
        self.mk_out = nc.dram_tensor("mk_out", [1, 1], F32, kind="Internal")
        self.q_scr = nc.dram_tensor("q_scr", [NH, 128, NQ], BF16, kind=scr_kind)
        self.x2_scr = nc.dram_tensor("x2_scr", [NCC, 128, NQ], F32, kind=scr_kind)
        if _DBG:
            self.dbg_xln = nc.dram_tensor("dbg_xln", [128, NCC * T], BF16,
                                          kind="ExternalOutput")
            self.dbg_misc = nc.dram_tensor("dbg_misc", [128, 16], F32,
                                           kind="ExternalOutput")

        with contextlib.ExitStack() as es:
            self.es = es
            self.alloc_persistent()
            self.p0_consts()
            if _DBG_PHASE >= 1:
                self.p1_ln1_stats()
            if _DBG_PHASE >= 2:
                self.p2_xln()
            self.ln1_es.close()
            if _DBG_PHASE >= 3:
                self.proj_feat("wk", self.wk, 8, self.bks, 0, norms=self.maxk2,
                               spill_ks=range(0, 16))
            if _DBG_PHASE >= 4:
                self.proj_feat("wv", self.wv, 8, self.bvs, 16, norms=None,
                               spill_ks=range(16, 32))
            if _DBG_PHASE >= 5:
                self.p5c_issue()
                self.proj_q()
                self.p5c_wait()
                self.p5b_bias()
            if _DBG_PHASE >= 6:
                self.p6_attention()
            self.spb_es.close()
            if _DBG_PHASE >= 7:
                self.br = es.enter_context(nc.sbuf_tensor("p8br", [128, NQ], F32))
                self.lgf_es = contextlib.ExitStack()
                self.lgf = self.lgf_es.enter_context(
                    nc.sbuf_tensor("lgf", [8, NQ], F32))
                self.gates = self.lgf_es.enter_context(
                    nc.sbuf_tensor("gates", [128, 256], BF16))
                self.p7_ln2()
            if _DBG_PHASE >= 8:
                self.p8_ffn()
            self.dbg_dumps()
        return nc

    def spill_load(self, g, k):
        b = k % 2
        if k >= 2:
            g.wait_ge(self.dSpS[b].s, 16 * (k // 2))
        if k < 64:
            src = self.up_w[k, :, :]
        else:
            j = k - 64
            src = self.dwn_w[j // 2, :, (j % 2) * 4096:(j % 2) * 4096 + 4096]
        g.dma_start(self.spb[b][:, :], src).then_inc(self.dSpL[b].s, 16)
        self.dSpL[b].n += 16

    def spill_store(self, sy, k):
        b = k % 2
        sy.wait_ge(self.dSpL[b].s, 16 * (k // 2 + 1))
        if k < 64:
            dst = self.up_bf[k, :, :]
        else:
            j = k - 64
            dst = self.dwn_bf[j // 2, :, (j % 2) * 4096:(j % 2) * 4096 + 4096]
        sy.dma_start(dst, self.spb[b][:, :]).then_inc(self.dSpS[b].s, 16)
        self.dSpS[b].n += 16

    def alloc_persistent(self):
        self.ones_bf = self.sb("ones_bf", [128, 128], BF16)
        self.ident_bf = self.sb("ident_bf", [128, 128], BF16)
        self.ident_f = self.sb("ident_f", [128, 128], F32)
        self.tri = self.sb("tri", [128, 128], F32)
        self.bqs = self.sb("bqs", [128, 32], F32)
        self.bks = self.sb("bks", [128, 16], F32)
        self.bvs = self.sb("bvs", [128, 16], F32)
        self.gams = self.sb("gams", [128, 32], F32)
        self.bets = self.sb("bets", [128, 32], F32)
        self.epar = self.sb("epar_s", [128, 1], F32)
        self.eps_t = self.sb("eps_t", [128, 1], F32)
        self.neg_half = self.sb("neg_half", [128, 1], F32)
        self.neg_one = self.sb("neg_one", [128, 1], F32)
        self.scale_t = self.sb("scale_t", [128, 1], F32)
        self.maxq2 = self.sb("maxq2", [128, 1], F32)
        self.maxk2 = self.sb("maxk2", [128, 1], F32)
        self.bias_m = self.sb("bias_m", [128, 1], F32)
        self.maskA = self.sb("maskA", [128, 128], F32)
        self.maskB = self.sb("maskB", [128, 128], F32)
        self.onesf = self.sb("onesf", [1, 128], F32)
        self.dSpL = [self.sem("sp_dLa"), self.sem("sp_dLb")]
        self.dSpS = [self.sem("sp_dSa"), self.sem("sp_dSb")]
        self.mk_sb = self.sb("mk_sb", [1, 1], F32)
        self.reg = self.es.enter_context(self.nc.sbuf_tensor("reg", [128, NCC * NQ], F32))
        self.reg_bf = self.reg.bitcast(BF16)
        self.spb_es = contextlib.ExitStack()
        self.spb = [self.spb_es.enter_context(
            self.nc.sbuf_tensor(f"spb{i}", [128, 4096], BF16)) for i in range(2)]
        self.ln1_es = contextlib.ExitStack()
        self.mu1 = self.ln1_es.enter_context(self.nc.sbuf_tensor("mu1", [128, NQ], F32))
        self.a1 = self.ln1_es.enter_context(self.nc.sbuf_tensor("a1", [128, NQ], F32))

    def p0_consts(self):
        nc = self.nc
        d0 = self.sem("p0_d")
        c0 = self.sem("p0_c")
        with nc.Block() as blk:
            @blk.gpsimd
            def _(g):
                c0.inc(g.memset(self.ident_bf[:, :], 1.0))
                c0.inc(g.memset(self.ident_f[:, :], 1.0))
                c0.inc(g.memset(self.tri[:, :], 0.0))
                g.wait_ge(c0.s, c0.n)
                c0.inc(g.affine_select(self.ident_bf[:, :], self.ident_bf[:, :],
                                       [[1, 128]], ALU.is_equal, 0.0,
                                       channel_multiplier=-1))
                c0.inc(g.affine_select(self.ident_f[:, :], self.ident_f[:, :],
                                       [[1, 128]], ALU.is_equal, 0.0,
                                       channel_multiplier=-1))
                c0.inc(g.affine_select(self.tri[:, :], self.tri[:, :], [[1, 128]],
                                       ALU.is_ge, -30000.0, channel_multiplier=-1))
                g.memset(self.ones_bf[:, :], 1.0)
                g.memset(self.eps_t[:, :], EPS)
                g.memset(self.neg_half[:, :], -0.5)
                g.memset(self.neg_one[:, :], -1.0)
                g.memset(self.scale_t[:, :], SCALE)
                g.memset(self.maxq2[:, :], 0.0)
                g.memset(self.maxk2[:, :], 0.0)
                g.memset(self.onesf[:, :], 1.0)

            @blk.sync
            def _(sy):
                for dst, src in ((self.bqs, self.bq_t), (self.bks, self.bk_t),
                                 (self.bvs, self.bv_t), (self.gams, self.gam_t),
                                 (self.bets, self.bet_t), (self.epar, self.epar_t)):
                    d0.dma(sy.dma_start(dst[:, :], src[:, :]))
                sy.wait_ge(d0.s, d0.n)

            @blk.vector
            def _(v):
                v.wait_ge(d0.s, d0.n)
                v.wait_ge(c0.s, 6)
                v.tensor_scalar(self.maskA[:, :], self.tri[:, :],
                                self.epar[:, 0:1], None, ALU.mult).then_inc(c0.s, 1)
                v.wait_ge(c0.s, 7)
                v.tensor_scalar(self.bias_m[:, :], self.epar[:, 0:1], 30000.0, -30000.0,
                                ALU.mult, ALU.add).then_inc(c0.s, 1)
                v.wait_ge(c0.s, 8)
                v.tensor_scalar(self.maskB[:, :], self.maskA[:, :],
                                self.bias_m[:, 0:1], None, ALU.add).then_inc(c0.s, 1)
                v.wait_ge(c0.s, 9)
                v.tensor_sub(self.maskA[:, :], self.tri[:, :],
                             self.maskA[:, :]).then_inc(c0.s, 1)

    def p1_ln1_stats(self):
        nc = self.nc
        xln = self.reg_bf
        with contextlib.ExitStack() as p:
            xbf = [p.enter_context(nc.sbuf_tensor(f"p1xbf{i}", [128, NQ], BF16)) for i in range(2)]
            sqb = [p.enter_context(nc.sbuf_tensor(f"p1sqb{i}", [128, NQ], BF16)) for i in range(2)]
            xf = [p.enter_context(nc.sbuf_tensor(f"p2xf{i}", [128, NQ], F32)) for i in range(2)]
            t1b = [p.enter_context(nc.sbuf_tensor(f"p2t1{i}", [128, NQ], F32)) for i in range(2)]
            ps_s1 = p.enter_context(nc.psum_tensor("p1s1", [128, NQ], F32))
            ps_s2 = p.enter_context(nc.psum_tensor("p1s2", [128, NQ], F32))
            t_e2 = p.enter_context(nc.sbuf_tensor("p1e2", [128, NQ], F32))
            dL = [self.sem("p1_dLa"), self.sem("p1_dLb")]
            dL2 = [self.sem("p2_dLa"), self.sem("p2_dLb")]
            cSq = self.sem("p1_cSq")
            cMM = self.sem("p1_cMM")
            cR = self.sem("p1_cR")
            cAp = self.sem("p2_cAp")

            with nc.Block() as blk:
                @blk.gpsimd
                def _(g):
                    for cc in range(NCC):
                        if cc >= 2:
                            g.wait_ge(cMM.s, cc - 1)
                        g.dma_start(xbf[cc % 2][:, :],
                                    self.xt[cc * 128:(cc + 1) * 128, :]
                                    ).then_inc(dL[cc % 2].s, 16)
                    g.wait_ge(cMM.s, NCC)

                @blk.sync
                def _(sy):
                    for cc in range(NCC):
                        if cc >= 2:
                            sy.wait_ge(cAp.s, 2 * (cc - 1))
                        sy.dma_start(xf[cc % 2][:, :],
                                     self.xt[cc * 128:(cc + 1) * 128, :]
                                     ).then_inc(dL2[cc % 2].s, 16)
                    sy.wait_ge(cAp.s, 2 * NCC)

                @blk.scalar
                def _(s):
                    for cc in range(NCC):
                        s.wait_ge(dL[cc % 2].s, 16 * (cc // 2 + 1))
                        if cc >= 2:
                            s.wait_ge(cMM.s, cc - 1)
                        s.activation(sqb[cc % 2][:, :], xbf[cc % 2][:, :],
                                     AF.Square).then_inc(cSq.s, 1)
                    s.wait_ge(cR.s, 4)
                    s.activation(t_e2[:, :], t_e2[:, :], AF.Ln,
                                 bias=self.eps_t[:, :]).then_inc(cR.s, 1)
                    s.wait_ge(cR.s, 5)
                    s.activation(self.a1[:, :], t_e2[:, :], AF.Exp,
                                 scale=self.neg_half[:, :]).then_inc(cR.s, 1)

                @blk.tensor
                def _(t):
                    for cc in range(NCC):
                        t.wait_ge(cSq.s, cc + 1)
                        for n in range(2):
                            t.matmul(ps_s1[:, 512 * n:512 * (n + 1)], self.ones_bf[:, :],
                                     xbf[cc % 2][:, 512 * n:512 * (n + 1)],
                                     start=(cc == 0), stop=(cc == NCC - 1))
                            mm = t.matmul(ps_s2[:, 512 * n:512 * (n + 1)], self.ones_bf[:, :],
                                          sqb[cc % 2][:, 512 * n:512 * (n + 1)],
                                          start=(cc == 0), stop=(cc == NCC - 1))
                        mm.then_inc(cMM.s, 1)

                @blk.vector
                def _(v):
                    v.wait_ge(cMM.s, NCC)
                    v.tensor_scalar(self.mu1[:, :], ps_s1[:, :], 1.0 / C,
                                    None, ALU.mult).then_inc(cR.s, 1)
                    v.tensor_scalar(t_e2[:, :], ps_s2[:, :], 1.0 / C,
                                    None, ALU.mult).then_inc(cR.s, 1)
                    v.wait_ge(cR.s, 2)
                    v.tensor_mul(self.a1[:, :], self.mu1[:, :],
                                 self.mu1[:, :]).then_inc(cR.s, 1)
                    v.wait_ge(cR.s, 3)
                    v.tensor_sub(t_e2[:, :], t_e2[:, :], self.a1[:, :]).then_inc(cR.s, 1)
                    v.wait_ge(cR.s, 6)
                    for cc in range(NCC):
                        v.wait_ge(dL2[cc % 2].s, 16 * (cc // 2 + 1))
                        v.tensor_sub(t1b[cc % 2][:, :], xf[cc % 2][:, :],
                                     self.mu1[:, :]).then_inc(cAp.s, 1)
                        v.wait_ge(cAp.s, 2 * cc + 1)
                        v.tensor_mul(xln[:, cc * NQ:(cc + 1) * NQ], t1b[cc % 2][:, :],
                                     self.a1[:, :]).then_inc(cAp.s, 1)

    def p2_xln(self):
        pass

    def proj_feat(self, name, wimg, nblk, bias_img, scr_off, norms, spill_ks=()):
        nc = self.nc
        xln = self.reg_bf
        n_slot = nblk * 2
        spill_ks = list(spill_ks)
        with contextlib.ExitStack() as p:
            wbuf = [p.enter_context(nc.sbuf_tensor(f"{name}wb{i}", [128, 8192], BF16))
                    for i in range(2)]
            kbf = [p.enter_context(nc.sbuf_tensor(f"{name}kb{i}", [128, 1024], BF16))
                   for i in range(2)]
            ksq = p.enter_context(nc.sbuf_tensor(f"{name}sq", [128, 1024], BF16))
            nrm_s = p.enter_context(nc.sbuf_tensor(f"{name}nrm", [128, 1], F32))
            ps = [p.enter_context(nc.psum_tensor(f"{name}ps{i}", [128, 1024], F32))
                  for i in range(2)]
            ps_n = (p.enter_context(nc.psum_tensor(f"{name}pn", [128, 1024], F32))
                    if norms is not None else None)
            dW = [self.sem(name + "_dWa"), self.sem(name + "_dWb")]
            dS = [self.sem(name + "_dSa"), self.sem(name + "_dSb")]
            cPE = self.sem(name + "_cPE")
            cC = self.sem(name + "_cC")
            cSq = self.sem(name + "_cSq")
            cN = self.sem(name + "_cN")
            cFree = cSq if norms is not None else cC

            def main_val(s):
                if norms is None:
                    return s + 1
                return 1 if s == 0 else 2 * s

            def norm_val(s):
                return 2 * s + 3 if s < n_slot - 1 else 2 * n_slot

            with nc.Block() as blk:
                @blk.gpsimd
                def _(g):
                    for b in range(nblk):
                        if b >= 2:
                            g.wait_ge(cFree.s, 2 * b - 2)
                        g.dma_start(wbuf[b % 2][:, :],
                                    wimg[b, :, :]).then_inc(dW[b % 2].s, 16)
                        for k in spill_ks[b * len(spill_ks) // nblk:
                                          (b + 1) * len(spill_ks) // nblk]:
                            self.spill_load(g, k)
                    g.wait_ge(cFree.s, n_slot)

                @blk.tensor
                def _(t):
                    for s in range(n_slot):
                        dch = s
                        b, sub = dch // 2, dch % 2
                        t.wait_ge(dW[b % 2].s, 16 * (b // 2 + 1))
                        if s >= 2:
                            t.wait_ge(cFree.s, s - 1)
                        for cc in range(NCC):
                            lhs = wbuf[b % 2][:, cc * 256 + sub * 128:cc * 256 + sub * 128 + 128]
                            for n in range(2):
                                mm = t.matmul(ps[s % 2][:, 512 * n:512 * (n + 1)], lhs,
                                              xln[:, cc * NQ + 512 * n:
                                                  cc * NQ + 512 * (n + 1)],
                                              start=(cc == 0), stop=(cc == NCC - 1))
                        mm.then_inc(cPE.s, 1)
                        if norms is not None and s >= 1:
                            t.wait_ge(cSq.s, s)
                            if s >= 2:
                                t.wait_ge(cN.s, s - 1)
                            t.matmul(ps_n[:, 0:512], self.ones_bf[:, :], ksq[:, 0:512],
                                     start=True, stop=True)
                            t.matmul(ps_n[:, 512:1024], self.ones_bf[:, :], ksq[:, 512:1024],
                                     start=True, stop=True).then_inc(cPE.s, 1)
                    if norms is not None:
                        t.wait_ge(cSq.s, n_slot)
                        t.wait_ge(cN.s, n_slot - 1)
                        t.matmul(ps_n[:, 0:512], self.ones_bf[:, :], ksq[:, 0:512],
                                 start=True, stop=True)
                        t.matmul(ps_n[:, 512:1024], self.ones_bf[:, :], ksq[:, 512:1024],
                                 start=True, stop=True).then_inc(cPE.s, 1)

                @blk.scalar
                def _(sc):
                    for s in range(n_slot):
                        dch = s
                        sc.wait_ge(cPE.s, main_val(s) if s == 0 or norms is None
                                   else norm_val(s - 1))
                        if s >= 2:
                            sc.wait_ge(dS[s % 2].s, 16 * (s // 2))
                        cC.inc(sc.activation(kbf[s % 2][:, :], ps[s % 2][:, :], AF.Identity,
                                             bias=bias_img[:, dch:dch + 1]))
                        if norms is not None:
                            sc.activation(ksq[:, :], ps[s % 2][:, :], AF.Square,
                                          bias=bias_img[:, dch:dch + 1]).then_inc(cSq.s, 1)

                if norms is not None:
                    cNr = self.sem(name + "_cNr")

                    @blk.vector
                    def _(v):
                        for s in range(n_slot):
                            v.wait_ge(cPE.s, norm_val(s))
                            v.reduce_max(nrm_s[:, :], ps_n[:, :], AX.X).then_inc(cNr.s, 1)
                            v.wait_ge(cNr.s, s + 1)
                            v.tensor_max(norms[:, :], norms[:, :],
                                         nrm_s[:, :]).then_inc(cN.s, 1)
                            if s < n_slot - 1:
                                v.wait_ge(cN.s, s + 1)

                @blk.sync
                def _(sy):
                    for s in range(n_slot):
                        sy.wait_ge(cC.s, s + 1)
                        sy.dma_start(
                            self.kvo_scr[(scr_off + s) * 128:(scr_off + s + 1) * 128, :],
                            kbf[s % 2][:, :]).then_inc(dS[s % 2].s, 16)
                        for k in spill_ks[s * len(spill_ks) // n_slot:
                                          (s + 1) * len(spill_ks) // n_slot]:
                            self.spill_store(sy, k)
                    sy.wait_ge(dS[0].s, 16 * (n_slot // 2))
                    sy.wait_ge(dS[1].s, 16 * (n_slot // 2))

    def p5c_issue(self):
        nc = self.nc
        groups = [[0, 1], [2, 3], [4, 5], [6, 7]]
        cg = self.sem("p5c_c")
        dg = self.sem("p5c_d")
        with nc.Block() as blk:
            @blk.gpsimd
            def _(g):
                g.dma_start(self.mk_in[:, :], self.maxk2[0:1, 0:1]).then_inc(dg.s, 16)
                g.wait_ge(dg.s, 16)
                g.collective_compute(
                    "AllGather", ALU.bypass, replica_groups=groups,
                    ins=[self.kvo_scr.ap().opt()],
                    outs=[self.kva_scr.ap().opt()]).then_inc(cg.s, 1)
                g.collective_compute(
                    "AllReduce", ALU.max, replica_groups=groups,
                    ins=[self.mk_in.ap().opt()],
                    outs=[self.mk_out.ap().opt()]).then_inc(cg.s, 1)

    def p5c_wait(self):
        nc = self.nc
        cg = self.sem("p5c_c")
        dg = self.sem("p5c_d")
        with contextlib.ExitStack() as pp:
            ps_mk = pp.enter_context(nc.psum_tensor("ps_mk", [128, 1], F32))
            with nc.Block() as blk:
                @blk.gpsimd
                def _(g):
                    g.wait_ge(cg.s, 2)
                    g.dma_start(self.mk_sb[:, :], self.mk_out[:, :]).then_inc(dg.s, 16)
                    g.wait_ge(dg.s, 32)

                @blk.tensor
                def _(t):
                    t.wait_ge(dg.s, 32)
                    t.matmul(ps_mk[:, :], self.onesf[:, :], self.mk_sb[:, :],
                             start=True, stop=True).then_inc(cg.s, 1)

                @blk.scalar
                def _(s):
                    s.wait_ge(cg.s, 3)
                    s.copy(self.maxk2[:, :], ps_mk[:, :])

    def proj_q(self):
        nc = self.nc
        xln = self.reg_bf
        n_slot = 64
        with contextlib.ExitStack() as p:
            wbuf = [p.enter_context(nc.sbuf_tensor(f"q_wb{i}", [128, 8192], BF16))
                    for i in range(2)]
            kbf = [p.enter_context(nc.sbuf_tensor(f"q_kb{i}", [128, 512], BF16))
                   for i in range(2)]
            ksq = p.enter_context(nc.sbuf_tensor("q_sq", [128, 512], BF16))
            nrm_s = p.enter_context(nc.sbuf_tensor("q_nrm", [128, 1], F32))
            ps = [p.enter_context(nc.psum_tensor(f"q_ps{i}", [128, 512], F32))
                  for i in range(2)]
            ps_n = p.enter_context(nc.psum_tensor("q_pn", [128, 512], F32))
            dW = [self.sem("q_dWa"), self.sem("q_dWb")]
            dS = [self.sem("q_dSa"), self.sem("q_dSb")]
            cPE = self.sem("q_cPE")
            cC = self.sem("q_cC")
            cSq = self.sem("q_cSq")
            cN = self.sem("q_cN")
            with nc.Block() as blk:
                @blk.gpsimd
                def _(g):
                    qs = list(range(32, 52))
                    for b in range(16):
                        if b >= 2:
                            g.wait_ge(cSq.s, 4 * b - 4)
                        g.dma_start(wbuf[b % 2][:, :],
                                    self.wq[b, :, :]).then_inc(dW[b % 2].s, 16)
                        for k in qs[b * 20 // 16:(b + 1) * 20 // 16]:
                            self.spill_load(g, k)
                    g.wait_ge(cSq.s, n_slot)

                @blk.tensor
                def _(t):
                    for s in range(n_slot):
                        dch, half = s // 2, s % 2
                        b, sub = dch // 2, dch % 2
                        t.wait_ge(dW[b % 2].s, 16 * (b // 2 + 1))
                        if s >= 2:
                            t.wait_ge(cSq.s, s - 1)
                        for cc in range(NCC):
                            lhs = wbuf[b % 2][:, cc * 256 + sub * 128:cc * 256 + sub * 128 + 128]
                            mm = t.matmul(ps[s % 2][:, :], lhs,
                                          xln[:, cc * NQ + half * 512:cc * NQ + half * 512 + 512],
                                          start=(cc == 0), stop=(cc == NCC - 1))
                        mm.then_inc(cPE.s, 1)
                        if s >= 1:
                            t.wait_ge(cSq.s, s)
                            if s >= 2:
                                t.wait_ge(cN.s, s - 1)
                            t.matmul(ps_n[:, :], self.ones_bf[:, :], ksq[:, :],
                                     start=True, stop=True).then_inc(cPE.s, 1)
                    t.wait_ge(cSq.s, n_slot)
                    t.wait_ge(cN.s, n_slot - 1)
                    t.matmul(ps_n[:, :], self.ones_bf[:, :], ksq[:, :],
                             start=True, stop=True).then_inc(cPE.s, 1)

                @blk.scalar
                def _(sc):
                    for s in range(n_slot):
                        dch = s // 2
                        val = 2 * (s + 1) - 1
                        sc.wait_ge(cPE.s, val)
                        if s >= 2:
                            sc.wait_ge(dS[s % 2].s, 16 * (s // 2))
                        cC.inc(sc.activation(kbf[s % 2][:, :], ps[s % 2][:, :], AF.Identity,
                                             bias=self.bqs[:, dch:dch + 1]))
                        sc.activation(ksq[:, :], ps[s % 2][:, :], AF.Square,
                                      bias=self.bqs[:, dch:dch + 1]).then_inc(cSq.s, 1)

                cNr = self.sem("q_cNr")

                @blk.vector
                def _(v):
                    for s in range(n_slot):
                        val = 2 * s + 3 if s < n_slot - 1 else 2 * n_slot
                        v.wait_ge(cPE.s, val)
                        v.reduce_max(nrm_s[:, :], ps_n[:, :], AX.X).then_inc(cNr.s, 1)
                        v.wait_ge(cNr.s, s + 1)
                        v.tensor_max(self.maxq2[:, :], self.maxq2[:, :],
                                     nrm_s[:, :]).then_inc(cN.s, 1)
                        if s < n_slot - 1:
                            v.wait_ge(cN.s, s + 1)

                @blk.sync
                def _(sy):
                    qs = list(range(32, 52))
                    for s in range(n_slot):
                        dch, half = s // 2, s % 2
                        sy.wait_ge(cC.s, s + 1)
                        sy.dma_start(self.q_scr[dch, :, half * 512:(half + 1) * 512],
                                     kbf[s % 2][:, :]).then_inc(dS[s % 2].s, 16)
                        for k in qs[s * 20 // n_slot:(s + 1) * 20 // n_slot]:
                            self.spill_store(sy, k)
                    sy.wait_ge(dS[0].s, 16 * (n_slot // 2))
                    sy.wait_ge(dS[1].s, 16 * (n_slot // 2))

    def p5b_bias(self):
        nc = self.nc
        cX = self.sem("p5b_c")
        with nc.Block() as blk:
            @blk.scalar
            def _(s):
                s.activation(self.maxq2[:, :], self.maxq2[:, :], AF.Sqrt)
                s.activation(self.maxk2[:, :], self.maxk2[:, :], AF.Sqrt).then_inc(cX.s, 1)

            @blk.vector
            def _(v):
                v.wait_ge(cX.s, 1)
                v.tensor_mul(self.bias_m[:, :], self.maxq2[:, :],
                             self.maxk2[:, :]).then_inc(cX.s, 1)
                v.wait_ge(cX.s, 2)
                v.tensor_scalar(self.bias_m[:, :], self.bias_m[:, :],
                                -SCALE * 1.02, None, ALU.mult).then_inc(cX.s, 1)

    def p6_attention(self):
        nc = self.nc
        x2 = self.reg

        heads = []
        v_idx = 0
        d_idx = 0
        for J in (0, 1):
            for kvh in range(NKVH):
                pair = J * NKVH + kvh
                for g in (0, 1):
                    hi = len(heads)
                    h = kvh + NKVH * g
                    visits = []
                    for side in (0, 1):
                        for w in range(4 * J + 4):
                            s0 = max(0, w - 4 * J)
                            masked = (w >= 4 * J)
                            visits.append(dict(
                                v=v_idx, side=side, w=w, s0=s0, masked=masked,
                                d=(d_idx if masked else None),
                                tile=side * (4 * J + 4) + w))
                            d_idx += 1 if masked else 0
                            v_idx += 1
                    heads.append(dict(hi=hi, J=J, kvh=kvh, g=g, h=h, pair=pair,
                                      visits=visits))
        n_heads = len(heads)
        n_pairs = 2 * NKVH
        exp_done = {v: v + 1 for v in range(v_idx)}

        with contextlib.ExitStack() as p:
            NKB, NQB, NPT = 4, 4, 4
            kbuf = [p.enter_context(nc.sbuf_tensor(f"p6k{i}", [128, 2048], BF16))
                    for i in range(NKB)]
            vbuf = [p.enter_context(nc.sbuf_tensor(f"p6v{i}", [128, 2048], BF16))
                    for i in range(NKB)]
            qbuf = [p.enter_context(nc.sbuf_tensor(f"p6q{i}", [128, 512], BF16))
                    for i in range(NQB)]
            xbuf = [p.enter_context(nc.sbuf_tensor(f"p6x{i}", [128, 512], F32))
                    for i in range(NQB)]
            pt = [p.enter_context(nc.sbuf_tensor(f"p6p{i}", [128, 512], BF16))
                  for i in range(NPT)]
            zl = [p.enter_context(nc.sbuf_tensor(f"p6zl{i}", [128, 512], F32))
                  for i in range(2)]
            tev = [p.enter_context(nc.sbuf_tensor(f"p6te{i}", [128, 512], F32))
                   for i in range(2)]
            psS = [p.enter_context(nc.psum_tensor(f"p6s{i}", [128, 512], F32))
                   for i in range(NPT)]
            psAV = [p.enter_context(nc.psum_tensor(f"p6av{i}", [128, 512], F32))
                    for i in range(2)]
            psZ = [p.enter_context(nc.psum_tensor(f"p6z{i}", [128, 512], F32))
                   for i in range(2)]
            dk = [self.sem(f"p6_dk{i}") for i in range(NKB)]
            dq = [self.sem(f"p6_dq{i}") for i in range(NQB)]
            cS = self.sem("p6_cS")
            cM = self.sem("p6_cM")
            cE = self.sem("p6_cE")
            cAV = self.sem("p6_cAV")
            cL = self.sem("p6_cL")
            cEv = self.sem("p6_cEv")
            cEvm = self.sem("p6_cEvm")

            kv_ready = {}
            q_ready = {}
            npar = [0] * NKB
            for J in (0, 1):
                for kvh in range(NKVH):
                    pair = J * NKVH + kvh
                    npar[pair % NKB] += 16 * (1 + 2 * (4 * J + 4))
                    kv_ready[pair] = npar[pair % NKB]
            hpar = [0] * NQB
            for hi in range(n_heads):
                hpar[hi % NQB] += 32
                q_ready[hi] = hpar[hi % NQB]

            with nc.Block() as blk:
                kv4 = self.kva_scr.rearrange("(r c p) k -> p r c k", r=2, p=128)
                p6ks = list(range(52, 64))

                @blk.sync
                def _(sy):
                    for J in (0, 1):
                        segw = 512 * (J + 1)
                        nt = 4 * J + 4
                        for kvh in range(NKVH):
                            pair = J * NKVH + kvh
                            if pair >= NKB:
                                sy.wait_ge(cEv.s, 2 * (pair - NKB + 1))
                            bi = pair % NKB
                            dst = kbuf[bi][:, 0:2 * segw].rearrange(
                                "p (s k) -> p s k", s=2)
                            src = kv4[:, :, kvh, 0:segw]
                            sy.dma_start(dst, src).then_inc(dk[bi].s, 16)
                            for side in (0, 1):
                                for w in range(nt):
                                    ti = side * nt + w
                                    sy.dma_start(
                                        vbuf[bi][:, ti * 128:(ti + 1) * 128],
                                        kv4[:, side, 16 + kvh,
                                            w * 128:w * 128 + 128],
                                        transpose=True).then_inc(dk[bi].s, 16)
                            for k in p6ks[pair * 12 // n_pairs:
                                          (pair + 1) * 12 // n_pairs]:
                                self.spill_store(sy, k)

                @blk.gpsimd
                def _(g):
                    for hd in heads:
                        hi, h, J = hd["hi"], hd["h"], hd["J"]
                        if hi >= NQB:
                            g.wait_ge(cEv.s, hi - NQB + 1)
                        g.dma_start(qbuf[hi % NQB][:, :],
                                    self.q_scr[h, :, J * 512:(J + 1) * 512]
                                    ).then_inc(dq[hi % NQB].s, 16)
                        g.dma_start(xbuf[hi % NQB][:, :],
                                    self.xt[h * 128:(h + 1) * 128,
                                            J * 512:(J + 1) * 512]
                                    ).then_inc(dq[hi % NQB].s, 16)
                        for k in p6ks[hi * 12 // n_heads:(hi + 1) * 12 // n_heads]:
                            self.spill_load(g, k)

                @blk.tensor
                def _(t):
                    def av_ones(hd, vis, first, last):
                        v, s0 = vis["v"], vis["s0"]
                        bi = hd["pair"] % NKB
                        hb = hd["hi"] % 2
                        cols = slice(s0 * 128, 512)
                        t.wait_ge(cE.s, exp_done[v])
                        if first and hd["hi"] >= 2:
                            t.wait_ge(cEv.s, hd["hi"] - 1)
                        t.matmul(psAV[hb][:, cols],
                                 vbuf[bi][:, vis["tile"] * 128:(vis["tile"] + 1) * 128],
                                 pt[v % NPT][:, cols], start=first, stop=last)
                        t.matmul(psZ[hb][:, cols], self.ones_bf[:, :],
                                 pt[v % NPT][:, cols], start=first,
                                 stop=last).then_inc(cAV.s, 1)

                    for hd in heads:
                        hi = hd["hi"]
                        bi = hd["pair"] % NKB
                        t.wait_ge(dk[hd["pair"] % NKB].s, kv_ready[hd["pair"]])
                        t.wait_ge(dq[hi % NQB].s, q_ready[hi])
                        vl = hd["visits"]
                        for i, vis in enumerate(vl):
                            v, s0 = vis["v"], vis["s0"]
                            cols = slice(s0 * 128, 512)
                            if v >= NPT:
                                t.wait_ge(cE.s, exp_done[v - NPT])
                            t.matmul(psS[v % NPT][:, cols],
                                     kbuf[bi][:, vis["tile"] * 128:(vis["tile"] + 1) * 128],
                                     qbuf[hi % NQB][:, cols],
                                     start=True, stop=True).then_inc(cS.s, 1)
                            if i >= 2:
                                av_ones(hd, vl[i - 2], i == 2, False)
                        if len(vl) >= 2:
                            av_ones(hd, vl[-2], len(vl) == 2, False)
                        av_ones(hd, vl[-1], len(vl) == 1, True)

                @blk.scalar
                def _(sc):
                    for hd in heads:
                        hi = hd["hi"]
                        for vis in hd["visits"]:
                            v, s0 = vis["v"], vis["s0"]
                            sc.wait_ge(cS.s, v + 1)
                            if vis["masked"]:
                                sc.wait_ge(cM.s, vis["d"] + 1)
                            if v >= NPT:
                                sc.wait_ge(cAV.s, v - NPT + 1)
                            sc.activation(pt[v % NPT][:, s0 * 128:512],
                                          psS[v % NPT][:, s0 * 128:512],
                                          AF.Exp, bias=self.bias_m[:, :],
                                          scale=self.scale_t[:, :]).then_inc(cE.s, 1)
                        hb = hi % 2
                        sc.wait_ge(cAV.s, hd["visits"][-1]["v"] + 1)
                        sc.activation(zl[hb][:, :], psZ[hb][:, :],
                                      AF.Ln).then_inc(cL.s, 1)
                        sc.wait_ge(cL.s, 2 * hi + 1)
                        sc.activation(zl[hb][:, :], zl[hb][:, :], AF.Exp,
                                      scale=self.neg_one[:, :]).then_inc(cL.s, 1)

                @blk.vector
                def _(v_):
                    for hd in heads:
                        hi, h, J = hd["hi"], hd["h"], hd["J"]
                        for vis in hd["visits"]:
                            if not vis["masked"]:
                                continue
                            v, s0 = vis["v"], vis["s0"]
                            msk = self.maskA if vis["side"] == 0 else self.maskB
                            v_.wait_ge(cS.s, v + 1)
                            v_.tensor_add(psS[v % NPT][:, s0 * 128:(s0 + 1) * 128],
                                          psS[v % NPT][:, s0 * 128:(s0 + 1) * 128],
                                          msk[:, :]).then_inc(cM.s, 1)
                        hb = hi % 2
                        v_.wait_ge(cL.s, 2 * (hi + 1))
                        v_.wait_ge(dq[hi % NQB].s, q_ready[hi])
                        v_.tensor_mul(tev[hb][:, :], psAV[hb][:, :],
                                      zl[hb][:, :]).then_inc(cEvm.s, 1)
                        v_.wait_ge(cEvm.s, hi + 1)
                        v_.tensor_add(x2[:, h * 1024 + J * 512:h * 1024 + J * 512 + 512],
                                      tev[hb][:, :], xbuf[hi % NQB][:, :]).then_inc(cEv.s, 1)

    def p7_ln2(self):
        nc = self.nc
        x2 = self.reg
        with contextlib.ExitStack() as p:
            x2bf = [p.enter_context(nc.sbuf_tensor(f"p7b{i}", [128, NQ], BF16))
                    for i in range(2)]
            sqb = [p.enter_context(nc.sbuf_tensor(f"p7s{i}", [128, NQ], BF16))
                   for i in range(2)]
            t1b = [p.enter_context(nc.sbuf_tensor(f"p7t{i}", [128, NQ], F32))
                   for i in range(2)]
            ps_s1 = p.enter_context(nc.psum_tensor("p7ps1", [128, NQ], F32))
            ps_s2 = p.enter_context(nc.psum_tensor("p7ps2", [128, NQ], F32))
            self.mu2 = p.enter_context(nc.sbuf_tensor("p7mu", [128, NQ], F32))
            self.a2 = p.enter_context(nc.sbuf_tensor("p7a", [128, NQ], F32))
            te2 = p.enter_context(nc.sbuf_tensor("p7e2", [128, NQ], F32))
            cCp = self.sem("p7_cCp")
            cSq = self.sem("p7_cSq")
            cMM = self.sem("p7_cMM")
            dS = self.sem("p7_dS")
            cR = self.sem("p7_cR")
            cAp = self.sem("p7_cAp")
            dg7 = self.sem("p7_dg")
            with nc.Block() as blk:
                @blk.gpsimd
                def _(g):
                    dg7.dma(g.dma_start(self.gates[:, :], self.gate_w[:, :]))
                    g.wait_ge(dg7.s, dg7.n)

                @blk.vector
                def _(v):
                    for cc in range(NCC):
                        if cc >= 2:
                            v.wait_ge(cMM.s, cc - 1)
                        v.tensor_copy(x2bf[cc % 2][:, :],
                                      x2[:, cc * NQ:(cc + 1) * NQ]).then_inc(cCp.s, 1)

                @blk.scalar
                def _(s):
                    for cc in range(NCC):
                        if cc >= 2:
                            s.wait_ge(cMM.s, cc - 1)
                        s.activation(sqb[cc % 2][:, :], x2[:, cc * NQ:(cc + 1) * NQ],
                                     AF.Square).then_inc(cSq.s, 1)

                @blk.tensor
                def _(t):
                    for cc in range(NCC):
                        t.wait_ge(cCp.s, cc + 1)
                        t.wait_ge(cSq.s, cc + 1)
                        for n in range(2):
                            t.matmul(ps_s1[:, 512 * n:512 * (n + 1)], self.ones_bf[:, :],
                                     x2bf[cc % 2][:, 512 * n:512 * (n + 1)],
                                     start=(cc == 0), stop=(cc == NCC - 1))
                            mm = t.matmul(ps_s2[:, 512 * n:512 * (n + 1)], self.ones_bf[:, :],
                                          sqb[cc % 2][:, 512 * n:512 * (n + 1)],
                                          start=(cc == 0), stop=(cc == NCC - 1))
                        mm.then_inc(cMM.s, 1)

                @blk.sync
                def _(sy):
                    for cc in range(NCC):
                        sy.dma_start(self.x2_scr[cc, :, :],
                                     x2[:, cc * NQ:(cc + 1) * NQ]).then_inc(dS.s, 16)
                    sy.wait_ge(dS.s, 16 * NCC)

            with nc.Block() as blk:
                @blk.vector
                def _(v):
                    v.tensor_scalar(self.mu2[:, :], ps_s1[:, :], 1.0 / C,
                                    None, ALU.mult).then_inc(cR.s, 1)
                    v.tensor_scalar(te2[:, :], ps_s2[:, :], 1.0 / C,
                                    None, ALU.mult).then_inc(cR.s, 1)
                    v.wait_ge(cR.s, 2)
                    v.tensor_mul(self.a2[:, :], self.mu2[:, :],
                                 self.mu2[:, :]).then_inc(cR.s, 1)
                    v.wait_ge(cR.s, 3)
                    v.tensor_sub(te2[:, :], te2[:, :], self.a2[:, :]).then_inc(cR.s, 1)

                @blk.scalar
                def _(s):
                    s.wait_ge(cR.s, 4)
                    s.activation(te2[:, :], te2[:, :], AF.Ln,
                                 bias=self.eps_t[:, :]).then_inc(cR.s, 1)
                    s.wait_ge(cR.s, 5)
                    s.activation(self.a2[:, :], te2[:, :], AF.Exp,
                                 scale=self.neg_half[:, :])

            hv = self.reg_bf
            ps_g = p.enter_context(nc.psum_tensor("p7pg", [8, NQ], F32))
            cGt = self.sem("p7_gt")
            with nc.Block() as blk:
                @blk.vector
                def _(v):
                    for cc in range(NCC):
                        v.tensor_sub(t1b[cc % 2][:, :], x2[:, cc * NQ:(cc + 1) * NQ],
                                     self.mu2[:, :]).then_inc(cAp.s, 1)
                        v.wait_ge(cAp.s, 2 * cc + 1)
                        v.tensor_mul(hv[:, cc * 2048:cc * 2048 + 1024],
                                     t1b[cc % 2][:, :], self.a2[:, :]).then_inc(cAp.s, 1)

                @blk.tensor
                def _(t):
                    for cc in range(NCC):
                        t.wait_ge(cAp.s, 2 * (cc + 1))
                        for n in range(2):
                            mm = t.matmul(ps_g[:, 512 * n:512 * (n + 1)],
                                          self.gates[:, cc * 8:(cc + 1) * 8],
                                          hv[:, cc * 2048 + 512 * n:
                                             cc * 2048 + 512 * (n + 1)],
                                          start=(cc == 0), stop=(cc == NCC - 1))
                    mm.then_inc(cGt.s, 1)

                @blk.scalar
                def _(s):
                    s.wait_ge(cGt.s, 1)
                    s.copy(self.lgf[:, :], ps_g[:, :])

    def p8_ffn(self):
        nc = self.nc
        hv = self.reg_bf
        comb = self.reg

        def hslice(cc, th):
            return hv[:, cc * 2048 + th * 512: cc * 2048 + th * 512 + 512]

        def cslice(hc):
            return comb[:, hc * NQ + 512: hc * NQ + 1024]

        br = self.br
        self.p8a_routing(br)
        self.lgf_es.close()
        for th in (0, 1):
            self.p8b_updown(th, hslice, cslice, None)
            self.p8c_routed(th, hslice, cslice, br)
            self.p8d_ln3_out(th, cslice)

    def p8a_routing(self, br):
        nc = self.nc
        with contextlib.ExitStack() as p:
            lgf = self.lgf
            lgT = p.enter_context(nc.sbuf_tensor("p8lgT", [128, 64], F32))
            scr1 = p.enter_context(nc.sbuf_tensor("p8r1", [128, 8], F32))
            scr2 = p.enter_context(nc.sbuf_tensor("p8r2", [128, 8], F32))
            r_all = p.enter_context(nc.sbuf_tensor("p8ra", [128, 8], F32))
            rT2 = p.enter_context(nc.sbuf_tensor("p8rT", [1, NQ], F32))
            onesf1 = p.enter_context(nc.sbuf_tensor("p8o1", [1, 128], F32))
            ps_t = [p.enter_context(nc.psum_tensor(f"p8pt{i}", [128, 8], F32))
                    for i in range(2)]
            ps_r = [p.enter_context(nc.psum_tensor(f"p8pr{i}", [1, 128], F32))
                    for i in range(2)]
            ps_b = p.enter_context(nc.psum_tensor("p8pb", [128, NQ], F32))
            cv = self.sem("p8a_cv")
            ct = self.sem("p8a_ct")
            ce = self.sem("p8a_ce")
            with nc.Block() as blk:
                @blk.tensor
                def _(t):
                    for u in range(8):
                        if u >= 2:
                            t.wait_ge(ce.s, u - 1)
                        t.transpose(ps_t[u % 2][:, :], lgf[:, u * 128:(u + 1) * 128],
                                    self.ident_f[0:8, 0:8]).then_inc(ct.s, 1)

                @blk.scalar
                def _(s):
                    for u in range(8):
                        s.wait_ge(ct.s, u + 1)
                        s.copy(lgT[:, u * 8:(u + 1) * 8],
                               ps_t[u % 2][:, :]).then_inc(ce.s, 1)

                @blk.vector
                def _(v):
                    n = 0
                    for u in range(8):
                        lg_u = lgT[:, u * 8:(u + 1) * 8]
                        v.wait_ge(ce.s, u + 1)
                        if n:
                            v.wait_ge(cv.s, n)
                        v.reduce_max(scr1[:, 0:1], lg_u, AX.X).then_inc(cv.s, 1)
                        n += 1
                        v.wait_ge(cv.s, n)
                        v.tensor_scalar(scr2[:, :], lg_u, scr1[:, 0:1], None,
                                        ALU.is_equal).then_inc(cv.s, 1)
                        n += 1
                        v.wait_ge(cv.s, n)
                        v.scalar_tensor_tensor(scr2[:, :], scr2[:, :], -1e30, lg_u,
                                               ALU.mult, ALU.add).then_inc(cv.s, 1)
                        n += 1
                        v.wait_ge(cv.s, n)
                        v.reduce_max(scr1[:, 1:2], scr2[:, :], AX.X).then_inc(cv.s, 1)
                        n += 1
                        v.wait_ge(cv.s, n)
                        v.tensor_add(r_all[:, u:u + 1], scr1[:, 0:1],
                                     scr1[:, 1:2]).then_inc(cv.s, 1)
                        n += 1

            cb = self.sem("p8a_cb")
            ct2 = self.sem("p8a_ct2")
            ce2 = self.sem("p8a_ce2")
            with nc.Block() as blk:
                @blk.gpsimd
                def _(g):
                    g.memset(onesf1[:, :], 1.0).then_inc(cb.s, 1)

                @blk.tensor
                def _(t):
                    for u in range(8):
                        if u >= 2:
                            t.wait_ge(ce2.s, u - 1)
                        t.transpose(ps_r[u % 2][:, :], r_all[:, u:u + 1],
                                    self.ident_f[:, :]).then_inc(ct2.s, 1)

                @blk.scalar
                def _(s):
                    for u in range(8):
                        s.wait_ge(ct2.s, u + 1)
                        s.copy(rT2[0:1, u * 128:(u + 1) * 128],
                               ps_r[u % 2][:, :]).then_inc(ce2.s, 1)

            with nc.Block() as blk:
                @blk.tensor
                def _(t):
                    for n in range(2):
                        mm = t.matmul(ps_b[:, 512 * n:512 * (n + 1)], onesf1[:, :],
                                      rT2[0:1, 512 * n:512 * (n + 1)],
                                      start=True, stop=True)
                    mm.then_inc(cb.s, 2)

                @blk.vector
                def _(v):
                    v.wait_ge(cb.s, 3)
                    v.tensor_copy(br[:, :], ps_b[:, :])

    def p8b_updown(self, th, hslice, cslice, _unused):
        nc = self.nc
        with contextlib.ExitStack() as p:
            NWU = 2
            wub = [p.enter_context(nc.sbuf_tensor(f"p8wu{th}{i}", [128, 4096], BF16))
                   for i in range(NWU)]
            NWD = 3
            wdb = [p.enter_context(nc.sbuf_tensor(f"p8wd{th}{i}", [128, 2048], BF16))
                   for i in range(NWD)]
            wdf = [p.enter_context(nc.sbuf_tensor(f"p8wf{th}{i}", [128, 2048], F32))
                   for i in range(NWD)]
            ubuf = p.enter_context(nc.sbuf_tensor(f"p8u{th}", [128, 16 * 512], BF16))
            uscr = [p.enter_context(nc.sbuf_tensor(f"p8us{th}{i}", [128, 512], F32))
                    for i in range(2)]
            psU = [p.enter_context(nc.psum_tensor(f"p8pu{th}{i}", [128, 512], F32))
                   for i in range(2)]
            psD = [p.enter_context(nc.psum_tensor(f"p8pd{th}{i}", [128, 512], F32))
                   for i in range(2)]
            dU = [self.sem(f"p8b_dU{i}") for i in range(3)]
            cPE = self.sem("p8b_cPE")
            cRl = self.sem("p8b_cRl")
            cSq = self.sem("p8b_cSq")
            dD = [self.sem(f"p8b_dD{i}") for i in range(3)]
            cPD = self.sem("p8b_cPD")
            cEv = self.sem("p8b_cEv")
            cDc = self.sem("p8b_cDc")
            bU = [d.n for d in dU]
            bD = [d.n for d in dD]
            bPE, bRl, bSq, bPD, bEv = cPE.n, cRl.n, cSq.n, cPD.n, cEv.n
            bDc = cDc.n
            with nc.Block() as blk:
                @blk.gpsimd
                def _(g):
                    for ib in range(4):
                        for j in range(16):
                            gj = 16 * ib + j
                            if gj >= NWU:
                                g.wait_ge(cPE.s, bPE + gj - NWU + 1)
                            g.dma_start(wub[gj % NWU][:, :],
                                        self.up_bf[gj, :, :]).then_inc(dU[gj % 3].s, 16)
                    for z in range(3):
                        dU[z].n += 16 * len([i for i in range(64) if i % 3 == z])

                @blk.sync
                def _(sy):
                    for ib in range(4):
                        for hh in range(32):
                            ghh = 32 * ib + hh
                            if ghh >= NWD:
                                sy.wait_ge(cDc.s, bDc + ghh - NWD + 1)
                            sy.dma_start(wdf[ghh % NWD][:, :],
                                         self.dwn_w[hh, :, ib * 2048:(ib + 1) * 2048]
                                         ).then_inc(dD[ghh % NWD].s, 16)
                    for z in range(3):
                        dD[z].n += 16 * len([i for i in range(128) if i % 3 == z])

                @blk.tensor
                def _(t):
                    for ib in range(4):
                        for ic in range(16):
                            gic = 16 * ib + ic
                            nthu = len([i for i in range(gic + 1) if i % 3 == gic % 3])
                            t.wait_ge(dU[gic % 3].s, bU[gic % 3] + 16 * nthu)
                            if gic >= 2:
                                t.wait_ge(cRl.s, bRl + gic - 1)
                            wsl = wub[gic % NWU]
                            for cc in range(NCC):
                                mm = t.matmul(psU[gic % 2][:, :],
                                              wsl[:, cc * 128:(cc + 1) * 128],
                                              hslice(cc, th),
                                              start=(cc == 0), stop=(cc == NCC - 1))
                            mm.then_inc(cPE.s, 1)
                        for hh in range(32):
                            ghh = 32 * ib + hh
                            t.wait_ge(cDc.s, bDc + ghh + 1)
                            t.wait_ge(cSq.s, bSq + 16 * (ib + 1))
                            if ghh >= 2:
                                t.wait_ge(cEv.s, bEv + ghh - 1)
                            wsl = wdb[ghh % NWD]
                            for ic in range(16):
                                mm = t.matmul(psD[ghh % 2][:, :],
                                              wsl[:, ic * 128:(ic + 1) * 128],
                                              ubuf[:, ic * 512:(ic + 1) * 512],
                                              start=(ic == 0), stop=(ic == 15))
                            mm.then_inc(cPD.s, 1)
                    cPE.n += 64
                    cPD.n += 128

                @blk.vector
                def _(v):
                    for ib in range(4):
                        for ic in range(16):
                            gic = 16 * ib + ic
                            v.wait_ge(cPE.s, bPE + gic + 1)
                            if gic >= 2:
                                v.wait_ge(cSq.s, bSq + gic - 1)
                            v.tensor_scalar(uscr[gic % 2][:, :], psU[gic % 2][:, :],
                                            0.0, None, ALU.max).then_inc(cRl.s, 1)
                        def cast_dn(ghh2):
                            nth = len([i for i in range(ghh2 + 1) if i % 3 == ghh2 % 3])
                            v.wait_ge(dD[ghh2 % 3].s, bD[ghh2 % 3] + 16 * nth)
                            if ghh2 >= NWD:
                                v.wait_ge(cPD.s, bPD + ghh2 - NWD + 1)
                            v.tensor_copy(wdb[ghh2 % NWD][:, :],
                                          wdf[ghh2 % NWD][:, :]).then_inc(cDc.s, 1)

                        for hh in range(32):
                            ghh = 32 * ib + hh
                            if ghh == 0:
                                for kk in range(NWD):
                                    cast_dn(kk)
                            if ghh + NWD < 128:
                                cast_dn(ghh + NWD)
                            v.wait_ge(cPD.s, bPD + ghh + 1)
                            if ib == 0:
                                v.tensor_copy(cslice(hh), psD[ghh % 2][:, :]).then_inc(cEv.s, 1)
                            else:
                                v.tensor_add(cslice(hh), cslice(hh),
                                             psD[ghh % 2][:, :]).then_inc(cEv.s, 1)
                    cRl.n += 64
                    cEv.n += 128
                    cDc.n += 128

                @blk.scalar
                def _(s):
                    for ib in range(4):
                        for ic in range(16):
                            gic = 16 * ib + ic
                            s.wait_ge(cRl.s, bRl + gic + 1)
                            if ib > 0:
                                s.wait_ge(cPD.s, bPD + 32 * ib)
                            s.activation(ubuf[:, ic * 512:(ic + 1) * 512],
                                         uscr[gic % 2][:, :], AF.Square).then_inc(cSq.s, 1)
                    cSq.n += 64

    def p8c_routed(self, th, hslice, cslice, br):
        nc = self.nc
        with contextlib.ExitStack() as p:
            NFC = 4
            wb = [p.enter_context(nc.sbuf_tensor(f"p8cw{th}{i}", [128, 4096], BF16))
                  for i in range(NFC)]
            xlat = p.enter_context(nc.sbuf_tensor(f"p8xl{th}", [128, 16 * 512], BF16))
            moe = p.enter_context(nc.sbuf_tensor(f"p8mo{th}", [128, 16 * 512], BF16))
            psF = [p.enter_context(nc.psum_tensor(f"p8pf{th}{i}", [128, 512], F32))
                   for i in range(NFC)]
            dW = [self.sem(f"p8c_dW{i}") for i in range(NFC)]
            cPE = self.sem("p8c_cPE")
            cEv = self.sem("p8c_cEv")

            def stage(nin, nk, wsrc, wwidth, rhs_fn, evac_fn):
                bW = [d.n for d in dW]
                bPE, bEv = cPE.n, cEv.n

                def nth(j):
                    return len([i for i in range(j + 1) if i % NFC == j % NFC])

                with nc.Block() as blk:
                    @blk.gpsimd
                    def _(g):
                        for j in range(nin):
                            if j >= NFC:
                                g.wait_ge(cEv.s, bEv + j - NFC + 1)
                            g.dma_start(wb[j % NFC][:, 0:wwidth],
                                        wsrc[j, :, :]).then_inc(dW[j % NFC].s, 16)
                        for z in range(NFC):
                            dW[z].n += 16 * len([i for i in range(nin) if i % NFC == z])

                    @blk.tensor
                    def _(t):
                        for j in range(nin):
                            t.wait_ge(dW[j % NFC].s, bW[j % NFC] + 16 * nth(j))
                            if j >= NFC:
                                t.wait_ge(cEv.s, bEv + j - NFC + 1)
                            for k in range(nk):
                                mm = t.matmul(psF[j % NFC][:, :],
                                              wb[j % NFC][:, k * 128:(k + 1) * 128],
                                              rhs_fn(k),
                                              start=(k == 0), stop=(k == nk - 1))
                            mm.then_inc(cPE.s, 1)
                        cPE.n += nin

                    evac_fn(blk, nin, bPE)

            def evac_fc1(blk, nin, bPE):
                @blk.scalar
                def _(s):
                    for j in range(nin):
                        s.wait_ge(cPE.s, bPE + j + 1)
                        s.copy(xlat[:, j * 512:(j + 1) * 512],
                               psF[j % NFC][:, :]).then_inc(cEv.s, 1)
                    cEv.n += nin

            stage(16, NCC, self.fc1_w, 4096, lambda cc: hslice(cc, th), evac_fc1)

            def evac_ew(blk, nin, bPE):
                @blk.vector
                def _(v):
                    for j in range(nin):
                        v.wait_ge(cPE.s, bPE + j + 1)
                        v.tensor_mul(moe[:, j * 512:(j + 1) * 512], psF[j % NFC][:, :],
                                     br[:, th * 512:(th + 1) * 512]).then_inc(cEv.s, 1)
                    cEv.n += nin

            stage(16, NLC, self.ew_w, 2048,
                  lambda lc: xlat[:, lc * 512:(lc + 1) * 512], evac_ew)

            def evac_fc2(blk, nin, bPE):
                @blk.vector
                def _(v):
                    for j in range(nin):
                        v.wait_ge(cPE.s, bPE + j + 1)
                        v.tensor_add(cslice(j), cslice(j),
                                     psF[j % NFC][:, :]).then_inc(cEv.s, 1)
                    cEv.n += nin

            stage(NCC, NLC, self.fc2_w, 2048,
                  lambda lc: moe[:, lc * 512:(lc + 1) * 512], evac_fc2)

    def p8d_ln3_out(self, th, cslice):
        nc = self.nc
        with contextlib.ExitStack() as p:
            cbf = [p.enter_context(nc.sbuf_tensor(f"p8db{th}{i}", [128, 512], BF16))
                   for i in range(2)]
            sqb = [p.enter_context(nc.sbuf_tensor(f"p8ds{th}{i}", [128, 512], BF16))
                   for i in range(2)]
            t1b = [p.enter_context(nc.sbuf_tensor(f"p8dt{th}{i}", [128, 512], F32))
                   for i in range(2)]
            t2b = [p.enter_context(nc.sbuf_tensor(f"p8dg{th}{i}", [128, 512], F32))
                   for i in range(2)]
            xrb = [p.enter_context(nc.sbuf_tensor(f"p8dx{th}{i}", [128, 512], F32))
                   for i in range(2)]
            obuf = [p.enter_context(nc.sbuf_tensor(f"p8do{th}{i}", [128, 512], F32))
                    for i in range(2)]
            mu3 = p.enter_context(nc.sbuf_tensor(f"p8dm{th}", [128, 512], F32))
            a3 = p.enter_context(nc.sbuf_tensor(f"p8da{th}", [128, 512], F32))
            te3 = p.enter_context(nc.sbuf_tensor(f"p8de{th}", [128, 512], F32))
            ps1 = p.enter_context(nc.psum_tensor(f"p8dp1{th}", [128, 512], F32))
            ps2 = p.enter_context(nc.psum_tensor(f"p8dp2{th}", [128, 512], F32))
            cCp = self.sem("p8d_cCp")
            cSq = self.sem("p8d_cSq")
            cMM = self.sem("p8d_cMM")
            cR = self.sem("p8d_cR")
            dX = [self.sem("p8d_dXa"), self.sem("p8d_dXb")]
            cT = self.sem("p8d_cT")
            cG = self.sem("p8d_cG")
            cO = self.sem("p8d_cO")
            dOut = [self.sem("p8d_dOa"), self.sem("p8d_dOb")]
            bCp, bSq, bMM, bR = cCp.n, cSq.n, cMM.n, cR.n
            bT, bG, bO = cT.n, cG.n, cO.n
            bX = [dX[0].n, dX[1].n]
            bOut = [dOut[0].n, dOut[1].n]
            with nc.Block() as blk:
                @blk.vector
                def _(v):
                    for hc in range(NCC):
                        if hc >= 2:
                            v.wait_ge(cMM.s, bMM + hc - 1)
                        v.tensor_copy(cbf[hc % 2][:, :], cslice(hc)).then_inc(cCp.s, 1)
                    cCp.n += NCC

                @blk.scalar
                def _(s):
                    for hc in range(NCC):
                        if hc >= 2:
                            s.wait_ge(cMM.s, bMM + hc - 1)
                        s.activation(sqb[hc % 2][:, :], cslice(hc),
                                     AF.Square).then_inc(cSq.s, 1)
                    cSq.n += NCC

                @blk.tensor
                def _(t):
                    for hc in range(NCC):
                        t.wait_ge(cCp.s, bCp + hc + 1)
                        t.wait_ge(cSq.s, bSq + hc + 1)
                        t.matmul(ps1[:, :], self.ones_bf[:, :], cbf[hc % 2][:, :],
                                 start=(hc == 0), stop=(hc == NCC - 1))
                        t.matmul(ps2[:, :], self.ones_bf[:, :], sqb[hc % 2][:, :],
                                 start=(hc == 0), stop=(hc == NCC - 1)).then_inc(cMM.s, 1)
                    cMM.n += NCC

            with nc.Block() as blk:
                @blk.vector
                def _(v):
                    v.tensor_scalar(mu3[:, :], ps1[:, :], 1.0 / C,
                                    None, ALU.mult).then_inc(cR.s, 1)
                    v.tensor_scalar(te3[:, :], ps2[:, :], 1.0 / C,
                                    None, ALU.mult).then_inc(cR.s, 1)
                    v.wait_ge(cR.s, bR + 2)
                    v.tensor_mul(a3[:, :], mu3[:, :], mu3[:, :]).then_inc(cR.s, 1)
                    v.wait_ge(cR.s, bR + 3)
                    v.tensor_sub(te3[:, :], te3[:, :], a3[:, :]).then_inc(cR.s, 1)
                    cR.n += 4

                @blk.scalar
                def _(s):
                    s.wait_ge(cR.s, bR + 4)
                    s.activation(te3[:, :], te3[:, :], AF.Ln,
                                 bias=self.eps_t[:, :]).then_inc(cR.s, 1)
                    cR.n += 1
                    s.wait_ge(cR.s, bR + 5)
                    s.activation(a3[:, :], te3[:, :], AF.Exp, scale=self.neg_half[:, :])

            with nc.Block() as blk:
                @blk.sync
                def _(sy):
                    for hc in range(NCC):
                        if hc >= 2:
                            sy.wait_ge(cO.s, bO + hc - 1)
                        sy.dma_start(xrb[hc % 2][:, :],
                                     self.x2_scr[hc, :, th * 512:(th + 1) * 512]
                                     ).then_inc(dX[hc % 2].s, 16)
                    for z in range(2):
                        dX[z].n += 16 * (NCC // 2)
                    sy.wait_ge(cO.s, bO + NCC)

                @blk.vector
                def _(v):
                    for hc in range(NCC):
                        if hc >= 2:
                            v.wait_ge(dOut[hc % 2].s,
                                      bOut[hc % 2] + 16 * (hc // 2))
                        v.tensor_sub(t1b[hc % 2][:, :], cslice(hc), mu3[:, :])
                        v.tensor_mul(t2b[hc % 2][:, :], t1b[hc % 2][:, :], a3[:, :])
                        v.tensor_scalar(t1b[hc % 2][:, :], t2b[hc % 2][:, :],
                                        self.gams[:, hc:hc + 1],
                                        self.bets[:, hc:hc + 1], ALU.mult, ALU.add)
                        v.wait_ge(dX[hc % 2].s, bX[hc % 2] + 16 * (hc // 2 + 1))
                        v.tensor_add(obuf[hc % 2][:, :], t1b[hc % 2][:, :],
                                     xrb[hc % 2][:, :]).then_inc(cO.s, 1)
                    cO.n += NCC

                @blk.gpsimd
                def _(g):
                    for hc in range(NCC):
                        g.wait_ge(cO.s, bO + hc + 1)
                        g.dma_start(self.out_t[hc * 128:(hc + 1) * 128,
                                               th * 512:(th + 1) * 512],
                                    obuf[hc % 2][:, :]).then_inc(dOut[hc % 2].s, 16)
                    for z in range(2):
                        dOut[z].n += 16 * (NCC // 2)
                        g.wait_ge(dOut[z].s, bOut[z] + 16 * (NCC // 2))

    def dbg_dumps(self):
        if not _DBG:
            return
        nc = self.nc
        dD = self.sem("dbg_d")
        with nc.Block() as blk:
            @blk.sync
            def _(sy):
                if 2 <= _DBG_PHASE <= 6:
                    dD.dma(sy.dma_start(self.dbg_xln[:, :], self.reg_bf[:, :]))
                pairs = [(0, self.bias_m[:, :]), (1, self.bias_o[:, :]),
                         (2, self.maxq2[:, :]), (3, self.maxk2[:, :])]
                for i, src in pairs:
                    dD.dma(sy.dma_start(self.dbg_misc[:, i:i + 1], src,
                                        allow_slow_non_contiguous=True))
                sy.wait_ge(dD.s, dD.n)


def _build():
    return Builder().build()


def _enable_ldw_opt():
    try:
        from concourse.compiler_utils import get_compiler_flags, set_compiler_flags
        flags = [f.replace("--enable-ldw-opt=false", "--enable-ldw-opt=true")
                 for f in get_compiler_flags()]
        set_compiler_flags(flags)
    except Exception:
        pass


def kernel(**inputs):
    _enable_ldw_opt()
    in_maps = _prep(inputs)
    nc = _build()
    res = run_bass_kernel_spmd(nc, in_maps, core_ids=list(range(8)))
    return _reassemble([np.asarray(r["out"]) for r in res.results])
